# version 06
# baseline (speedup 1.0000x reference)
import numpy as np
import ml_dtypes
import concourse.bass as bass
import concourse.mybir as mybir
from concourse.bass_utils import run_bass_kernel_spmd

F32 = mybir.dt.float32
BF16 = mybir.dt.bfloat16
I32 = mybir.dt.int32
AF = mybir.ActivationFunctionType
ALU = mybir.AluOpType
AX = mybir.AxisListType

B, S, D = 4, 8192, 1024
T = 4096
HALO = 128
XT = T + 2 * HALO
NH, HD = 16, 64
HID = 3072
DIN, DST, NCH, CH = 2048, 128, 16, 256
CONV_DIM = DIN + 2 * DST
NPROJ = 2 * DIN + 2 * DST + NH
XBCDT = CONV_DIM + NH
EPS = 1e-5
NEG = -1e30
Q127 = 1.0 / 127.0

DEBUG = False
ENGS = ("sync", "tensor", "vector", "scalar", "gpsimd")


class Sem:
    def __init__(self, nc, name):
        self.h = nc.alloc_semaphore(name)
        self.v = 0


def _pad_slice_T(x_b, lo, hi):
    n = hi - lo
    out = np.zeros((D, n), np.float32)
    a, b2 = max(lo, 0), min(hi, S)
    if b2 > a:
        out[:, a - lo:b2 - lo] = np.ascontiguousarray(x_b[a:b2]).T
    return out


def _build_gp(rel, hf):
    p = np.arange(128)[:, None]
    f = np.arange(384)[None, :]
    idx = f - p
    band = (idx >= 0) & (idx <= 256)
    g = np.where(band[None], np.take(rel, np.clip(idx, 0, 256), axis=1), NEG)
    g = np.ascontiguousarray(g.astype(np.float32))
    g_first = g.copy()
    if hf == 0:
        g_first[:, :, :128] = NEG
    g_last = g.copy()
    if hf == 1:
        g_last[:, :, 256:] = NEG
    return g, g_first, g_last


def make_core_inputs(c, full):
    b, hf = c // 2, c % 2
    x = full["L0_attn_a_x"][b]
    lo = hf * T - HALO
    d = {}
    d["xin"] = _pad_slice_T(x, lo, lo + XT)
    d["wq"] = np.ascontiguousarray(full["L0_attn_a_Wq"], np.float32)
    d["wk"] = np.ascontiguousarray(full["L0_attn_a_Wk"], np.float32)
    d["wv"] = np.ascontiguousarray(full["L0_attn_a_Wv"], np.float32)
    d["wo"] = np.ascontiguousarray(full["L0_attn_a_Wo"], np.float32)
    for nm in ("bq", "bk", "bv", "bo"):
        v = np.asarray(full[f"L0_attn_a_{nm}"], np.float32)
        d[nm + "c"] = np.ascontiguousarray(v.reshape(8, 128).T)
        d[nm + "r"] = np.ascontiguousarray(v.reshape(1, D))
    g, gf, gl = _build_gp(np.asarray(full["L0_attn_a_rel_pos_bias"],
                                     np.float32), hf)
    d["gp_int"], d["gp_first"], d["gp_last"] = g, gf, gl
    d["wgT"] = np.ascontiguousarray(full["L0_ffn_w_gate"].T.astype(np.int32))
    d["wuT"] = np.ascontiguousarray(full["L0_ffn_w_up"].T.astype(np.int32))
    d["wdT"] = np.ascontiguousarray(full["L0_ffn_w_down"].T.astype(np.int32))
    d["winT"] = np.ascontiguousarray(full["L0_attn_b_W_in"].T.astype(np.float32))
    d["woutT"] = np.ascontiguousarray(full["L0_attn_b_W_out"].T.astype(np.float32))
    d["convw"] = np.ascontiguousarray(
        np.asarray(full["L0_attn_b_conv_w"], np.float32).reshape(CONV_DIM, 4))
    d["convbc"] = np.ascontiguousarray(
        np.asarray(full["L0_attn_b_conv_b"], np.float32).reshape(18, 128).T)
    d["smalls"] = np.ascontiguousarray(np.stack(
        [np.asarray(full["L0_attn_b_dt_bias"], np.float32),
         np.asarray(full["L0_attn_b_A_log"], np.float32),
         np.asarray(full["L0_attn_b_D"], np.float32)], axis=1))
    d["drow"] = np.ascontiguousarray(
        np.asarray(full["L0_attn_b_D"], np.float32).reshape(1, NH))
    d["normw"] = np.ascontiguousarray(
        np.asarray(full["L0_attn_b_norm_weight"], np.float32).reshape(16, 128).T)
    oh = np.zeros((16, 2048), np.float32)
    e16 = np.zeros((16, 256), np.float32)
    for h in range(16):
        oh[h, h * 128:(h + 1) * 128] = 1.0
        e16[h, h::16] = 1.0
    d["oh"] = oh
    d["e16"] = e16
    rm = np.zeros((128, 8), np.float32)
    if hf == 1:
        rm[:, c - 1] = 1.0
    d["rdmask"] = rm
    return d


class KB:
    def __init__(self, nc):
        self.nc = nc
        DR = {}

        def din(name, shape, dt=F32):
            DR[name] = nc.dram_tensor(name, list(shape), dt,
                                      kind="ExternalInput").ap()
        din("xin", [D, XT])
        for w in ("wq", "wk", "wv", "wo"):
            din(w, [D, D])
        for bn in ("bq", "bk", "bv", "bo"):
            din(bn + "c", [128, 8])
            din(bn + "r", [1, D])
        din("gp_int", [NH, 128, 384])
        din("gp_first", [NH, 128, 384])
        din("gp_last", [NH, 128, 384])
        din("wgT", [D, HID], I32)
        din("wuT", [D, HID], I32)
        din("wdT", [HID, D], I32)
        din("winT", [D, NPROJ])
        din("woutT", [DIN, D])
        din("convw", [CONV_DIM, 4])
        din("convbc", [128, 18])
        din("smalls", [NH, 3])
        din("drow", [1, NH])
        din("normw", [128, 16])
        din("oh", [16, 2048])
        din("e16", [16, 256])
        din("rdmask", [128, 8])
        self.DR = DR

        self.OUT = nc.dram_tensor("out", [D, T], F32, kind="ExternalOutput").ap()
        self.DBG = {}
        if DEBUG:
            for nm in ("dbg_x1", "dbg_x2", "dbg_x3"):
                self.DBG[nm] = nc.dram_tensor(nm, [D, T], F32,
                                              kind="ExternalOutput").ap()
            self.DBG["dbg_ao"] = nc.dram_tensor(
                "dbg_ao", [128, 8 * 1024], BF16, kind="ExternalOutput").ap()

        self.x1 = nc.dram_tensor("x1s", [D, T], F32).ap()
        self.x2 = nc.dram_tensor("x2s", [D, T], F32).ap()
        self.x3 = nc.dram_tensor("x3s", [D, T], F32).ap()
        kw = dict(kind="ExternalOutput") if DEBUG else {}
        self.qsc = nc.dram_tensor("qscr", [D, T], BF16, **kw).ap()
        self.ksc = nc.dram_tensor("kscr", [D, XT], BF16, **kw).ap()
        self.vsc = nc.dram_tensor("vscr", [XT, D], BF16, **kw).ap()
        self.ysp = nc.dram_tensor("yspill", [128, 16 * T], F32, **kw).ap()
        self.cfms = nc.dram_tensor("cfms", [DST, T], BF16, **kw).ap()
        self.cc_in = nc.dram_tensor("cc_in", [128, DIN], F32).ap()
        self.cc_out = nc.dram_tensor("cc_out", [8 * 128, DIN], F32).ap()
        self.ch_in = nc.dram_tensor("ch_in", [128, 54], F32).ap()
        self.ch_out = nc.dram_tensor("ch_out", [8 * 128, 54], F32).ap()

        self.PB = nc.alloc_sbuf_tensor("PB", [128, 96 * 1024 + 512], BF16)
        self.off = 0
        self.marks = []

        A = nc.alloc_sbuf_tensor
        self.ident = A("ident", [128, 128], BF16)
        self.identf = A("identf", [128, 128], F32)
        self.onesb = A("onesb", [128, 128], BF16)
        self.onesf = A("onesf", [128, 128], F32)
        self.ones16f = A("ones16f", [16, 128], F32)
        self.ones1 = A("ones1", [1, 512], BF16)
        self.biasc = A("biasc", [128, 32], F32)
        self.biasr = A("biasr", [1, D], BF16)
        self.convw_sb = A("convw_sb", [128, 72], F32)
        self.convb_sb = A("convb_sb", [128, 18], F32)
        self.smalls_sb = A("smalls_sb", [16, 8], F32)
        self.drow_sb = A("drow_sb", [1, NH], BF16)
        self.dcols = A("dcols", [128, NH], F32)
        self.normw_sb = A("normw_sb", [128, 16], F32)
        self.rdmask_sb = A("rdmask_sb", [128, 8], F32)
        self.etm = A("etm", [128, 512], F32)
        self.zeros16 = A("zeros16", [16, 256], F32)
        self.acsT_g = A("acsT_g", [128, NCH * 32], F32)
        self.lasts = A("lasts", [16, 40], F32)
        self.pj_sb = A("pj_sb", [16, NCH], F32)
        self.pj128 = A("pj128", [128, NCH * 16], F32)
        self.cdecb = A("cdecb", [128, NCH * 16], F32)
        self.e16_sb = A("e16_sb", [16, 256], F32)
        self.halo3 = A("halo3", [128, 54], F32)
        self.cvals = A("cvals", [128, 8], F32)
        self.scr_nop = A("scrnop", [1, 8], F32)

        self.ps = [nc.alloc_psum_tensor(f"ps{i}", [128, 512], F32)
                   for i in range(6)]
        self.pbf = [nc.alloc_psum_tensor(f"pbf{i}", [128, 1024], BF16)
                    for i in range(2)]
        self.ptp = [t[:, :].bitcast(F32) for t in self.pbf]

        for s in ("sS", "sG", "sC", "sT", "sV", "sA", "sP", "sO", "sRT",
                  "sRP"):
            setattr(self, s, Sem(nc, s))

        self.blr = {}

        self.CONSTS = [EPS, 1.0, -1.0, Q127, 1.0 / DIN, 0.5, 2.0, 0.0]
        for i, v in enumerate(self.CONSTS):
            nc.const_aps.register(F32, v, self.cvals[:, i:i + 1])

    def a16(self, cols):
        cols = (cols + 15) // 16 * 16
        v = self.PB[:, self.off:self.off + cols]
        self.off += cols
        assert self.off <= 96 * 1024 + 512, f"SBUF pool overflow: {self.off}"
        return v

    def a32(self, cols):
        return self.a16(2 * cols).bitcast(F32)

    def push(self):
        self.marks.append(self.off)

    def pop(self):
        self.off = self.marks.pop()

    def rtrip(self, e, eng="vector"):
        s = self.sRT if eng == "vector" else self.sRP
        s.v += 1
        e.engine_nop().then_inc(s.h, 1)
        e.wait_ge(s.h, s.v)

    def rd(self, bank, tok):
        self.blr[bank] = tok

    def wr(self, *banks):
        return [self.blr.get(b) for b in banks]

    def mamba_top(self):
        base = self.off
        t = dict(acs=self.a32(T), oh=self.a32(2048), sin=self.a32(2048),
                 sinb=self.a16(2048))
        if getattr(self, "_mtop_base", None) is None:
            self._mtop_base = base
        else:
            assert self._mtop_base == base, (self._mtop_base, base)
        return t


PE_DELAY = 500


class E:

    def __init__(self, K):
        self.K = K
        self.tape = {e: [] for e in ENGS}

    def op(self, eng, fn, waits=(), inc=None, incn=1):
        K = self.K
        wl = [(t[0].h, t[1]) for t in waits if t is not None]
        pe_tok = any((t is not None and t[0] is K.sT) for t in waits)
        if inc is not None:
            inc.v += incn

        def run(e, fn=fn, wl=wl, inc=inc, incn=incn, eng=eng, pe_tok=pe_tok):
            for h, v in wl:
                e.wait_ge(h, v)
            if pe_tok and eng != "tensor":
                e.nop(cycle_cnt=PE_DELAY)
            r = fn(e)
            if inc is not None:
                if isinstance(r, list):
                    r = r[-1] if r else None
                if eng == "tensor":
                    e.drain()
                    r = e.nop()
                elif r is None:
                    if eng in ("vector", "gpsimd"):
                        r = e.engine_nop()
                    elif eng == "scalar":
                        r = e.copy(K.scr_nop[0:1, 0:1], K.scr_nop[0:1, 1:2])
                    else:
                        r = e.nop()
                r.then_inc(inc.h, incn)
        self.tape[eng].append(run)
        return (inc, inc.v) if inc is not None else None

    def dma(self, eng, dst, src, waits=(), out_sem=None):
        K = self.K
        sem = out_sem if out_sem is not None else (
            K.sG if eng == "gpsimd" else (K.sC if eng == "scalar" else K.sS))
        return self.op(eng, lambda e: e.dma_start(out=dst, in_=src),
                       waits=waits, inc=sem, incn=16)

    def emit(self, name):
        with self.K.nc.Block(name) as blk:
            for eng in ENGS:
                lst = self.tape[eng]
                if not lst:
                    continue

                def body(e, lst=lst):
                    for fn in lst:
                        fn(e)
                getattr(blk, eng)(body)


def emit_ln(K, E_, src, col0, ncols, u_dst, extra=()):
    import os
    lnp = int(os.environ.get("LN_PART", 9))
    nt = (ncols + 511) // 512
    prev_ln = getattr(K, "_ln_prev", [])
    K.push()
    xc = [K.a16(8 * 512), K.a16(8 * 512)]
    x2b = K.a16(8 * 512)
    mbc = K.a32(512)
    rbc = K.a32(512)
    tsc = K.a32(512)
    done = []
    lastmm = None
    for t in range(nt):
        w = min(512, ncols - t * 512)
        bb = xc[t % 2]
        reuse = [done[t - 2]] if t >= 2 else list(prev_ln) + list(extra)
        ld = None
        for dch in range(8):
            ld = E_.dma("gpsimd", bb[:, dch * 512:dch * 512 + w],
                        src[dch * 128:dch * 128 + 128,
                            col0 + t * 512:col0 + t * 512 + w],
                        waits=reuse if dch == 0 else ())

        if lnp < 2:
            done.append(ld)
            continue

        def sf(e, bb=bb, w=w):
            r = None
            for dch in range(8):
                r = e.activation(x2b[:, dch * 512:dch * 512 + w],
                                 bb[:, dch * 512:dch * 512 + w], AF.Square)
            return r
        sq = E_.op("scalar", sf, waits=[ld] + ([done[t - 1]] if t >= 1
                                               else list(prev_ln)), inc=K.sA)
        if lnp < 3:
            done.append(sq)
            continue

        def mf(e, bb=bb, w=w):
            for dch in range(8):
                e.matmul(K.ps[0][:, 0:w], K.onesb[:],
                         bb[:, dch * 512:dch * 512 + w],
                         start=(dch == 0), stop=(dch == 7))
            for dch in range(8):
                e.matmul(K.ps[1][:, 0:w], K.onesb[:],
                         x2b[:, dch * 512:dch * 512 + w],
                         start=(dch == 0), stop=(dch == 7))
        mm = E_.op("tensor", mf,
                   waits=[ld, sq] + K.wr(("ps", 0), ("ps", 1)), inc=K.sT)
        lastmm = mm
        if lnp < 4:
            done.append(mm)
            continue

        def v1(e, w=w):
            e.tensor_scalar(mbc[:, 0:w], K.ps[0][:, 0:w], 1.0 / 1024, None,
                            ALU.mult)
            e.tensor_scalar(tsc[:, 0:w], K.ps[1][:, 0:w], 1.0 / 1024, None,
                            ALU.mult)
        t1 = E_.op("vector", v1, waits=[mm], inc=K.sV)
        K.rd(("ps", 0), t1)
        K.rd(("ps", 1), t1)
        if lnp < 5:
            done.append(t1)
            continue

        t2 = E_.op("scalar", lambda e, w=w: e.activation(
            rbc[:, 0:w], mbc[:, 0:w], AF.Square), waits=[t1], inc=K.sA)
        t3 = E_.op("vector", lambda e, w=w: e.tensor_sub(
            tsc[:, 0:w], tsc[:, 0:w], rbc[:, 0:w]), waits=[t2], inc=K.sV)
        t4 = E_.op("scalar", lambda e, w=w: e.activation(
            tsc[:, 0:w], tsc[:, 0:w], AF.Sqrt, bias=EPS), waits=[t3], inc=K.sA)

        def v2(e, bb=bb, w=w, t=t):
            e.reciprocal(rbc[:, 0:w], tsc[:, 0:w])
            for dch in range(8):
                e.tensor_sub(x2b[:, dch * 512:dch * 512 + w],
                             bb[:, dch * 512:dch * 512 + w], mbc[:, 0:w])
                e.tensor_mul(u_dst[:, dch * ncols + t * 512:
                                   dch * ncols + t * 512 + w],
                             x2b[:, dch * 512:dch * 512 + w], rbc[:, 0:w])
        t5 = E_.op("vector", v2, waits=[t4] + list(extra), inc=K.sV)
        done.append(t5)
    K.pop()
    K._ln_prev = [done[-1], lastmm]
    return done[-1]


def ph_init(K, E_):
    DR = K.DR

    def g0(e):
        def d(dst, src):
            e.dma_start(out=dst, in_=src).then_inc(K.sG.h, 16)
        d(K.biasc[:, 0:8], DR["bqc"])
        d(K.biasc[:, 8:16], DR["bkc"])
        d(K.biasc[:, 16:24], DR["bvc"])
        d(K.biasc[:, 24:32], DR["boc"])
        d(K.biasr[:, 0:D], DR["bor"])
        for a in range(18):
            d(K.convw_sb[:, a * 4:(a + 1) * 4],
              DR["convw"][a * 128:(a + 1) * 128, :])
        d(K.convb_sb[:], DR["convbc"])
        d(K.smalls_sb[0:16, 0:3], DR["smalls"])
        d(K.drow_sb[:], DR["drow"])
        d(K.normw_sb[:], DR["normw"])
        d(K.rdmask_sb[:], DR["rdmask"])
        d(K.e16_sb[:], DR["e16"])
        for i, v in enumerate(K.CONSTS):
            e.memset(K.cvals[:, i:i + 1], v)
        e.memset(K.onesb[:], 1.0)
        e.memset(K.onesf[:], 1.0)
        e.memset(K.ones16f[:], 1.0)
        e.memset(K.ones1[:], 1.0)
        e.memset(K.zeros16[:], 0.0)
        e.memset(K.scr_nop[:], 0.0)
        e.memset(K.etm[:], 0.0)
        K.rtrip(e, "gpsimd")
        e.affine_select(K.ident[:], K.onesb[:], pattern=[[-1, 128]],
                        compare_op=ALU.is_equal, fill=0.0, base=0,
                        channel_multiplier=1)
        e.affine_select(K.identf[:], K.onesf[:], pattern=[[-1, 128]],
                        compare_op=ALU.is_equal, fill=0.0, base=0,
                        channel_multiplier=1)
        e.affine_select(K.etm[:, 0:256], K.etm[:, 0:256], pattern=[[1, 256]],
                        compare_op=ALU.is_ge, fill=NEG, base=0,
                        channel_multiplier=-1)
        e.affine_select(K.etm[:, 256:512], K.etm[:, 256:512],
                        pattern=[[1, 256]], compare_op=ALU.is_ge, fill=NEG,
                        base=-128, channel_multiplier=-1)
    K.sG.v += 16 * 29
    ld = E_.op("gpsimd", g0, inc=K.sP)
    ldt = (K.sG, K.sG.v)

    s0a = E_.op("scalar", lambda e: e.activation(
        K.smalls_sb[0:16, 3:4], K.smalls_sb[0:16, 1:2], AF.Exp),
        waits=[ldt, ld], inc=K.sA)
    E_.op("vector", lambda e: e.tensor_scalar(
        K.smalls_sb[0:16, 3:4], K.smalls_sb[0:16, 3:4], -1.0, None,
        ALU.mult), waits=[s0a], inc=K.sV)

    mm = E_.op("tensor", lambda e: e.matmul(
        K.ps[0][0:128, 0:NH], K.ones1[0:1, 0:128], K.drow_sb[:],
        start=True, stop=True), waits=[ldt, ld], inc=K.sT)
    ev = E_.op("vector", lambda e: e.tensor_copy(K.dcols[:],
                                                 K.ps[0][0:128, 0:NH]),
               waits=[mm], inc=K.sV)
    K.rd(("ps", 0), ev)


def ph_qkv(K, E_):
    import os
    part = int(os.environ.get("QKV_PART", 9))
    K.push()
    u = K.a16(8 * XT)
    wbuf = [K.a16(8 * 1024), K.a16(8 * 1024)]
    stage = [K.a16(4 * 512), K.a16(4 * 512)]
    bvr = K.a16(1024)[0:1, :]

    bv_ld = E_.dma("gpsimd", bvr, K.DR["bvr"])
    ln_done = emit_ln(K, E_, K.DR["xin"], 0, XT, u)

    wname = ("wq", "wk", "wv")
    wld = [None] * 3

    def load_w(wi, waits):
        t = None
        for dch in range(8):
            t = E_.dma("gpsimd", wbuf[wi % 2][:, dch * 1024:(dch + 1) * 1024],
                       K.DR[wname[wi]][dch * 128:(dch + 1) * 128, :],
                       waits=waits if dch == 0 else ())
        return t

    if part < 1:
        K.pop()
        return
    wld[0] = load_w(0, [])
    wld[1] = load_w(1, [])

    def proj(wi, outdram, c0, nst, length, biasoff, qmul):
        groups = [list(range(i, min(i + 4, nst))) for i in range(0, nst, 4)]
        last_dma = None
        for grp in groups:
            prev_ev = None
            for oc in range(8):
                def mf(e, grp=grp, oc=oc, wi=wi, c0=c0, length=length):
                    for dch in range(8):
                        for si, st in enumerate(grp):
                            w0 = c0 + st * 512
                            wdt = min(512, length - st * 512)
                            e.matmul(K.ps[si][0:128, 0:wdt],
                                     wbuf[wi % 2][:, dch * 1024 + oc * 128:
                                                  dch * 1024 + oc * 128 + 128],
                                     u[:, dch * XT + w0:dch * XT + w0 + wdt],
                                     start=(dch == 0), stop=(dch == 7))
                w = [ln_done, wld[wi], prev_ev]
                w += K.wr(*[("ps", si) for si in range(len(grp))])
                mm = E_.op("tensor", mf, waits=w, inc=K.sT)

                def vf(e, grp=grp, oc=oc, biasoff=biasoff, qmul=qmul,
                       length=length):
                    for si, st in enumerate(grp):
                        wdt = min(512, length - st * 512)
                        if qmul is None:
                            e.tensor_scalar(
                                stage[oc % 2][:, si * 512:si * 512 + wdt],
                                K.ps[si][0:128, 0:wdt],
                                K.biasc[:, biasoff + oc:biasoff + oc + 1],
                                None, ALU.add)
                        else:
                            e.tensor_scalar(
                                stage[oc % 2][:, si * 512:si * 512 + wdt],
                                K.ps[si][0:128, 0:wdt],
                                K.biasc[:, biasoff + oc:biasoff + oc + 1],
                                qmul, ALU.add, ALU.mult)
                ev = E_.op("vector", vf, waits=[mm, last_dma], inc=K.sV)
                for si in range(len(grp)):
                    K.rd(("ps", si), ev)
                prev_ev = ev
                for si, st in enumerate(grp):
                    w0 = c0 + st * 512
                    wdt = min(512, length - st * 512)
                    last_dma = E_.dma(
                        "sync",
                        outdram[oc * 128:(oc + 1) * 128, w0 - c0:w0 - c0 + wdt],
                        stage[oc % 2][:, si * 512:si * 512 + wdt],
                        waits=[ev] if si == 0 else ())
        return last_dma

    if part < 2:
        K.pop()
        return
    qd = proj(0, K.qsc, HALO, 8, T, 0, 0.125)
    if part < 3:
        K.pop()
        return
    kd = proj(1, K.ksc, 0, 9, XT, 8, None)
    if part < 4:
        K.pop()
        return

    wld[2] = load_w(2, [qd])
    prevv = None
    for tc in range(34):
        def mf(e, tc=tc):
            for dch in range(8):
                for half in range(2):
                    e.matmul(K.ps[(tc % 2) * 2 + half][0:128, 0:512],
                             u[:, dch * XT + tc * 128:dch * XT + tc * 128 + 128],
                             wbuf[0][:, dch * 1024 + half * 512:
                                     dch * 1024 + half * 512 + 512],
                             start=(dch == 0), stop=False)
            for half in range(2):
                e.matmul(K.ps[(tc % 2) * 2 + half][0:128, 0:512],
                         K.ones1[0:1, 0:128],
                         bvr[0:1, half * 512:half * 512 + 512],
                         start=False, stop=True)
        b0, b1 = (tc % 2) * 2, (tc % 2) * 2 + 1
        mm = E_.op("tensor", mf, waits=[wld[2], bv_ld, ln_done] +
                   K.wr(("ps", b0), ("ps", b1)), inc=K.sT)

        def vf(e, tc=tc):
            for half in range(2):
                e.tensor_copy(stage[tc % 2][:, half * 512:half * 512 + 512],
                              K.ps[(tc % 2) * 2 + half][0:128, 0:512])
        ev = E_.op("vector", vf, waits=[mm, prevv], inc=K.sV)
        K.rd(("ps", b0), ev)
        K.rd(("ps", b1), ev)
        prevv = E_.dma("sync", K.vsc[tc * 128:(tc + 1) * 128, :],
                       stage[tc % 2][:, 0:1024], waits=[ev])
    K.pop()


def ph_attn(K, E_, ap, al):
    qs = ap * 1024
    K.push()
    qb = K.a16(8 * 1024)
    kb = K.a16(8 * 1280)
    vb = K.a16(10 * 1024)
    ao = K.a16(8 * 1024)
    ptg = K.a16(2 * 3072)
    pn = K.a16(4 * 768)
    nml = K.a32(16)
    dg = K.a16(4 * 256)
    xr = K.a32(8 * 1024)
    stage = [K.a32(1024), K.a32(1024)]

    if ap == 0:
        t = None
        for vi, vn in enumerate(("gp_int", "gp_first", "gp_last")):
            for h in range(NH):
                t = E_.dma("gpsimd",
                           al["gp"][:, (vi * NH + h) * 384:
                                    (vi * NH + h) * 384 + 384],
                           K.DR[vn][h])
        def expgp(e):
            r = None
            for ch in range(9):
                r = e.activation(al["gp"][:, ch * 2048:(ch + 1) * 2048],
                                 al["gp"][:, ch * 2048:(ch + 1) * 2048],
                                 AF.Exp)
            return r
        t = E_.op("scalar", expgp, waits=[t], inc=K.sA)
        for dch in range(8):
            t = E_.dma("gpsimd", al["wo"][:, dch * 1024:(dch + 1) * 1024],
                       K.DR["wo"][dch * 128:(dch + 1) * 128, :],
                       waits=[t] if dch == 0 else ())
        al["load"] = t

    spl = None
    for dch in range(8):
        spl = E_.dma("scalar", qb[:, dch * 1024:(dch + 1) * 1024],
                     K.qsc[dch * 128:(dch + 1) * 128, qs:qs + 1024])
        spl = E_.dma("scalar", kb[:, dch * 1280:(dch + 1) * 1280],
                     K.ksc[dch * 128:(dch + 1) * 128, qs:qs + 1280])
    for tc in range(10):
        spl = E_.dma("scalar", vb[:, tc * 1024:(tc + 1) * 1024],
                     K.vsc[qs + tc * 128:qs + (tc + 1) * 128, :])
    xrl = None
    for dch in range(8):
        xrl = E_.dma("sync", xr[:, dch * 1024:(dch + 1) * 1024],
                     K.DR["xin"][dch * 128:(dch + 1) * 128,
                                 HALO + qs:HALO + qs + 1024])

    def gpsel(qtg, h):
        vi = 1 if qtg == 0 else (2 if qtg == 31 else 0)
        return al["gp"][:, (vi * NH + h) * 384:(vi * NH + h) * 384 + 384]

    PVRANGE = {0: (0, 128), 1: (0, 256), 2: (0, 384), 3: (128, 512),
               4: (256, 512), 5: (384, 512)}

    exp_toks = {}
    nm_toks = {}
    dg_toks = {}
    tr_ev = {}
    pv_ev = {}

    for c in range(8):
        for sp in range(2):
            for qtj in range(4):
                qt = sp * 4 + qtj
                qtg = ap * 8 + qt
                bA = (qt % 2) * 2
                idx = c * 8 + qt

                sl = idx % 4

                def smf(e, c=c, qt=qt, bA=bA):
                    e.matmul(K.ps[bA][0:128, 0:384],
                             qb[0:64, c * 1024 + qt * 128:
                                c * 1024 + qt * 128 + 128],
                             kb[0:64, c * 1280 + qt * 128:
                                c * 1280 + qt * 128 + 384],
                             start=True, stop=True, tile_position=(0, 0))
                    e.matmul(K.ps[bA + 1][0:128, 0:384],
                             qb[64:128, c * 1024 + qt * 128:
                                c * 1024 + qt * 128 + 128],
                             kb[64:128, c * 1280 + qt * 128:
                                c * 1280 + qt * 128 + 384],
                             start=True, stop=True, tile_position=(64, 0))
                w = [spl] + K.wr(("ps", bA), ("ps", bA + 1))
                mm = E_.op("tensor", smf, waits=w, inc=K.sT)

                def emf(e, bA=bA, sl=sl):
                    o = sl * 768
                    e.activation(pn[:, o:o + 384], K.ps[bA][0:128, 0:384],
                                 AF.Exp)
                    return e.activation(pn[:, o + 384:o + 768],
                                        K.ps[bA + 1][0:128, 0:384], AF.Exp)
                w = [mm]
                if idx >= 4:
                    w.append(tr_ev[(idx - 4, 5)])
                ex = E_.op("scalar", emf, waits=w, inc=K.sA)
                exp_toks[idx] = ex
                K.rd(("ps", bA), ex)
                K.rd(("ps", bA + 1), ex)

                def lmf(e, c=c, qtg=qtg, sl=sl):
                    o = sl * 768
                    e.scalar_tensor_tensor(
                        pn[:, o:o + 384], pn[:, o:o + 384], 1.0,
                        gpsel(qtg, 2 * c), ALU.mult, ALU.mult,
                        accum_out=nml[:, sl * 4:sl * 4 + 1])
                    return e.scalar_tensor_tensor(
                        pn[:, o + 384:o + 768], pn[:, o + 384:o + 768], 1.0,
                        gpsel(qtg, 2 * c + 1), ALU.mult, ALU.mult,
                        accum_out=nml[:, sl * 4 + 1:sl * 4 + 2])
                w = [ex, al["load"]]
                if idx >= 4:
                    w.append(dg_toks[idx - 4])
                lm = E_.op("vector", lmf, waits=w, inc=K.sV)

                def nrm(e, sl=sl):
                    K.rtrip(e)
                    return e.reciprocal(nml[:, sl * 4 + 2:sl * 4 + 4],
                                        nml[:, sl * 4:sl * 4 + 2])
                nm = E_.op("vector", nrm, waits=[lm], inc=K.sV)
                nm_toks[idx] = nm

                def dgf(e, sl=sl):
                    p = sl * 256
                    e.tensor_scalar(dg[:, p:p + 128], K.ident[:],
                                    nml[:, sl * 4 + 2:sl * 4 + 3],
                                    None, ALU.mult)
                    return e.tensor_scalar(dg[:, p + 128:p + 256], K.ident[:],
                                           nml[:, sl * 4 + 3:sl * 4 + 4],
                                           None, ALU.mult)
                w = [nm]
                if idx >= 4:
                    w.append(tr_ev[(idx - 4, 5)])
                dgt = E_.op("gpsimd", dgf, waits=w, inc=K.sG)
                dg_toks[idx] = dgt

                def tf6(e, sl=sl):
                    for h2 in range(2):
                        for kk in range(3):
                            o = sl * 768
                            p = sl * 256
                            e.matmul(K.ptp[h2][0:128, kk * 128:kk * 128 + 128],
                                     pn[:, o + h2 * 384 + kk * 128:
                                        o + h2 * 384 + kk * 128 + 128],
                                     dg[:, p + h2 * 128:p + h2 * 128 + 128],
                                     start=True, stop=True)
                w = [dgt] + K.wr(("pbf", 0), ("pbf", 1))
                tt = E_.op("tensor", tf6, waits=w, inc=K.sT)

                def ef6(e, qtj=qtj):
                    r = None
                    for h2 in range(2):
                        for kk in range(3):
                            rr = qtj + kk
                            r = e.tensor_copy(
                                ptg[:, h2 * 3072 + rr * 512 + qtj * 128:
                                    h2 * 3072 + rr * 512 + qtj * 128 + 128],
                                K.ptp[h2][0:128, kk * 128:kk * 128 + 128])
                    return r
                te = E_.op("vector", ef6, waits=[tt], inc=K.sV)
                K.rd(("pbf", 0), te)
                K.rd(("pbf", 1), te)
                tr_ev[(idx, 5)] = te

            spi = c * 2 + sp
            bank = 4 + (spi % 2)

            def pvf(e, c=c, sp=sp, bank=bank):
                for r in range(6):
                    c0, c1 = PVRANGE[r]
                    tc = sp * 4 + r
                    e.matmul(K.ps[bank][64:128, c0:c1],
                             vb[:, tc * 1024 + (2 * c + 1) * 64:
                                tc * 1024 + (2 * c + 1) * 64 + 64],
                             ptg[:, 3072 + r * 512 + c0:3072 + r * 512 + c1],
                             start=(r == 0), stop=(r == 5),
                             tile_position=(0, 64))
                    e.matmul(K.ps[bank][0:64, c0:c1],
                             vb[:, tc * 1024 + (2 * c) * 64:
                                tc * 1024 + (2 * c) * 64 + 64],
                             ptg[:, r * 512 + c0:r * 512 + c1],
                             start=(r == 0), stop=(r == 5),
                             tile_position=(0, 0))
            w = [tr_ev[(c * 8 + sp * 4 + j, 5)] for j in range(4)]
            w += K.wr(("ps", bank))
            pv = E_.op("tensor", pvf, waits=w, inc=K.sT)

            def pve(e, c=c, sp=sp, bank=bank):
                e.tensor_copy(ao[:, c * 1024 + sp * 512:
                                 c * 1024 + sp * 512 + 512],
                              K.ps[bank][0:128, 0:512])
            pev = E_.op("vector", pve, waits=[pv], inc=K.sV)
            K.rd(("ps", bank), pev)
            pv_ev[spi] = pev

    ao_done = pv_ev[15]
    if DEBUG and ap == 0:
        ao_done = E_.dma("sync", K.DBG["dbg_ao"], ao[:, 0:8 * 1024],
                         waits=[ao_done])

    prev_dma = None
    for oc in range(8):
        def mf(e, oc=oc):
            for dch in range(8):
                for st in range(2):
                    e.matmul(K.ps[st][0:128, 0:512],
                             al["wo"][:, dch * 1024 + oc * 128:
                                      dch * 1024 + oc * 128 + 128],
                             ao[:, dch * 1024 + st * 512:
                                dch * 1024 + st * 512 + 512],
                             start=(dch == 0), stop=False)
            for st in range(2):
                e.matmul(K.ps[st][0:128, 0:512],
                         K.biasr[0:1, oc * 128:oc * 128 + 128],
                         K.ones1[0:1, 0:512], start=False, stop=True)
        w = [ao_done, xrl, al["load"]] + K.wr(("ps", 0), ("ps", 1))
        mm = E_.op("tensor", mf, waits=w, inc=K.sT)

        def vf(e, oc=oc):
            for st in range(2):
                e.tensor_add(stage[oc % 2][:, st * 512:st * 512 + 512],
                             K.ps[st][0:128, 0:512],
                             xr[:, oc * 1024 + st * 512:
                                oc * 1024 + st * 512 + 512])
        ev = E_.op("vector", vf, waits=[mm, prev_dma], inc=K.sV)
        K.rd(("ps", 0), ev)
        K.rd(("ps", 1), ev)
        prev_dma = E_.dma("sync", K.x1[oc * 128:(oc + 1) * 128, qs:qs + 1024],
                          stage[oc % 2][:, 0:1024], waits=[ev])
        if DEBUG:
            E_.dma("sync", K.DBG["dbg_x1"][oc * 128:(oc + 1) * 128,
                                           qs:qs + 1024],
                   stage[oc % 2][:, 0:1024], out_sem=K.sO)
    K.pop()


def ph_ffn(K, E_, src, dst, final, dbg=None):
    K.push()
    wd = K.a16(24 * 1024)
    hb = K.a16(24 * 1024)
    u4 = K.a16(8 * 1024)
    wg = K.a16(8 * 768)
    wu = K.a16(8 * 768)
    sg = K.a16(1024)
    xst = [K.a32(1024), K.a32(1024)]
    stage = [K.a32(1024), K.a32(1024)]

    wdl = None
    for kc in range(24):
        wdl = E_.dma("gpsimd", wd[:, kc * 1024:(kc + 1) * 1024],
                     K.DR["wdT"][kc * 128:(kc + 1) * 128, :])

    prev_down = None
    last_gm = None
    last_um = None
    for tq in range(4):
        q0 = tq * 1024
        lnd = emit_ln(K, E_, src, q0, 1024, u4, extra=[last_gm, last_um])
        hdone = []
        for hs in range(4):
            wgl = None
            for dch in range(8):
                wgl = E_.dma("gpsimd", wg[:, dch * 768:(dch + 1) * 768],
                             K.DR["wgT"][dch * 128:(dch + 1) * 128,
                                         hs * 768:(hs + 1) * 768],
                             waits=[last_gm] if dch == 0 else ())
            wul = None
            for dch in range(8):
                wul = E_.dma("gpsimd", wu[:, dch * 768:(dch + 1) * 768],
                             K.DR["wuT"][dch * 128:(dch + 1) * 128,
                                         hs * 768:(hs + 1) * 768],
                             waits=[last_um] if dch == 0 else ())
            for oc in range(6):
                def gmf(e, oc=oc):
                    for dch in range(8):
                        for st in range(2):
                            e.matmul(K.ps[st][0:128, 0:512],
                                     wg[:, dch * 768 + oc * 128:
                                        dch * 768 + oc * 128 + 128],
                                     u4[:, dch * 1024 + st * 512:
                                        dch * 1024 + st * 512 + 512],
                                     start=(dch == 0), stop=(dch == 7))
                gm = E_.op("tensor", gmf,
                           waits=[lnd, wgl] + K.wr(("ps", 0), ("ps", 1)),
                           inc=K.sT)
                last_gm = gm

                def umf(e, oc=oc):
                    for dch in range(8):
                        for st in range(2):
                            e.matmul(K.ps[2 + st][0:128, 0:512],
                                     wu[:, dch * 768 + oc * 128:
                                        dch * 768 + oc * 128 + 128],
                                     u4[:, dch * 1024 + st * 512:
                                        dch * 1024 + st * 512 + 512],
                                     start=(dch == 0), stop=(dch == 7))
                um = E_.op("tensor", umf,
                           waits=[lnd, wul] + K.wr(("ps", 2), ("ps", 3)),
                           inc=K.sT)
                last_um = um

                silu = E_.op("scalar", lambda e: [
                    e.activation(sg[:, 0:512], K.ps[0][0:128, 0:512],
                                 AF.Silu, scale=Q127),
                    e.activation(sg[:, 512:1024], K.ps[1][0:128, 0:512],
                                 AF.Silu, scale=Q127)],
                    waits=[gm, hdone[-1] if hdone else None], inc=K.sA)
                K.rd(("ps", 0), silu)
                K.rd(("ps", 1), silu)

                def hf(e, oc=oc, hs=hs):
                    blk = hs * 6 + oc
                    for st in range(2):
                        e.scalar_tensor_tensor(
                            hb[:, blk * 1024 + st * 512:
                               blk * 1024 + st * 512 + 512],
                            K.ps[2 + st][0:128, 0:512], Q127,
                            sg[:, st * 512:st * 512 + 512],
                            ALU.mult, ALU.mult)
                ht = E_.op("vector", hf, waits=[um, silu], inc=K.sV)
                K.rd(("ps", 2), ht)
                K.rd(("ps", 3), ht)
                hdone.append(ht)

        prev_dma = None
        ev_of = {}
        for oc in range(8):
            xl = E_.dma("sync", xst[oc % 2][:, 0:1024],
                        src[oc * 128:(oc + 1) * 128, q0:q0 + 1024],
                        waits=[ev_of.get(oc - 2, prev_down)])

            def dmf(e, oc=oc):
                for kc in range(24):
                    for st in range(2):
                        e.matmul(K.ps[4 + st][0:128, 0:512],
                                 wd[:, kc * 1024 + oc * 128:
                                    kc * 1024 + oc * 128 + 128],
                                 hb[:, kc * 1024 + st * 512:
                                    kc * 1024 + st * 512 + 512],
                                 start=(kc == 0), stop=(kc == 23))
            dm = E_.op("tensor", dmf, waits=[hdone[-1], wdl] +
                       K.wr(("ps", 4), ("ps", 5)), inc=K.sT)

            def ef(e, oc=oc):
                for st in range(2):
                    e.scalar_tensor_tensor(
                        stage[oc % 2][:, st * 512:st * 512 + 512],
                        K.ps[4 + st][0:128, 0:512], Q127,
                        xst[oc % 2][:, st * 512:st * 512 + 512],
                        ALU.mult, ALU.add)
            ev = E_.op("vector", ef, waits=[dm, xl, prev_dma], inc=K.sV)
            ev_of[oc] = ev
            K.rd(("ps", 4), ev)
            K.rd(("ps", 5), ev)
            osem = K.sO if final else None
            prev_dma = E_.dma("sync", dst[oc * 128:(oc + 1) * 128, q0:q0 + 1024],
                              stage[oc % 2][:, 0:1024], waits=[ev],
                              out_sem=osem)
            if dbg is not None:
                E_.dma("sync", K.DBG[dbg][oc * 128:(oc + 1) * 128,
                                          q0:q0 + 1024],
                       stage[oc % 2][:, 0:1024], out_sem=K.sO)
        prev_down = prev_dma
    K.pop()
    return prev_down


def ph_mamba1(K, E_):
    K.push()
    top = K.mamba_top()
    acs = top["acs"]
    oh = top["oh"]
    gA = K.a16(18 * 1028)
    GBW = 1028
    dtg = K.a32(1024)
    u4 = K.a16(8 * 1024)
    winb = [K.a16(8 * 128), K.a16(8 * 128)]
    xc2 = K.a16(18 * 512)
    xdtT = K.a16(16 * 256)
    xdd = K.a16(2 * 128)
    btok = K.a16(2 * 128)
    cbt = K.a16(2 * 256)
    mt = K.a16(2 * 256)
    ebc = K.a16(256)
    cexp = K.a16(256)
    carryb = K.a16(128)
    dxs = K.a16(256)
    tail3 = K.a16(18 * 16)
    ysl = K.a32(8 * 256)
    carry = K.a32(DIN)
    dtf = K.a32(256)
    dtm = K.a32(256)
    eb2 = K.a32(512)
    dg4 = K.a16(2 * 512)
    decT = K.a32(32)
    dtTb = K.a32(32)
    haccum = K.a32(8 * 54)
    u3 = K.a16(8 * 16)
    hs54 = K.a32(54)

    sm = K.smalls_sb

    ohld = E_.dma("gpsimd", oh[0:16, :], K.DR["oh"])

    x3 = u3
    x3ld = None
    for dch in range(8):
        x3ld = E_.dma("gpsimd", x3[:, dch * 16:dch * 16 + 3],
                      K.x2[dch * 128:(dch + 1) * 128, T - 3:T])

    def mstat(e):
        for dch in range(8):
            e.matmul(K.ps[0][:, 0:3], K.onesb[:], x3[:, dch * 16:dch * 16 + 3],
                     start=(dch == 0), stop=(dch == 7))
    ms = E_.op("tensor", mstat, waits=[x3ld] + K.wr(("ps", 0)), inc=K.sT)

    def msq(e):
        r = None
        for dch in range(8):
            r = e.activation(u3[:, dch * 16 + 8:dch * 16 + 11],
                             x3[:, dch * 16:dch * 16 + 3], AF.Square)
        return r
    ms2 = E_.op("scalar", msq, waits=[x3ld], inc=K.sA)

    def mstat2(e):
        for dch in range(8):
            e.matmul(K.ps[1][:, 0:3], K.onesb[:],
                     u3[:, dch * 16 + 8:dch * 16 + 11],
                     start=(dch == 0), stop=(dch == 7))
    ms3 = E_.op("tensor", mstat2, waits=[ms2] + K.wr(("ps", 1)), inc=K.sT)

    def mv1(e):
        e.tensor_scalar(decT[:, 0:3], K.ps[0][:, 0:3], 1.0 / 1024, None,
                        ALU.mult)
        e.tensor_scalar(dtTb[:, 0:3], K.ps[1][:, 0:3], 1.0 / 1024, None,
                        ALU.mult)
    mv = E_.op("vector", mv1, waits=[ms, ms3], inc=K.sV)
    K.rd(("ps", 0), mv)
    K.rd(("ps", 1), mv)
    mv2 = E_.op("scalar", lambda e: e.activation(
        dtf[:, 0:3], decT[:, 0:3], AF.Square), waits=[mv], inc=K.sA)
    mv3 = E_.op("vector", lambda e: e.tensor_sub(
        dtTb[:, 0:3], dtTb[:, 0:3], dtf[:, 0:3]), waits=[mv2], inc=K.sV)
    mv4 = E_.op("scalar", lambda e: e.activation(
        dtTb[:, 0:3], dtTb[:, 0:3], AF.Sqrt, bias=EPS), waits=[mv3],
        inc=K.sA)

    def mv5a(e):
        for dch in range(8):
            e.tensor_sub(eb2[:, dch * 16:dch * 16 + 3],
                         x3[:, dch * 16:dch * 16 + 3], decT[:, 0:3])
    sub_d = E_.op("gpsimd", mv5a, waits=[mv, x3ld], inc=K.sG)

    def mv5(e):
        e.reciprocal(dtTb[:, 0:3], dtTb[:, 0:3])
        K.rtrip(e)
        for dch in range(8):
            e.tensor_mul(u3[:, dch * 16 + 4:dch * 16 + 7],
                         eb2[:, dch * 16:dch * 16 + 3], dtTb[:, 0:3])
    u3d = E_.op("vector", mv5, waits=[mv4, sub_d], inc=K.sV)

    hprev = None
    for oc in range(18):
        wl = None
        for dch in range(8):
            wl = E_.dma("gpsimd",
                        winb[oc % 2][:, dch * 128:dch * 128 + 128],
                        K.DR["winT"][dch * 128:(dch + 1) * 128,
                                     DIN + oc * 128:DIN + oc * 128 + 128],
                        waits=[hprev] if dch == 0 else ())

        def hmf(e, oc=oc):
            for dch in range(8):
                e.matmul(K.ps[2][0:128, 0:3],
                         winb[oc % 2][:, dch * 128:dch * 128 + 128],
                         u3[:, dch * 16 + 4:dch * 16 + 7],
                         start=(dch == 0), stop=(dch == 7))
        hm = E_.op("tensor", hmf, waits=[wl, u3d] + K.wr(("ps", 2)),
                   inc=K.sT)
        hprev = E_.op("vector", lambda e, oc=oc: e.tensor_copy(
            hs54[:, oc * 3:oc * 3 + 3], K.ps[2][0:128, 0:3]),
            waits=[hm], inc=K.sV)
        K.rd(("ps", 2), hprev)
    hdma = E_.dma("sync", K.ch_in[:, :], hs54[:, 0:54], waits=[hprev])

    def ccf(e):
        return e.collective_compute(
            "AllGather", ALU.bypass, replica_groups=[list(range(8))],
            ins=[K.ch_in[:, :].opt()], outs=[K.ch_out[:, :].opt()])
    cc1 = E_.op("gpsimd", ccf, waits=[hdma], inc=K.sP)
    hl = None
    for j in range(8):
        hl = E_.dma("sync", haccum[:, j * 54:(j + 1) * 54],
                    K.ch_out[j * 128:(j + 1) * 128, :], waits=[cc1])

    def hmask(e):
        e.tensor_scalar(K.halo3[:], haccum[:, 0:54], K.rdmask_sb[:, 0:1],
                        None, ALU.mult)
        for j in range(1, 8):
            K.rtrip(e)
            e.scalar_tensor_tensor(K.halo3[:], haccum[:, j * 54:(j + 1) * 54],
                                   K.rdmask_sb[:, j:j + 1], K.halo3[:],
                                   ALU.mult, ALU.add)
    halo_done = E_.op("vector", hmask, waits=[hl], inc=K.sV)

    cz = E_.op("vector", lambda e: e.memset(carry[:, :], 0.0), inc=K.sV)

    mm_of_oc = {}
    group_done = {}
    chunk_state = {}

    def emit_inproj(g, gb_free, winb_free=None):
        lnd = emit_ln(K, E_, K.x2, g * 1024, 1024, u4,
                      extra=[mm_of_oc.get(("last", 0))])
        prev_ev = None
        for oc in range(19):
            ow = 128 if oc < 18 else 16
            wl = None
            for dch in range(8):
                wl = E_.dma("gpsimd",
                            winb[oc % 2][:, dch * 128:dch * 128 + ow],
                            K.DR["winT"][dch * 128:(dch + 1) * 128,
                                         DIN + oc * 128:DIN + oc * 128 + ow],
                            waits=[mm_of_oc.get((g, oc - 2),
                                                mm_of_oc.get(("last", 0))),
                                   winb_free if oc < 2 else None]
                            if dch == 0 else ())

            def mf(e, oc=oc, ow=ow):
                op_ = min(ow, 128)
                for dch in range(8):
                    for st in range(2):
                        e.matmul(K.ps[st][0:op_, 0:512],
                                 winb[oc % 2][:, dch * 128:dch * 128 + ow],
                                 u4[:, dch * 1024 + st * 512:
                                    dch * 1024 + st * 512 + 512],
                                 start=(dch == 0), stop=(dch == 7))
            w = [lnd, wl, prev_ev] + K.wr(("ps", 0), ("ps", 1))
            mm = E_.op("tensor", mf, waits=w, inc=K.sT)
            mm_of_oc[(g, oc)] = mm
            mm_of_oc[("last", 0)] = mm

            def vf(e, oc=oc, ow=ow):
                for st in range(2):
                    if ow == 128:
                        e.tensor_copy(gA[:, oc * GBW + 4 + st * 512:
                                         oc * GBW + 4 + st * 512 + 512],
                                      K.ps[st][0:128, 0:512])
                    else:
                        e.tensor_copy(dtg[0:16, st * 512:st * 512 + 512],
                                      K.ps[st][0:16, 0:512])
            prev_ev = E_.op("vector", vf, waits=[mm, gb_free], inc=K.sV)
            K.rd(("ps", 0), prev_ev)
            K.rd(("ps", 1), prev_ev)
        return prev_ev

    W_ETM = K.etm

    def emit_chunk(j):
        g = j // 4
        jr = j % 4
        gdone = group_done[g]

        po = jr % 2
        if po == 0:
            if jr == 0:
                def ctxf(e, j=j):
                    e.tensor_copy(
                        gA[:, 0:18 * GBW].rearrange("p (a k) -> p a k", a=18)
                        [:, :, 1:4],
                        (K.halo3 if j == 0 else tail3)
                        [:, 0:18 * (3 if j == 0 else 16)]
                        .rearrange("p (a k) -> p a k", a=18)[:, :, 0:3])
                ctx = E_.op("vector", ctxf, waits=[gdone,
                                                   halo_done if j == 0
                                                   else None], inc=K.sV)
            else:
                ctx = None
            conv_sil = []
            dg_mm = {}
            for cch in range(18):
                slot = cch % 2

                def dgf(e, cch=cch, slot=slot):
                    for k in range(4):
                        e.tensor_scalar(
                            dg4[:, slot * 512 + k * 128:
                                slot * 512 + k * 128 + 128],
                            K.ident[:],
                            K.convw_sb[:, cch * 4 + k:cch * 4 + k + 1],
                            None, ALU.mult)
                dg = E_.op("vector", dgf,
                           waits=[ctx, dg_mm.get(cch - 2),
                                  chunk_state.get((j - 1, "ye15"))],
                           inc=K.sV)

                bank = 4 + (cch % 2)

                def cmf(e, cch=cch, slot=slot, jr=jr, bank=bank):
                    bb = cch * GBW
                    for k in range(4):
                        e.matmul(K.ps[bank][0:128, 0:512],
                                 dg4[:, slot * 512 + k * 128:
                                     slot * 512 + k * 128 + 128],
                                 gA[:, bb + 1 + jr * 256 + k:
                                    bb + 1 + jr * 256 + k + 512],
                                 start=(k == 0), stop=(k == 3))
                cm = E_.op("tensor", cmf,
                           waits=[dg, gdone] + K.wr(("ps", bank)), inc=K.sT)
                dg_mm[cch] = cm

                si = E_.op("scalar", lambda e, cch=cch, bank=bank:
                           e.activation(xc2[:, cch * 512:(cch + 1) * 512],
                                        K.ps[bank][0:128, 0:512], AF.Silu,
                                        bias=K.convb_sb[:, cch:cch + 1]),
                           waits=[cm, chunk_state.get((j - 1, "ye15"))],
                           inc=K.sA)
                K.rd(("ps", bank), si)
                conv_sil.append(si)
            chunk_state[(j, "convdone")] = conv_sil[-1]
        conv_done = chunk_state[(j - po, "convdone")]
        xc = xc2[:, po * 256:]

        def t3f(e, jr=jr):
            e.tensor_copy(
                tail3[:, 0:18 * 16].rearrange("p (a k) -> p a k", a=18)
                [:, :, 0:3],
                gA[:, 0:18 * GBW].rearrange("p (a k) -> p a k", a=18)
                [:, :, 4 + jr * 256 + 253:4 + jr * 256 + 256])
        t3 = E_.op("vector", t3f, waits=[gdone, conv_done], inc=K.sV)

        def dtc(e, jr=jr):
            e.tensor_scalar(dtf[0:16, :], dtg[0:16, jr * 256:(jr + 1) * 256],
                            sm[0:16, 0:1], None, ALU.add)
        d1 = E_.op("vector", dtc,
                   waits=[gdone, chunk_state.get((j - 1, "l2")),
                          chunk_state.get((j - 1, "dtT_done"))], inc=K.sV)
        d2 = E_.op("scalar", lambda e: [
            e.activation(dtm[0:16, 0:256], dtf[0:16, 0:256], AF.Abs),
            e.activation(dtm[0:16, 0:256], dtm[0:16, 0:256], AF.Exp,
                         scale=-1.0)],
            waits=[d1], inc=K.sA)
        d3 = E_.op("scalar", lambda e: e.activation(
            dtm[0:16, 0:256], dtm[0:16, 0:256], AF.Ln, bias=1.0),
            waits=[d2], inc=K.sA)

        def dtc2(e):
            e.tensor_relu(dtf[0:16, :], dtf[0:16, :])
            e.tensor_add(dtf[0:16, :], dtf[0:16, :], dtm[0:16, :])
            e.tensor_scalar(dtm[0:16, :], dtf[0:16, :], sm[0:16, 3:4],
                            None, ALU.mult)
        d4 = E_.op("vector", dtc2, waits=[d3], inc=K.sV)

        def scanf(e, j=j):
            e.tensor_tensor_scan(acs[0:16, j * 256:(j + 1) * 256],
                                 dtm[0:16, :], K.zeros16[0:16, :], 0.0,
                                 ALU.add, ALU.add)
        d5 = E_.op("vector", scanf, waits=[d4], inc=K.sV)
        d5b = E_.op("gpsimd", lambda e, j=j: e.tensor_copy(
            K.lasts[0:16, j:j + 1],
            acs[0:16, j * 256 + 255:j * 256 + 256]), waits=[d5], inc=K.sG)

        acsT_toks = []
        for sh in range(2):
            tt = E_.op("tensor", lambda e, j=j, sh=sh: e.transpose(
                K.ps[2][0:128, 0:16],
                acs[0:16, j * 256 + sh * 128:j * 256 + sh * 128 + 128],
                K.identf[0:16, 0:16]),
                waits=[d5] + K.wr(("ps", 2)), inc=K.sT)
            ev = E_.op("vector", lambda e, j=j, sh=sh: e.tensor_copy(
                K.acsT_g[:, j * 32 + sh * 16:j * 32 + sh * 16 + 16],
                K.ps[2][0:128, 0:16]), waits=[tt], inc=K.sV)
            K.rd(("ps", 2), ev)
            acsT_toks.append(ev)
        acsT_done = acsT_toks[-1]
        dtT_toks = []
        for sh in range(2):
            tt = E_.op("tensor", lambda e, sh=sh: e.transpose(
                K.ps[2][0:128, 0:16], dtf[0:16, sh * 128:sh * 128 + 128],
                K.identf[0:16, 0:16]),
                waits=[d4] + K.wr(("ps", 2)), inc=K.sT)
            ev = E_.op("vector", lambda e, sh=sh: e.tensor_copy(
                dtTb[:, sh * 16:sh * 16 + 16], K.ps[2][0:128, 0:16]),
                waits=[tt, chunk_state.get((j - 1, "xdt31"))], inc=K.sV)
            K.rd(("ps", 2), ev)
            dtT_toks.append(ev)
        dtT_done = dtT_toks[-1]

        l1 = E_.op("vector", lambda e, j=j: e.tensor_scalar(
            dtm[0:16, 0:32], K.e16_sb[0:16, 0:32], K.lasts[0:16, j:j + 1],
            None, ALU.mult), waits=[d5b], inc=K.sV)
        l2 = E_.op("tensor", lambda e: e.matmul(
            K.ps[3][0:128, 0:32], K.ones16f[0:16, :], dtm[0:16, 0:32],
            start=True, stop=True), waits=[l1] + K.wr(("ps", 3)), inc=K.sT)
        l3 = E_.op("vector", lambda e, j=j: e.tensor_sub(
            decT[:, 0:32], K.ps[3][0:128, 0:32],
            K.acsT_g[:, j * 32:j * 32 + 32]),
            waits=[l2, acsT_done, chunk_state.get((j - 1, "xdd15"))],
            inc=K.sV)

        def l4f(e, j=j):
            e.activation(decT[:, 0:32], decT[:, 0:32], AF.Exp)
            return e.activation(K.cdecb[:, j * 16:(j + 1) * 16],
                                K.ps[3][0:128, 0:16], AF.Exp)
        l4 = E_.op("scalar", l4f, waits=[l3], inc=K.sA)
        K.rd(("ps", 3), l4)

        xt_ev = {}
        for h in range(16):
            for sh in range(2):
                i = h * 2 + sh
                bank = i % 2

                def tf(e, h=h, sh=sh, bank=bank):
                    e.transpose(K.pbf[bank][0:128, 0:128],
                                xc[:, h * 512 + sh * 128:
                                   h * 512 + sh * 128 + 128],
                                K.ident[:])
                tt = E_.op("tensor", tf,
                           waits=[conv_done] + K.wr(("pbf", bank)), inc=K.sT)

                def ef(e, h=h, sh=sh, bank=bank):
                    e.tensor_scalar(
                        xdtT[:, h * 256 + sh * 128:h * 256 + sh * 128 + 128],
                        K.pbf[bank][0:128, 0:128],
                        dtTb[:, sh * 16 + h:sh * 16 + h + 1], None, ALU.mult)
                ev = E_.op("vector", ef, waits=[tt, dtT_done], inc=K.sV)
                K.rd(("pbf", bank), ev)
                xt_ev[i] = ev
        xdt_done = xt_ev[31]

        bt_toks = []
        for sh in range(2):
            bank = sh % 2
            tt = E_.op("tensor", lambda e, sh=sh, bank=bank: e.transpose(
                K.pbf[bank][0:128, 128:256],
                xc[:, 16 * 512 + sh * 128:16 * 512 + sh * 128 + 128],
                K.ident[:]),
                waits=[conv_done] + K.wr(("pbf", bank)), inc=K.sT)
            ev = E_.op("vector", lambda e, sh=sh, bank=bank: e.tensor_copy(
                btok[:, sh * 128:sh * 128 + 128],
                K.pbf[bank][0:128, 128:256]),
                waits=[tt, chunk_state.get((j - 1, "stm15"))], inc=K.sV)
            K.rd(("pbf", bank), ev)
            bt_toks.append(ev)
        btok_done = bt_toks[-1]

        E_.dma("sync", K.cfms[:, j * 256:(j + 1) * 256],
               xc[:, 17 * 512:17 * 512 + 256], waits=[conv_done])

        cb_toks = []
        for sh in range(2):
            mmc = E_.op("tensor", lambda e, sh=sh: e.matmul(
                K.ps[sh][0:128, 0:256],
                xc[:, 16 * 512 + sh * 128:16 * 512 + sh * 128 + 128],
                xc[:, 17 * 512:17 * 512 + 256], start=True, stop=True),
                waits=[conv_done] + K.wr(("ps", sh)), inc=K.sT)
            ev = E_.op("vector", lambda e, sh=sh: e.tensor_copy(
                cbt[:, sh * 256:(sh + 1) * 256], K.ps[sh][0:128, 0:256]),
                waits=[mmc, chunk_state.get((j - 1, "mt15"))], inc=K.sV)
            K.rd(("ps", sh), ev)
            cb_toks.append(ev)
        cbt_done = cb_toks[-1]

        prev = {}
        for kk2 in ("ym", "cu", "ye", "stm", "mt", "cexp"):
            prev[(kk2, -1)] = chunk_state.get((j - 1, kk2 + "15"))
        for h in range(16):
            ab = 2 + (h % 2)
            yb = 4 + (h % 2)
            stb = h % 2

            bca = E_.op("tensor", lambda e, h=h, j=j, ab=ab: e.matmul(
                K.ps[ab][0:128, 0:256],
                oh[0:16, h * 128:(h + 1) * 128],
                acs[0:16, j * 256:(j + 1) * 256],
                start=True, stop=True),
                waits=[d5, ohld] + K.wr(("ps", ab)), inc=K.sT)

            def ebf(e, h=h, j=j, ab=ab):
                for sh in range(2):
                    e.scalar_tensor_tensor(
                        eb2[:, sh * 256:(sh + 1) * 256],
                        K.ps[ab][0:128, 0:256],
                        K.acsT_g[:, j * 32 + sh * 16 + h:
                                 j * 32 + sh * 16 + h + 1],
                        W_ETM[:, sh * 256:(sh + 1) * 256],
                        ALU.subtract, ALU.add)
            eb = E_.op("vector", ebf,
                       waits=[bca, acsT_done,
                              prev.get(("mt", h - 1), prev[("mt", -1)])],
                       inc=K.sV)

            def exf(e, ab=ab):
                e.activation(ebc[:, 0:256], K.ps[ab][0:128, 0:256], AF.Exp)
                return e.activation(eb2[:, 0:512], eb2[:, 0:512], AF.Exp)
            ex = E_.op("scalar", exf,
                       waits=[eb, prev.get(("cexp", h - 1),
                                           prev[("cexp", -1)])], inc=K.sA)
            K.rd(("ps", ab), ex)

            def mtf(e, h=h):
                for sh in range(2):
                    e.tensor_mul(mt[:, sh * 256:(sh + 1) * 256],
                                 eb2[:, sh * 256:(sh + 1) * 256],
                                 cbt[:, sh * 256:(sh + 1) * 256])
                e.tensor_mul(cexp[:, 0:256], xc[:, 17 * 512:17 * 512 + 256],
                             ebc[:, 0:256])
            mtt = E_.op("gpsimd", mtf,
                        waits=[ex, cbt_done,
                               prev.get(("ym", h - 1), prev[("ym", -1)])],
                        inc=K.sG)
            prev[("mt", h)] = mtt
            prev[("cexp", h)] = mtt

            def xdf(e, h=h):
                for sh in range(2):
                    e.tensor_scalar(xdd[:, sh * 128:(sh + 1) * 128],
                                    xdtT[:, h * 256 + sh * 128:
                                         h * 256 + sh * 128 + 128],
                                    decT[:, sh * 16 + h:sh * 16 + h + 1],
                                    None, ALU.mult)
            xdt2 = E_.op("vector", xdf,
                         waits=[xdt_done, l4,
                                prev.get(("stm", h - 1), prev[("stm", -1)])],
                         inc=K.sV)

            stm = E_.op("tensor", lambda e, stb=stb: [
                e.matmul(K.ps[stb][0:128, 0:128], btok[:, 0:128],
                         xdd[:, 0:128], start=True, stop=False),
                e.matmul(K.ps[stb][0:128, 0:128], btok[:, 128:256],
                         xdd[:, 128:256], start=False, stop=True)],
                waits=[btok_done, xdt2] + K.wr(("ps", stb)), inc=K.sT)
            prev[("stm", h)] = stm

            cbk = E_.op("gpsimd", lambda e, h=h: e.tensor_copy(
                carryb[:, 0:128], carry[:, h * 128:(h + 1) * 128]),
                waits=[cz, prev.get(("cu", h - 1), prev[("cu", -1)]),
                       prev.get(("ym", h - 1), prev[("ym", -1)])],
                inc=K.sG)

            ym = E_.op("tensor", lambda e, h=h, yb=yb: [
                e.matmul(K.ps[yb][0:128, 0:256], carryb[:, 0:128],
                         cexp[:, 0:256], start=True, stop=False),
                e.matmul(K.ps[yb][0:128, 0:256],
                         xdtT[:, h * 256:h * 256 + 128],
                         mt[:, 0:256], start=False, stop=False),
                e.matmul(K.ps[yb][0:128, 0:256],
                         xdtT[:, h * 256 + 128:h * 256 + 256],
                         mt[:, 256:512], start=False, stop=True)],
                waits=[cbk, mtt] + K.wr(("ps", yb)), inc=K.sT)
            prev[("ym", h)] = ym

            dx = E_.op("gpsimd", lambda e, h=h: e.tensor_scalar(
                dxs[:, 0:256], xc[:, h * 512:h * 512 + 256],
                K.dcols[:, h:h + 1], None, ALU.mult),
                waits=[conv_done,
                       prev.get(("ye", h - 1), prev[("ye", -1)])], inc=K.sG)

            def yef(e, h=h, yb=yb, j=j):
                e.scalar_tensor_tensor(
                    ysl[:, (h % 8) * 256:(h % 8) * 256 + 256],
                    K.ps[yb][0:128, 0:256], 1.0, dxs[:, 0:256],
                    ALU.mult, ALU.add)
            ye = E_.op("vector", yef,
                       waits=[ym, dx,
                              chunk_state.get((j, "ysl0")) if h >= 8 else
                              chunk_state.get((j - 1, "ysl1"))],
                       inc=K.sV)
            K.rd(("ps", yb), ye)
            prev[("ye", h)] = ye
            if h == 7:
                chunk_state[(j, "ysl0")] = E_.dma(
                    "sync",
                    K.ysp[:, :].rearrange("p (a t) -> p a t", a=16)
                    [:, 0:8, j * 256:(j + 1) * 256],
                    ysl[:, 0:2048].rearrange("p (a t) -> p a t", a=8),
                    waits=[ye])
            if h == 15:
                chunk_state[(j, "ysl1")] = E_.dma(
                    "sync",
                    K.ysp[:, :].rearrange("p (a t) -> p a t", a=16)
                    [:, 8:16, j * 256:(j + 1) * 256],
                    ysl[:, 0:2048].rearrange("p (a t) -> p a t", a=8),
                    waits=[ye])

            def cuf(e, h=h, stb=stb, j=j):
                e.scalar_tensor_tensor(
                    carry[:, h * 128:(h + 1) * 128],
                    carry[:, h * 128:(h + 1) * 128],
                    K.cdecb[:, j * 16 + h:j * 16 + h + 1],
                    K.ps[stb][0:128, 0:128], ALU.mult, ALU.add)
            cu = E_.op("vector", cuf, waits=[stm, cbk, l4], inc=K.sV)
            K.rd(("ps", stb), cu)
            prev[("cu", h)] = cu

        for kk2 in ("ym", "cu", "ye", "stm", "mt", "cexp"):
            chunk_state[(j, kk2 + "15")] = prev[(kk2, 15)]
        chunk_state[(j, "carry")] = prev[("cu", 15)]
        chunk_state[(j, "free")] = t3
        chunk_state[(j, "l2")] = l2
        chunk_state[(j, "dtT_done")] = dtT_done
        chunk_state[(j, "xdt31")] = xt_ev[31]
        chunk_state[(j, "xdd15")] = prev[("stm", 15)]
        chunk_state[(j, "stm15")] = prev[("stm", 15)]
        chunk_state[(j, "mt15")] = prev[("mt", 15)]

    for g in range(4):
        gfree = chunk_state.get((g * 4 - 1, "free"))
        gd = emit_inproj(g, gfree, winb_free=hprev if g == 0 else None)
        group_done[g] = gd
        for j in range(g * 4, g * 4 + 4):
            emit_chunk(j)

    st_dma = E_.dma("sync", K.cc_in[:, :], carry[:, :],
                    waits=[chunk_state[(15, "carry")]])

    def ccf2(e):
        return e.collective_compute(
            "AllGather", ALU.bypass, replica_groups=[list(range(8))],
            ins=[K.cc_in[:, :].opt()], outs=[K.cc_out[:, :].opt()])
    cc2 = E_.op("gpsimd", ccf2, waits=[st_dma], inc=K.sP)

    sbb = [u4.bitcast(F32)[:, 0:2048], u4.bitcast(F32)[:, 2048:4096]]
    acc = None
    for jj in range(8):
        ldt = E_.dma("sync", sbb[jj % 2][:, 0:2048],
                     K.cc_out[jj * 128:(jj + 1) * 128, :],
                     waits=[cc2, acc, mm_of_oc.get(("last", 0))])
        if jj == 0:
            acc = E_.op("vector", lambda e: e.tensor_scalar(
                top["sin"][:, :], sbb[0][:, 0:2048], K.rdmask_sb[:, 0:1],
                None, ALU.mult), waits=[ldt], inc=K.sV)
        else:
            acc = E_.op("vector", lambda e, jj=jj: e.scalar_tensor_tensor(
                top["sin"][:, :], sbb[jj % 2][:, 0:2048],
                K.rdmask_sb[:, jj:jj + 1], top["sin"][:, :],
                ALU.mult, ALU.add), waits=[ldt], inc=K.sV)
    sin_done = E_.op("vector", lambda e: e.tensor_copy(
        top["sinb"][:, :], top["sin"][:, :]), waits=[acc], inc=K.sV)

    pj1 = E_.op("vector", lambda e: e.tensor_tensor_scan(
        K.lasts[0:16, 20:36], K.lasts[0:16, 0:16],
        K.zeros16[0:16, 0:16], 0.0, ALU.add, ALU.add),
        waits=[chunk_state[(15, "carry")]], inc=K.sV)

    def pjshift(e):
        e.memset(K.lasts[0:16, 17:18], 0.0)
        e.tensor_copy(K.lasts[0:16, 18:18 + 15], K.lasts[0:16, 20:35])
    pj2 = E_.op("gpsimd", pjshift, waits=[pj1], inc=K.sG)
    pj3 = E_.op("scalar", lambda e: e.activation(
        K.pj_sb[0:16, 0:16], K.lasts[0:16, 17:33], AF.Exp),
        waits=[pj2], inc=K.sA)

    pjb_last = None
    for j in range(16):
        pd = E_.op("vector", lambda e, j=j: e.tensor_scalar(
            dtm[0:16, 0:16], K.e16_sb[0:16, 0:16], K.pj_sb[0:16, j:j + 1],
            None, ALU.mult), waits=[pj3, pjb_last], inc=K.sV)
        pm = E_.op("tensor", lambda e: e.matmul(
            K.ps[2][0:128, 0:16], K.ones16f[0:16, :], dtm[0:16, 0:16],
            start=True, stop=True), waits=[pd] + K.wr(("ps", 2)), inc=K.sT)
        pe = E_.op("vector", lambda e, j=j: e.tensor_copy(
            K.pj128[:, j * 16:(j + 1) * 16], K.ps[2][0:128, 0:16]),
            waits=[pm], inc=K.sV)
        K.rd(("ps", 2), pe)
        pjb_last = pe

    K.pop()
    return dict(sin=sin_done, pj=pjb_last,
                ydone=chunk_state[(15, "ysl1")])


def ph_mamba2(K, E_, m1):
    K.push()
    top = K.mamba_top()
    acs = top["acs"]
    oh = top["oh"]
    sz = K.a16(16 * 1024)
    wout = K.a16(16 * 1024)
    u4 = K.a16(8 * 1024)
    wzb = [K.a16(8 * 128), K.a16(8 * 128)]
    ybuf = K.a32(16 * 256)
    cfb = K.a16(256)
    ebc2 = K.a16(2 * 256)
    g2b = K.a16(16 * 256)
    xst = [K.a32(1024), K.a32(1024)]
    stage = [K.a32(1024), K.a32(1024)]
    rbc = K.a32(256)

    wol = None
    for kc in range(16):
        wol = E_.dma("gpsimd", wout[:, kc * 1024:(kc + 1) * 1024],
                     K.DR["woutT"][kc * 128:(kc + 1) * 128, :])

    zmm_of = {}
    prev_out_dma = None
    for tg in range(4):
        q0 = tg * 1024
        lnd = emit_ln(K, E_, K.x2, q0, 1024, u4,
                      extra=[zmm_of.get((tg - 1, 15))])

        zprev = None
        for zc in range(16):
            wl = None
            for dch in range(8):
                wl = E_.dma("gpsimd",
                            wzb[zc % 2][:, dch * 128:(dch + 1) * 128],
                            K.DR["winT"][dch * 128:(dch + 1) * 128,
                                         zc * 128:(zc + 1) * 128],
                            waits=[zmm_of.get((tg, zc - 2),
                                              zmm_of.get((tg - 1, 14 + zc)))]
                            if dch == 0 else ())

            def zmf(e, zc=zc):
                for dch in range(8):
                    for st in range(2):
                        e.matmul(K.ps[st][0:128, 0:512],
                                 wzb[zc % 2][:, dch * 128:(dch + 1) * 128],
                                 u4[:, dch * 1024 + st * 512:
                                    dch * 1024 + st * 512 + 512],
                                 start=(dch == 0), stop=(dch == 7))
            zm = E_.op("tensor", zmf,
                       waits=[lnd, wl, prev_out_dma] +
                       K.wr(("ps", 0), ("ps", 1)), inc=K.sT)
            zmm_of[(tg, zc)] = zm
            zprev = E_.op("scalar", lambda e, zc=zc: [
                e.activation(sz[:, zc * 1024:zc * 1024 + 512],
                             K.ps[0][0:128, 0:512], AF.Silu),
                e.activation(sz[:, zc * 1024 + 512:(zc + 1) * 1024],
                             K.ps[1][0:128, 0:512], AF.Silu)],
                waits=[zm, prev_out_dma], inc=K.sA)
            K.rd(("ps", 0), zprev)
            K.rd(("ps", 1), zprev)
        z_done = zprev

        gn_done = []
        prev_ctok = [None]
        for jj in range(4):
            j = tg * 4 + jj
            yl = E_.dma("sync",
                        ybuf[:, 0:4096].rearrange("p (a t) -> p a t", a=16),
                        K.ysp[:, :].rearrange("p (a t) -> p a t", a=16)
                        [:, :, j * 256:(j + 1) * 256],
                        waits=[m1["ydone"],
                               gn_done[-1] if gn_done else None])
            cl = E_.dma("sync", cfb[:, 0:256],
                        K.cfms[:, j * 256:(j + 1) * 256],
                        waits=[prev_ctok[0]])
            ctoks = []
            for h in range(16):
                ab = 2 + (h % 2)
                yb2 = 4 + (h % 2)
                eo = (h % 2) * 256
                bca = E_.op("tensor", lambda e, h=h, j=j, ab=ab: e.matmul(
                    K.ps[ab][0:128, 0:256],
                    oh[0:16, h * 128:(h + 1) * 128],
                    acs[0:16, j * 256:(j + 1) * 256],
                    start=True, stop=True),
                    waits=[m1["sin"], m1["pj"], yl, cl] + K.wr(("ps", ab)),
                    inc=K.sT)
                exb = E_.op("scalar", lambda e, ab=ab, eo=eo: e.activation(
                    ebc2[:, eo:eo + 256], K.ps[ab][0:128, 0:256], AF.Exp),
                    waits=[bca, ctoks[-2] if len(ctoks) >= 2 else None],
                    inc=K.sA)
                K.rd(("ps", ab), exb)
                cx = E_.op("vector", lambda e, eo=eo: e.tensor_mul(
                    ebc2[:, eo:eo + 256], cfb[:, 0:256], ebc2[:, eo:eo + 256]),
                    waits=[exb, cl], inc=K.sV)
                ym = E_.op("tensor", lambda e, h=h, yb2=yb2, eo=eo: e.matmul(
                    K.ps[yb2][0:128, 0:256],
                    top["sinb"][:, h * 128:(h + 1) * 128],
                    ebc2[:, eo:eo + 256],
                    start=True, stop=True),
                    waits=[cx, m1["sin"]] + K.wr(("ps", yb2)), inc=K.sT)
                ce = E_.op("vector",
                           lambda e, h=h, j=j, yb2=yb2:
                           e.scalar_tensor_tensor(
                               ybuf[:, h * 256:(h + 1) * 256],
                               K.ps[yb2][0:128, 0:256],
                               K.pj128[:, j * 16 + h:j * 16 + h + 1],
                               ybuf[:, h * 256:(h + 1) * 256],
                               ALU.mult, ALU.add),
                           waits=[ym, yl], inc=K.sV)
                K.rd(("ps", yb2), ce)
                ctoks.append(ce)
            corr_done = ctoks[-1]
            prev_ctok[0] = corr_done

            def gf(e, jj=jj):
                e.tensor_mul(
                    ybuf[:, 0:4096].rearrange("p (a t) -> p a t", a=16),
                    ybuf[:, 0:4096].rearrange("p (a t) -> p a t", a=16),
                    sz[:, 0:16 * 1024].rearrange("p (a t) -> p a t", a=16)
                    [:, :, jj * 256:(jj + 1) * 256])
            gt = E_.op("vector", gf, waits=[corr_done, z_done], inc=K.sV)

            sq = E_.op("scalar", lambda e: e.activation(
                g2b[:, 0:4096], ybuf[:, 0:4096], AF.Square), waits=[gt],
                inc=K.sA)

            def rmf(e):
                for blk in range(16):
                    e.matmul(K.ps[2][0:128, 0:256], K.onesb[:],
                             g2b[:, blk * 256:(blk + 1) * 256],
                             start=(blk == 0), stop=(blk == 15))
            rm = E_.op("tensor", rmf, waits=[sq] + K.wr(("ps", 2)), inc=K.sT)
            r2 = E_.op("scalar", lambda e: e.activation(
                rbc[:, 0:256], K.ps[2][0:128, 0:256], AF.Sqrt,
                scale=1.0 / DIN, bias=EPS), waits=[rm], inc=K.sA)
            K.rd(("ps", 2), r2)

            def gnf(e, jj=jj):
                e.reciprocal(rbc[:, 0:256], rbc[:, 0:256])
                K.rtrip(e)
                for blk in range(16):
                    e.scalar_tensor_tensor(
                        sz[:, blk * 1024 + jj * 256:
                           blk * 1024 + jj * 256 + 256],
                        ybuf[:, blk * 256:(blk + 1) * 256],
                        K.normw_sb[:, blk:blk + 1], rbc[:, 0:256],
                        ALU.mult, ALU.mult)
            g4 = E_.op("vector", gnf, waits=[r2, gt], inc=K.sV)
            gn_done.append(g4)

        prev_dma = None
        ev_of = {}
        for oc in range(8):
            xl = E_.dma("sync", xst[oc % 2][:, 0:1024],
                        K.x2[oc * 128:(oc + 1) * 128, q0:q0 + 1024],
                        waits=[ev_of.get(oc - 2, prev_out_dma)])

            def omf(e, oc=oc):
                for kc in range(16):
                    for st in range(2):
                        e.matmul(K.ps[st][0:128, 0:512],
                                 wout[:, kc * 1024 + oc * 128:
                                      kc * 1024 + oc * 128 + 128],
                                 sz[:, kc * 1024 + st * 512:
                                    kc * 1024 + st * 512 + 512],
                                 start=(kc == 0), stop=(kc == 15))
            om = E_.op("tensor", omf, waits=[gn_done[-1], wol] +
                       K.wr(("ps", 0), ("ps", 1)), inc=K.sT)

            def oef(e, oc=oc):
                for st in range(2):
                    e.tensor_add(stage[oc % 2][:, st * 512:st * 512 + 512],
                                 K.ps[st][0:128, 0:512],
                                 xst[oc % 2][:, st * 512:st * 512 + 512])
            ev = E_.op("vector", oef, waits=[om, xl, prev_dma], inc=K.sV)
            ev_of[oc] = ev
            K.rd(("ps", 0), ev)
            K.rd(("ps", 1), ev)
            prev_dma = E_.dma("sync",
                              K.x3[oc * 128:(oc + 1) * 128, q0:q0 + 1024],
                              stage[oc % 2][:, 0:1024], waits=[ev])
            if DEBUG:
                E_.dma("sync", K.DBG["dbg_x3"][oc * 128:(oc + 1) * 128,
                                               q0:q0 + 1024],
                       stage[oc % 2][:, 0:1024], out_sem=K.sO)
        prev_out_dma = prev_dma
    K.pop()


def build_program(upto=7):
    nc = bass.Bass("TRN2", target_bir_lowering=False)
    K = KB(nc)

    e0 = E(K)
    ph_init(K, e0)
    e0.emit("init")

    if upto >= 1:
        e1 = E(K)
        ph_qkv(K, e1)
        e1.emit("qkv")

    if upto >= 2:
        K.push()
        al = {"gp": K.a16(3 * NH * 384), "wo": K.a16(8 * 1024), "load": None}
        for ap in range(4):
            ei = E(K)
            ph_attn(K, ei, ap, al)
            ei.emit(f"attn{ap}")
            if upto == 2 and ap == 0:
                break
        K.pop()

    if upto >= 3:
        e2 = E(K)
        ph_ffn(K, e2, K.x1, K.x2, final=False,
               dbg="dbg_x2" if DEBUG else None)
        e2.emit("ffn1")

    if upto >= 4:
        e3 = E(K)
        m1 = ph_mamba1(K, e3)
        e3.emit("mamba1")

    if upto >= 5:
        e4 = E(K)
        ph_mamba2(K, e4, m1)
        e4.emit("mamba2")

    if upto >= 6:
        e5 = E(K)
        ph_ffn(K, e5, K.x3, K.OUT, final=True, dbg=None)

        def fin(e):
            e.wait_ge(K.sO.h, K.sO.v)
        e5.tape["sync"].append(fin)
        e5.emit("ffn2")
    return nc, K


def kernel(**inputs):
    full = {k: np.asarray(v) for k, v in inputs.items()}
    nc, K = build_program()
    in_maps = [make_core_inputs(c, full) for c in range(8)]
    res = run_bass_kernel_spmd(nc, in_maps, core_ids=list(range(8)))
    out = np.empty((B, S, D), np.float32)
    for c in range(8):
        b, hf = c // 2, c % 2
        out[b, hf * T:(hf + 1) * T] = np.asarray(res.results[c]["out"]).T
    return out


if __name__ == "__main__":
    nc, K = build_program()
    try:
        print("built ok; instructions:", nc.n_instructions())
    except Exception:
        print("built ok")


# version 07
# speedup vs baseline: 1.0482x; 1.0482x over previous
import numpy as np
import ml_dtypes
import concourse.bass as bass
import concourse.mybir as mybir
from concourse.bass_utils import run_bass_kernel_spmd

F32 = mybir.dt.float32
BF16 = mybir.dt.bfloat16
I32 = mybir.dt.int32
AF = mybir.ActivationFunctionType
ALU = mybir.AluOpType
AX = mybir.AxisListType

B, S, D = 4, 8192, 1024
T = 4096
HALO = 128
XT = T + 2 * HALO
NH, HD = 16, 64
HID = 3072
DIN, DST, NCH, CH = 2048, 128, 16, 256
CONV_DIM = DIN + 2 * DST
NPROJ = 2 * DIN + 2 * DST + NH
XBCDT = CONV_DIM + NH
EPS = 1e-5
NEG = -1e30
Q127 = 1.0 / 127.0

DEBUG = False
ENGS = ("sync", "tensor", "vector", "scalar", "gpsimd")


class Sem:
    def __init__(self, nc, name):
        self.h = nc.alloc_semaphore(name)
        self.v = 0


def _pad_slice_T(x_b, lo, hi):
    n = hi - lo
    out = np.zeros((D, n), np.float32)
    a, b2 = max(lo, 0), min(hi, S)
    if b2 > a:
        out[:, a - lo:b2 - lo] = np.ascontiguousarray(x_b[a:b2]).T
    return out


def _build_gp(rel, hf):
    p = np.arange(128)[:, None]
    f = np.arange(384)[None, :]
    idx = f - p
    band = (idx >= 0) & (idx <= 256)
    g = np.where(band[None], np.take(rel, np.clip(idx, 0, 256), axis=1), NEG)
    g = np.ascontiguousarray(g.astype(np.float32))
    g_first = g.copy()
    if hf == 0:
        g_first[:, :, :128] = NEG
    g_last = g.copy()
    if hf == 1:
        g_last[:, :, 256:] = NEG
    return g, g_first, g_last


def make_core_inputs(c, full):
    b, hf = c // 2, c % 2
    x = full["L0_attn_a_x"][b]
    lo = hf * T - HALO
    d = {}
    d["xin"] = _pad_slice_T(x, lo, lo + XT)
    d["wq"] = np.ascontiguousarray(full["L0_attn_a_Wq"], np.float32)
    d["wk"] = np.ascontiguousarray(full["L0_attn_a_Wk"], np.float32)
    d["wv"] = np.ascontiguousarray(full["L0_attn_a_Wv"], np.float32)
    d["wo"] = np.ascontiguousarray(full["L0_attn_a_Wo"], np.float32)
    for nm in ("bq", "bk", "bv", "bo"):
        v = np.asarray(full[f"L0_attn_a_{nm}"], np.float32)
        d[nm + "c"] = np.ascontiguousarray(v.reshape(8, 128).T)
        d[nm + "r"] = np.ascontiguousarray(v.reshape(1, D))
    g, gf, gl = _build_gp(np.asarray(full["L0_attn_a_rel_pos_bias"],
                                     np.float32), hf)
    d["gp_int"], d["gp_first"], d["gp_last"] = g, gf, gl
    d["wgT"] = np.ascontiguousarray(full["L0_ffn_w_gate"].T.astype(np.int32))
    d["wuT"] = np.ascontiguousarray(full["L0_ffn_w_up"].T.astype(np.int32))
    d["wdT"] = np.ascontiguousarray(full["L0_ffn_w_down"].T.astype(np.int32))
    d["winT"] = np.ascontiguousarray(full["L0_attn_b_W_in"].T.astype(np.float32))
    d["woutT"] = np.ascontiguousarray(full["L0_attn_b_W_out"].T.astype(np.float32))
    d["convw"] = np.ascontiguousarray(
        np.asarray(full["L0_attn_b_conv_w"], np.float32).reshape(CONV_DIM, 4))
    d["convbc"] = np.ascontiguousarray(
        np.asarray(full["L0_attn_b_conv_b"], np.float32).reshape(18, 128).T)
    d["smalls"] = np.ascontiguousarray(np.stack(
        [np.asarray(full["L0_attn_b_dt_bias"], np.float32),
         np.asarray(full["L0_attn_b_A_log"], np.float32),
         np.asarray(full["L0_attn_b_D"], np.float32)], axis=1))
    d["drow"] = np.ascontiguousarray(
        np.asarray(full["L0_attn_b_D"], np.float32).reshape(1, NH))
    d["normw"] = np.ascontiguousarray(
        np.asarray(full["L0_attn_b_norm_weight"], np.float32).reshape(16, 128).T)
    oh = np.zeros((16, 2048), np.float32)
    e16 = np.zeros((16, 256), np.float32)
    for h in range(16):
        oh[h, h * 128:(h + 1) * 128] = 1.0
        e16[h, h::16] = 1.0
    d["oh"] = oh
    d["e16"] = e16
    rm = np.zeros((128, 8), np.float32)
    if hf == 1:
        rm[:, c - 1] = 1.0
    d["rdmask"] = rm
    return d


class KB:
    def __init__(self, nc):
        self.nc = nc
        DR = {}

        def din(name, shape, dt=F32):
            DR[name] = nc.dram_tensor(name, list(shape), dt,
                                      kind="ExternalInput").ap()
        din("xin", [D, XT])
        for w in ("wq", "wk", "wv", "wo"):
            din(w, [D, D])
        for bn in ("bq", "bk", "bv", "bo"):
            din(bn + "c", [128, 8])
            din(bn + "r", [1, D])
        din("gp_int", [NH, 128, 384])
        din("gp_first", [NH, 128, 384])
        din("gp_last", [NH, 128, 384])
        din("wgT", [D, HID], I32)
        din("wuT", [D, HID], I32)
        din("wdT", [HID, D], I32)
        din("winT", [D, NPROJ])
        din("woutT", [DIN, D])
        din("convw", [CONV_DIM, 4])
        din("convbc", [128, 18])
        din("smalls", [NH, 3])
        din("drow", [1, NH])
        din("normw", [128, 16])
        din("oh", [16, 2048])
        din("e16", [16, 256])
        din("rdmask", [128, 8])
        self.DR = DR

        self.OUT = nc.dram_tensor("out", [D, T], F32, kind="ExternalOutput").ap()
        self.DBG = {}
        if DEBUG:
            for nm in ("dbg_x1", "dbg_x2", "dbg_x3"):
                self.DBG[nm] = nc.dram_tensor(nm, [D, T], F32,
                                              kind="ExternalOutput").ap()
            self.DBG["dbg_ao"] = nc.dram_tensor(
                "dbg_ao", [128, 8 * 1024], BF16, kind="ExternalOutput").ap()

        self.x1 = nc.dram_tensor("x1s", [D, T], F32).ap()
        self.x2 = nc.dram_tensor("x2s", [D, T], F32).ap()
        self.x3 = nc.dram_tensor("x3s", [D, T], F32).ap()
        kw = dict(kind="ExternalOutput") if DEBUG else {}
        self.qsc = nc.dram_tensor("qscr", [D, T], BF16, **kw).ap()
        self.ksc = nc.dram_tensor("kscr", [D, XT], BF16, **kw).ap()
        self.vsc = nc.dram_tensor("vscr", [XT, D], BF16, **kw).ap()
        self.ysp = nc.dram_tensor("yspill", [128, 16 * T], F32, **kw).ap()
        self.cfms = nc.dram_tensor("cfms", [DST, T], BF16, **kw).ap()
        self.cc_in = nc.dram_tensor("cc_in", [128, DIN], F32).ap()
        self.cc_out = nc.dram_tensor("cc_out", [8 * 128, DIN], F32).ap()
        self.ch_in = nc.dram_tensor("ch_in", [128, 54], F32).ap()
        self.ch_out = nc.dram_tensor("ch_out", [8 * 128, 54], F32).ap()

        self.PB = nc.alloc_sbuf_tensor("PB", [128, 96 * 1024 + 512], BF16)
        self.off = 0
        self.marks = []

        A = nc.alloc_sbuf_tensor
        self.ident = A("ident", [128, 128], BF16)
        self.identf = A("identf", [128, 128], F32)
        self.onesb = A("onesb", [128, 128], BF16)
        self.onesf = A("onesf", [128, 128], F32)
        self.ones16f = A("ones16f", [16, 128], F32)
        self.ones1 = A("ones1", [1, 512], BF16)
        self.biasc = A("biasc", [128, 32], F32)
        self.biasr = A("biasr", [1, D], BF16)
        self.convw_sb = A("convw_sb", [128, 72], F32)
        self.convb_sb = A("convb_sb", [128, 18], F32)
        self.smalls_sb = A("smalls_sb", [16, 8], F32)
        self.drow_sb = A("drow_sb", [1, NH], BF16)
        self.dcols = A("dcols", [128, NH], F32)
        self.normw_sb = A("normw_sb", [128, 16], F32)
        self.rdmask_sb = A("rdmask_sb", [128, 8], F32)
        self.etm = A("etm", [128, 512], F32)
        self.zeros16 = A("zeros16", [16, 256], F32)
        self.acsT_g = A("acsT_g", [128, NCH * 32], F32)
        self.lasts = A("lasts", [16, 40], F32)
        self.pj_sb = A("pj_sb", [16, NCH], F32)
        self.pj128 = A("pj128", [128, NCH * 16], F32)
        self.cdecb = A("cdecb", [128, NCH * 16], F32)
        self.e16_sb = A("e16_sb", [16, 256], F32)
        self.halo3 = A("halo3", [128, 54], F32)
        self.cvals = A("cvals", [128, 8], F32)
        self.scr_nop = A("scrnop", [1, 8], F32)

        self.ps = [nc.alloc_psum_tensor(f"ps{i}", [128, 512], F32)
                   for i in range(6)]
        self.pbf = [nc.alloc_psum_tensor(f"pbf{i}", [128, 1024], BF16)
                    for i in range(2)]
        self.ptp = [t[:, :].bitcast(F32) for t in self.pbf]

        for s in ("sS", "sG", "sC", "sT", "sV", "sA", "sP", "sO", "sRT",
                  "sRP"):
            setattr(self, s, Sem(nc, s))

        self.blr = {}

        self.CONSTS = [EPS, 1.0, -1.0, Q127, 1.0 / DIN, 0.5, 2.0, 0.0]
        for i, v in enumerate(self.CONSTS):
            nc.const_aps.register(F32, v, self.cvals[:, i:i + 1])

    def a16(self, cols):
        cols = (cols + 15) // 16 * 16
        v = self.PB[:, self.off:self.off + cols]
        self.off += cols
        assert self.off <= 96 * 1024 + 512, f"SBUF pool overflow: {self.off}"
        return v

    def a32(self, cols):
        return self.a16(2 * cols).bitcast(F32)

    def push(self):
        self.marks.append(self.off)

    def pop(self):
        self.off = self.marks.pop()

    def rtrip(self, e, eng="vector"):
        s = self.sRT if eng == "vector" else self.sRP
        s.v += 1
        e.engine_nop().then_inc(s.h, 1)
        e.wait_ge(s.h, s.v)

    def rd(self, bank, tok):
        self.blr[bank] = tok

    def wr(self, *banks):
        return [self.blr.get(b) for b in banks]

    def mamba_top(self):
        base = self.off
        t = dict(acs=self.a32(T), oh=self.a32(2048), sin=self.a32(2048),
                 sinb=self.a16(2048))
        if getattr(self, "_mtop_base", None) is None:
            self._mtop_base = base
        else:
            assert self._mtop_base == base, (self._mtop_base, base)
        return t


PE_DELAY = 500


class E:

    def __init__(self, K):
        self.K = K
        self.tape = {e: [] for e in ENGS}

    def op(self, eng, fn, waits=(), inc=None, incn=1):
        K = self.K
        wl = [(t[0].h, t[1]) for t in waits if t is not None]
        pe_tok = any((t is not None and t[0] is K.sT) for t in waits)
        if inc is not None:
            inc.v += incn

        def run(e, fn=fn, wl=wl, inc=inc, incn=incn, eng=eng, pe_tok=pe_tok):
            for h, v in wl:
                e.wait_ge(h, v)
            if pe_tok and eng != "tensor":
                e.nop(cycle_cnt=PE_DELAY)
            r = fn(e)
            if inc is not None:
                if isinstance(r, list):
                    r = r[-1] if r else None
                if eng == "tensor":
                    e.drain()
                    r = e.nop()
                elif r is None:
                    if eng in ("vector", "gpsimd"):
                        r = e.engine_nop()
                    elif eng == "scalar":
                        r = e.copy(K.scr_nop[0:1, 0:1], K.scr_nop[0:1, 1:2])
                    else:
                        r = e.nop()
                r.then_inc(inc.h, incn)
        self.tape[eng].append(run)
        return (inc, inc.v) if inc is not None else None

    def dma(self, eng, dst, src, waits=(), out_sem=None):
        K = self.K
        sem = out_sem if out_sem is not None else (
            K.sG if eng == "gpsimd" else (K.sC if eng == "scalar" else K.sS))
        return self.op(eng, lambda e: e.dma_start(out=dst, in_=src),
                       waits=waits, inc=sem, incn=16)

    def emit(self, name):
        with self.K.nc.Block(name) as blk:
            for eng in ENGS:
                lst = self.tape[eng]
                if not lst:
                    continue

                def body(e, lst=lst):
                    for fn in lst:
                        fn(e)
                getattr(blk, eng)(body)


def emit_ln(K, E_, src, col0, ncols, u_dst, extra=()):
    import os
    lnp = int(os.environ.get("LN_PART", 9))
    nt = (ncols + 511) // 512
    prev_ln = getattr(K, "_ln_prev", [])
    K.push()
    xc = [K.a16(8 * 512), K.a16(8 * 512)]
    x2b = K.a16(8 * 512)
    mbc = K.a32(512)
    rbc = K.a32(512)
    tsc = K.a32(512)
    done = []
    lastmm = None
    for t in range(nt):
        w = min(512, ncols - t * 512)
        bb = xc[t % 2]
        reuse = [done[t - 2]] if t >= 2 else list(prev_ln) + list(extra)
        ld = None
        for dch in range(8):
            ld = E_.dma("gpsimd", bb[:, dch * 512:dch * 512 + w],
                        src[dch * 128:dch * 128 + 128,
                            col0 + t * 512:col0 + t * 512 + w],
                        waits=reuse if dch == 0 else ())

        if lnp < 2:
            done.append(ld)
            continue

        def sf(e, bb=bb, w=w):
            r = None
            for dch in range(8):
                r = e.activation(x2b[:, dch * 512:dch * 512 + w],
                                 bb[:, dch * 512:dch * 512 + w], AF.Square)
            return r
        sq = E_.op("scalar", sf, waits=[ld] + ([done[t - 1]] if t >= 1
                                               else list(prev_ln)), inc=K.sA)
        if lnp < 3:
            done.append(sq)
            continue

        def mf(e, bb=bb, w=w):
            for dch in range(8):
                e.matmul(K.ps[0][:, 0:w], K.onesb[:],
                         bb[:, dch * 512:dch * 512 + w],
                         start=(dch == 0), stop=(dch == 7))
            for dch in range(8):
                e.matmul(K.ps[1][:, 0:w], K.onesb[:],
                         x2b[:, dch * 512:dch * 512 + w],
                         start=(dch == 0), stop=(dch == 7))
        mm = E_.op("tensor", mf,
                   waits=[ld, sq] + K.wr(("ps", 0), ("ps", 1)), inc=K.sT)
        lastmm = mm
        if lnp < 4:
            done.append(mm)
            continue

        def v1(e, w=w):
            e.tensor_scalar(mbc[:, 0:w], K.ps[0][:, 0:w], 1.0 / 1024, None,
                            ALU.mult)
            e.tensor_scalar(tsc[:, 0:w], K.ps[1][:, 0:w], 1.0 / 1024, None,
                            ALU.mult)
        t1 = E_.op("vector", v1, waits=[mm], inc=K.sV)
        K.rd(("ps", 0), t1)
        K.rd(("ps", 1), t1)
        if lnp < 5:
            done.append(t1)
            continue

        t2 = E_.op("scalar", lambda e, w=w: e.activation(
            rbc[:, 0:w], mbc[:, 0:w], AF.Square), waits=[t1], inc=K.sA)
        t3 = E_.op("vector", lambda e, w=w: e.tensor_sub(
            tsc[:, 0:w], tsc[:, 0:w], rbc[:, 0:w]), waits=[t2], inc=K.sV)
        t4 = E_.op("scalar", lambda e, w=w: e.activation(
            tsc[:, 0:w], tsc[:, 0:w], AF.Sqrt, bias=EPS), waits=[t3], inc=K.sA)

        def v2(e, bb=bb, w=w, t=t):
            e.reciprocal(rbc[:, 0:w], tsc[:, 0:w])
            for dch in range(8):
                e.tensor_sub(x2b[:, dch * 512:dch * 512 + w],
                             bb[:, dch * 512:dch * 512 + w], mbc[:, 0:w])
                e.tensor_mul(u_dst[:, dch * ncols + t * 512:
                                   dch * ncols + t * 512 + w],
                             x2b[:, dch * 512:dch * 512 + w], rbc[:, 0:w])
        t5 = E_.op("vector", v2, waits=[t4] + list(extra), inc=K.sV)
        done.append(t5)
    K.pop()
    K._ln_prev = [done[-1], lastmm]
    return done[-1]


def ph_init(K, E_):
    DR = K.DR

    def g0(e):
        def d(dst, src):
            e.dma_start(out=dst, in_=src).then_inc(K.sG.h, 16)
        d(K.biasc[:, 0:8], DR["bqc"])
        d(K.biasc[:, 8:16], DR["bkc"])
        d(K.biasc[:, 16:24], DR["bvc"])
        d(K.biasc[:, 24:32], DR["boc"])
        d(K.biasr[:, 0:D], DR["bor"])
        for a in range(18):
            d(K.convw_sb[:, a * 4:(a + 1) * 4],
              DR["convw"][a * 128:(a + 1) * 128, :])
        d(K.convb_sb[:], DR["convbc"])
        d(K.smalls_sb[0:16, 0:3], DR["smalls"])
        d(K.drow_sb[:], DR["drow"])
        d(K.normw_sb[:], DR["normw"])
        d(K.rdmask_sb[:], DR["rdmask"])
        d(K.e16_sb[:], DR["e16"])
        for i, v in enumerate(K.CONSTS):
            e.memset(K.cvals[:, i:i + 1], v)
        e.memset(K.onesb[:], 1.0)
        e.memset(K.onesf[:], 1.0)
        e.memset(K.ones16f[:], 1.0)
        e.memset(K.ones1[:], 1.0)
        e.memset(K.zeros16[:], 0.0)
        e.memset(K.scr_nop[:], 0.0)
        e.memset(K.etm[:], 0.0)
        K.rtrip(e, "gpsimd")
        e.affine_select(K.ident[:], K.onesb[:], pattern=[[-1, 128]],
                        compare_op=ALU.is_equal, fill=0.0, base=0,
                        channel_multiplier=1)
        e.affine_select(K.identf[:], K.onesf[:], pattern=[[-1, 128]],
                        compare_op=ALU.is_equal, fill=0.0, base=0,
                        channel_multiplier=1)
        e.affine_select(K.etm[:, 0:256], K.etm[:, 0:256], pattern=[[1, 256]],
                        compare_op=ALU.is_ge, fill=NEG, base=0,
                        channel_multiplier=-1)
        e.affine_select(K.etm[:, 256:512], K.etm[:, 256:512],
                        pattern=[[1, 256]], compare_op=ALU.is_ge, fill=NEG,
                        base=-128, channel_multiplier=-1)
    K.sG.v += 16 * 29
    ld = E_.op("gpsimd", g0, inc=K.sP)
    ldt = (K.sG, K.sG.v)

    s0a = E_.op("scalar", lambda e: e.activation(
        K.smalls_sb[0:16, 3:4], K.smalls_sb[0:16, 1:2], AF.Exp),
        waits=[ldt, ld], inc=K.sA)
    E_.op("vector", lambda e: e.tensor_scalar(
        K.smalls_sb[0:16, 3:4], K.smalls_sb[0:16, 3:4], -1.0, None,
        ALU.mult), waits=[s0a], inc=K.sV)

    mm = E_.op("tensor", lambda e: e.matmul(
        K.ps[0][0:128, 0:NH], K.ones1[0:1, 0:128], K.drow_sb[:],
        start=True, stop=True), waits=[ldt, ld], inc=K.sT)
    ev = E_.op("vector", lambda e: e.tensor_copy(K.dcols[:],
                                                 K.ps[0][0:128, 0:NH]),
               waits=[mm], inc=K.sV)
    K.rd(("ps", 0), ev)


def ph_qkv(K, E_):
    import os
    part = int(os.environ.get("QKV_PART", 9))
    K.push()
    u = K.a16(8 * XT)
    wbuf = [K.a16(8 * 1024), K.a16(8 * 1024)]
    stage = [K.a16(4 * 512), K.a16(4 * 512)]
    bvr = K.a16(1024)[0:1, :]

    bv_ld = E_.dma("gpsimd", bvr, K.DR["bvr"])
    ln_done = emit_ln(K, E_, K.DR["xin"], 0, XT, u)

    wname = ("wq", "wk", "wv")
    wld = [None] * 3

    def load_w(wi, waits):
        t = None
        for dch in range(8):
            t = E_.dma("gpsimd", wbuf[wi % 2][:, dch * 1024:(dch + 1) * 1024],
                       K.DR[wname[wi]][dch * 128:(dch + 1) * 128, :],
                       waits=waits if dch == 0 else ())
        return t

    if part < 1:
        K.pop()
        return
    wld[0] = load_w(0, [])
    wld[1] = load_w(1, [])

    def proj(wi, outdram, c0, nst, length, biasoff, qmul):
        groups = [list(range(i, min(i + 4, nst))) for i in range(0, nst, 4)]
        last_dma = None
        for grp in groups:
            prev_ev = None
            for oc in range(8):
                def mf(e, grp=grp, oc=oc, wi=wi, c0=c0, length=length):
                    for dch in range(8):
                        for si, st in enumerate(grp):
                            w0 = c0 + st * 512
                            wdt = min(512, length - st * 512)
                            e.matmul(K.ps[si][0:128, 0:wdt],
                                     wbuf[wi % 2][:, dch * 1024 + oc * 128:
                                                  dch * 1024 + oc * 128 + 128],
                                     u[:, dch * XT + w0:dch * XT + w0 + wdt],
                                     start=(dch == 0), stop=(dch == 7))
                w = [ln_done, wld[wi], prev_ev]
                w += K.wr(*[("ps", si) for si in range(len(grp))])
                mm = E_.op("tensor", mf, waits=w, inc=K.sT)

                def vf(e, grp=grp, oc=oc, biasoff=biasoff, qmul=qmul,
                       length=length):
                    for si, st in enumerate(grp):
                        wdt = min(512, length - st * 512)
                        if qmul is None:
                            e.tensor_scalar(
                                stage[oc % 2][:, si * 512:si * 512 + wdt],
                                K.ps[si][0:128, 0:wdt],
                                K.biasc[:, biasoff + oc:biasoff + oc + 1],
                                None, ALU.add)
                        else:
                            e.tensor_scalar(
                                stage[oc % 2][:, si * 512:si * 512 + wdt],
                                K.ps[si][0:128, 0:wdt],
                                K.biasc[:, biasoff + oc:biasoff + oc + 1],
                                qmul, ALU.add, ALU.mult)
                ev = E_.op("vector", vf, waits=[mm, last_dma], inc=K.sV)
                for si in range(len(grp)):
                    K.rd(("ps", si), ev)
                prev_ev = ev
                for si, st in enumerate(grp):
                    w0 = c0 + st * 512
                    wdt = min(512, length - st * 512)
                    last_dma = E_.dma(
                        "sync",
                        outdram[oc * 128:(oc + 1) * 128, w0 - c0:w0 - c0 + wdt],
                        stage[oc % 2][:, si * 512:si * 512 + wdt],
                        waits=[ev] if si == 0 else ())
        return last_dma

    if part < 2:
        K.pop()
        return
    qd = proj(0, K.qsc, HALO, 8, T, 0, 0.125)
    if part < 3:
        K.pop()
        return
    kd = proj(1, K.ksc, 0, 9, XT, 8, None)
    if part < 4:
        K.pop()
        return

    wld[2] = load_w(2, [qd])
    prevv = None
    for tc in range(34):
        def mf(e, tc=tc):
            for dch in range(8):
                for half in range(2):
                    e.matmul(K.ps[(tc % 2) * 2 + half][0:128, 0:512],
                             u[:, dch * XT + tc * 128:dch * XT + tc * 128 + 128],
                             wbuf[0][:, dch * 1024 + half * 512:
                                     dch * 1024 + half * 512 + 512],
                             start=(dch == 0), stop=False)
            for half in range(2):
                e.matmul(K.ps[(tc % 2) * 2 + half][0:128, 0:512],
                         K.ones1[0:1, 0:128],
                         bvr[0:1, half * 512:half * 512 + 512],
                         start=False, stop=True)
        b0, b1 = (tc % 2) * 2, (tc % 2) * 2 + 1
        mm = E_.op("tensor", mf, waits=[wld[2], bv_ld, ln_done] +
                   K.wr(("ps", b0), ("ps", b1)), inc=K.sT)

        def vf(e, tc=tc):
            for half in range(2):
                e.tensor_copy(stage[tc % 2][:, half * 512:half * 512 + 512],
                              K.ps[(tc % 2) * 2 + half][0:128, 0:512])
        ev = E_.op("vector", vf, waits=[mm, prevv], inc=K.sV)
        K.rd(("ps", b0), ev)
        K.rd(("ps", b1), ev)
        prevv = E_.dma("sync", K.vsc[tc * 128:(tc + 1) * 128, :],
                       stage[tc % 2][:, 0:1024], waits=[ev])
    K.pop()


def ph_attn(K, E_, ap, al):
    qs = ap * 1024
    K.push()
    qb = K.a16(8 * 1024)
    kb = K.a16(8 * 1280)
    vb = K.a16(10 * 1024)
    ao = K.a16(8 * 1024)
    ptg = K.a16(2 * 3072)
    pn = K.a16(4 * 768)
    nml = K.a32(16)
    dg = K.a16(4 * 256)
    xr = K.a32(8 * 1024)
    stage = [K.a32(1024), K.a32(1024)]

    if ap == 0:
        t = None
        for vi, vn in enumerate(("gp_int", "gp_first", "gp_last")):
            for h in range(NH):
                t = E_.dma("gpsimd",
                           al["gp"][:, (vi * NH + h) * 384:
                                    (vi * NH + h) * 384 + 384],
                           K.DR[vn][h])
        def expgp(e):
            r = None
            for ch in range(9):
                r = e.activation(al["gp"][:, ch * 2048:(ch + 1) * 2048],
                                 al["gp"][:, ch * 2048:(ch + 1) * 2048],
                                 AF.Exp)
            return r
        t = E_.op("scalar", expgp, waits=[t], inc=K.sA)
        for dch in range(8):
            t = E_.dma("gpsimd", al["wo"][:, dch * 1024:(dch + 1) * 1024],
                       K.DR["wo"][dch * 128:(dch + 1) * 128, :],
                       waits=[t] if dch == 0 else ())
        al["load"] = t

    spl = None
    for dch in range(8):
        spl = E_.dma("scalar", qb[:, dch * 1024:(dch + 1) * 1024],
                     K.qsc[dch * 128:(dch + 1) * 128, qs:qs + 1024])
        spl = E_.dma("scalar", kb[:, dch * 1280:(dch + 1) * 1280],
                     K.ksc[dch * 128:(dch + 1) * 128, qs:qs + 1280])
    for tc in range(10):
        spl = E_.dma("scalar", vb[:, tc * 1024:(tc + 1) * 1024],
                     K.vsc[qs + tc * 128:qs + (tc + 1) * 128, :])
    xrl = None
    for dch in range(8):
        xrl = E_.dma("sync", xr[:, dch * 1024:(dch + 1) * 1024],
                     K.DR["xin"][dch * 128:(dch + 1) * 128,
                                 HALO + qs:HALO + qs + 1024])

    def gpsel(qtg, h):
        vi = 1 if qtg == 0 else (2 if qtg == 31 else 0)
        return al["gp"][:, (vi * NH + h) * 384:(vi * NH + h) * 384 + 384]

    PVRANGE = {0: (0, 128), 1: (0, 256), 2: (0, 384), 3: (128, 512),
               4: (256, 512), 5: (384, 512)}

    exp_toks = {}
    nm_toks = {}
    dg_toks = {}
    tr_ev = {}
    pv_ev = {}

    for c in range(8):
        for sp in range(2):
            for qtj in range(4):
                qt = sp * 4 + qtj
                qtg = ap * 8 + qt
                bA = (qt % 2) * 2
                idx = c * 8 + qt

                sl = idx % 4

                def smf(e, c=c, qt=qt, bA=bA):
                    e.matmul(K.ps[bA][0:128, 0:384],
                             qb[0:64, c * 1024 + qt * 128:
                                c * 1024 + qt * 128 + 128],
                             kb[0:64, c * 1280 + qt * 128:
                                c * 1280 + qt * 128 + 384],
                             start=True, stop=True, tile_position=(0, 0))
                    e.matmul(K.ps[bA + 1][0:128, 0:384],
                             qb[64:128, c * 1024 + qt * 128:
                                c * 1024 + qt * 128 + 128],
                             kb[64:128, c * 1280 + qt * 128:
                                c * 1280 + qt * 128 + 384],
                             start=True, stop=True, tile_position=(64, 0))
                w = [spl] + K.wr(("ps", bA), ("ps", bA + 1))
                mm = E_.op("tensor", smf, waits=w, inc=K.sT)

                def emf(e, bA=bA, sl=sl):
                    o = sl * 768
                    e.activation(pn[:, o:o + 384], K.ps[bA][0:128, 0:384],
                                 AF.Exp)
                    return e.activation(pn[:, o + 384:o + 768],
                                        K.ps[bA + 1][0:128, 0:384], AF.Exp)
                w = [mm]
                if idx >= 4:
                    w.append(tr_ev[(idx - 4, 5)])
                ex = E_.op("scalar", emf, waits=w, inc=K.sA)
                exp_toks[idx] = ex
                K.rd(("ps", bA), ex)
                K.rd(("ps", bA + 1), ex)

                def lmf(e, c=c, qtg=qtg, sl=sl):
                    o = sl * 768
                    e.scalar_tensor_tensor(
                        pn[:, o:o + 384], pn[:, o:o + 384], 1.0,
                        gpsel(qtg, 2 * c), ALU.mult, ALU.mult,
                        accum_out=nml[:, sl * 4:sl * 4 + 1])
                    return e.scalar_tensor_tensor(
                        pn[:, o + 384:o + 768], pn[:, o + 384:o + 768], 1.0,
                        gpsel(qtg, 2 * c + 1), ALU.mult, ALU.mult,
                        accum_out=nml[:, sl * 4 + 1:sl * 4 + 2])
                w = [ex, al["load"]]
                if idx >= 4:
                    w.append(dg_toks[idx - 4])
                lm = E_.op("vector", lmf, waits=w, inc=K.sV)

                def nrm(e, sl=sl):
                    K.rtrip(e)
                    return e.reciprocal(nml[:, sl * 4 + 2:sl * 4 + 4],
                                        nml[:, sl * 4:sl * 4 + 2])
                nm = E_.op("vector", nrm, waits=[lm], inc=K.sV)
                nm_toks[idx] = nm

                def dgf(e, sl=sl):
                    p = sl * 256
                    e.tensor_scalar(dg[:, p:p + 128], K.ident[:],
                                    nml[:, sl * 4 + 2:sl * 4 + 3],
                                    None, ALU.mult)
                    return e.tensor_scalar(dg[:, p + 128:p + 256], K.ident[:],
                                           nml[:, sl * 4 + 3:sl * 4 + 4],
                                           None, ALU.mult)
                w = [nm]
                if idx >= 4:
                    w.append(tr_ev[(idx - 4, 5)])
                dgt = E_.op("gpsimd", dgf, waits=w, inc=K.sG)
                dg_toks[idx] = dgt

                def tf6(e, sl=sl):
                    for h2 in range(2):
                        for kk in range(3):
                            o = sl * 768
                            p = sl * 256
                            e.matmul(K.ptp[h2][0:128, kk * 128:kk * 128 + 128],
                                     pn[:, o + h2 * 384 + kk * 128:
                                        o + h2 * 384 + kk * 128 + 128],
                                     dg[:, p + h2 * 128:p + h2 * 128 + 128],
                                     start=True, stop=True)
                w = [dgt] + K.wr(("pbf", 0), ("pbf", 1))
                tt = E_.op("tensor", tf6, waits=w, inc=K.sT)

                def ef6(e, qtj=qtj):
                    r = None
                    for h2 in range(2):
                        for kk in range(3):
                            rr = qtj + kk
                            r = e.tensor_copy(
                                ptg[:, h2 * 3072 + rr * 512 + qtj * 128:
                                    h2 * 3072 + rr * 512 + qtj * 128 + 128],
                                K.ptp[h2][0:128, kk * 128:kk * 128 + 128])
                    return r
                te = E_.op("vector", ef6, waits=[tt], inc=K.sV)
                K.rd(("pbf", 0), te)
                K.rd(("pbf", 1), te)
                tr_ev[(idx, 5)] = te

            spi = c * 2 + sp
            bank = 4 + (spi % 2)

            def pvf(e, c=c, sp=sp, bank=bank):
                for r in range(6):
                    c0, c1 = PVRANGE[r]
                    tc = sp * 4 + r
                    e.matmul(K.ps[bank][64:128, c0:c1],
                             vb[:, tc * 1024 + (2 * c + 1) * 64:
                                tc * 1024 + (2 * c + 1) * 64 + 64],
                             ptg[:, 3072 + r * 512 + c0:3072 + r * 512 + c1],
                             start=(r == 0), stop=(r == 5),
                             tile_position=(0, 64))
                    e.matmul(K.ps[bank][0:64, c0:c1],
                             vb[:, tc * 1024 + (2 * c) * 64:
                                tc * 1024 + (2 * c) * 64 + 64],
                             ptg[:, r * 512 + c0:r * 512 + c1],
                             start=(r == 0), stop=(r == 5),
                             tile_position=(0, 0))
            w = [tr_ev[(c * 8 + sp * 4 + j, 5)] for j in range(4)]
            w += K.wr(("ps", bank))
            pv = E_.op("tensor", pvf, waits=w, inc=K.sT)

            def pve(e, c=c, sp=sp, bank=bank):
                e.tensor_copy(ao[:, c * 1024 + sp * 512:
                                 c * 1024 + sp * 512 + 512],
                              K.ps[bank][0:128, 0:512])
            pev = E_.op("vector", pve, waits=[pv], inc=K.sV)
            K.rd(("ps", bank), pev)
            pv_ev[spi] = pev

    ao_done = pv_ev[15]
    if DEBUG and ap == 0:
        ao_done = E_.dma("sync", K.DBG["dbg_ao"], ao[:, 0:8 * 1024],
                         waits=[ao_done])

    prev_dma = None
    for oc in range(8):
        def mf(e, oc=oc):
            for dch in range(8):
                for st in range(2):
                    e.matmul(K.ps[st][0:128, 0:512],
                             al["wo"][:, dch * 1024 + oc * 128:
                                      dch * 1024 + oc * 128 + 128],
                             ao[:, dch * 1024 + st * 512:
                                dch * 1024 + st * 512 + 512],
                             start=(dch == 0), stop=False)
            for st in range(2):
                e.matmul(K.ps[st][0:128, 0:512],
                         K.biasr[0:1, oc * 128:oc * 128 + 128],
                         K.ones1[0:1, 0:512], start=False, stop=True)
        w = [ao_done, xrl, al["load"]] + K.wr(("ps", 0), ("ps", 1))
        mm = E_.op("tensor", mf, waits=w, inc=K.sT)

        def vf(e, oc=oc):
            for st in range(2):
                e.tensor_add(stage[oc % 2][:, st * 512:st * 512 + 512],
                             K.ps[st][0:128, 0:512],
                             xr[:, oc * 1024 + st * 512:
                                oc * 1024 + st * 512 + 512])
        ev = E_.op("vector", vf, waits=[mm, prev_dma], inc=K.sV)
        K.rd(("ps", 0), ev)
        K.rd(("ps", 1), ev)
        prev_dma = E_.dma("sync", K.x1[oc * 128:(oc + 1) * 128, qs:qs + 1024],
                          stage[oc % 2][:, 0:1024], waits=[ev])
        if DEBUG:
            E_.dma("sync", K.DBG["dbg_x1"][oc * 128:(oc + 1) * 128,
                                           qs:qs + 1024],
                   stage[oc % 2][:, 0:1024], out_sem=K.sO)
    K.pop()


def ph_ffn(K, E_, src, dst, final, dbg=None):
    K.push()
    wd = K.a16(24 * 1024)
    hb = K.a16(24 * 1024)
    u4 = K.a16(8 * 1024)
    wg = K.a16(8 * 768)
    wu = K.a16(8 * 768)
    sg = K.a16(1024)
    xst = [K.a32(1024), K.a32(1024)]
    stage = [K.a32(1024), K.a32(1024)]

    wdl = None
    for kc in range(24):
        wdl = E_.dma("gpsimd", wd[:, kc * 1024:(kc + 1) * 1024],
                     K.DR["wdT"][kc * 128:(kc + 1) * 128, :])

    prev_down = None
    last_gm = None
    last_um = None
    for tq in range(4):
        q0 = tq * 1024
        lnd = emit_ln(K, E_, src, q0, 1024, u4, extra=[last_gm, last_um])
        hdone = []
        for hs in range(4):
            wgl = None
            for dch in range(8):
                wgl = E_.dma("gpsimd", wg[:, dch * 768:(dch + 1) * 768],
                             K.DR["wgT"][dch * 128:(dch + 1) * 128,
                                         hs * 768:(hs + 1) * 768],
                             waits=[last_gm] if dch == 0 else ())
            wul = None
            for dch in range(8):
                wul = E_.dma("gpsimd", wu[:, dch * 768:(dch + 1) * 768],
                             K.DR["wuT"][dch * 128:(dch + 1) * 128,
                                         hs * 768:(hs + 1) * 768],
                             waits=[last_um] if dch == 0 else ())
            for oc in range(6):
                def gmf(e, oc=oc):
                    for dch in range(8):
                        for st in range(2):
                            e.matmul(K.ps[st][0:128, 0:512],
                                     wg[:, dch * 768 + oc * 128:
                                        dch * 768 + oc * 128 + 128],
                                     u4[:, dch * 1024 + st * 512:
                                        dch * 1024 + st * 512 + 512],
                                     start=(dch == 0), stop=(dch == 7))
                gm = E_.op("tensor", gmf,
                           waits=[lnd, wgl] + K.wr(("ps", 0), ("ps", 1)),
                           inc=K.sT)
                last_gm = gm

                def umf(e, oc=oc):
                    for dch in range(8):
                        for st in range(2):
                            e.matmul(K.ps[2 + st][0:128, 0:512],
                                     wu[:, dch * 768 + oc * 128:
                                        dch * 768 + oc * 128 + 128],
                                     u4[:, dch * 1024 + st * 512:
                                        dch * 1024 + st * 512 + 512],
                                     start=(dch == 0), stop=(dch == 7))
                um = E_.op("tensor", umf,
                           waits=[lnd, wul] + K.wr(("ps", 2), ("ps", 3)),
                           inc=K.sT)
                last_um = um

                silu = E_.op("scalar", lambda e: [
                    e.activation(sg[:, 0:512], K.ps[0][0:128, 0:512],
                                 AF.Silu, scale=Q127),
                    e.activation(sg[:, 512:1024], K.ps[1][0:128, 0:512],
                                 AF.Silu, scale=Q127)],
                    waits=[gm, hdone[-1] if hdone else None], inc=K.sA)
                K.rd(("ps", 0), silu)
                K.rd(("ps", 1), silu)

                def hf(e, oc=oc, hs=hs):
                    blk = hs * 6 + oc
                    for st in range(2):
                        e.scalar_tensor_tensor(
                            hb[:, blk * 1024 + st * 512:
                               blk * 1024 + st * 512 + 512],
                            K.ps[2 + st][0:128, 0:512], Q127,
                            sg[:, st * 512:st * 512 + 512],
                            ALU.mult, ALU.mult)
                ht = E_.op("vector", hf, waits=[um, silu], inc=K.sV)
                K.rd(("ps", 2), ht)
                K.rd(("ps", 3), ht)
                hdone.append(ht)

        prev_dma = None
        ev_of = {}
        for oc in range(8):
            xl = E_.dma("sync", xst[oc % 2][:, 0:1024],
                        src[oc * 128:(oc + 1) * 128, q0:q0 + 1024],
                        waits=[ev_of.get(oc - 2, prev_down)])

            def dmf(e, oc=oc):
                for kc in range(24):
                    for st in range(2):
                        e.matmul(K.ps[4 + st][0:128, 0:512],
                                 wd[:, kc * 1024 + oc * 128:
                                    kc * 1024 + oc * 128 + 128],
                                 hb[:, kc * 1024 + st * 512:
                                    kc * 1024 + st * 512 + 512],
                                 start=(kc == 0), stop=(kc == 23))
            dm = E_.op("tensor", dmf, waits=[hdone[-1], wdl] +
                       K.wr(("ps", 4), ("ps", 5)), inc=K.sT)

            def ef(e, oc=oc):
                for st in range(2):
                    e.scalar_tensor_tensor(
                        stage[oc % 2][:, st * 512:st * 512 + 512],
                        K.ps[4 + st][0:128, 0:512], Q127,
                        xst[oc % 2][:, st * 512:st * 512 + 512],
                        ALU.mult, ALU.add)
            ev = E_.op("vector", ef, waits=[dm, xl, prev_dma], inc=K.sV)
            ev_of[oc] = ev
            K.rd(("ps", 4), ev)
            K.rd(("ps", 5), ev)
            osem = K.sO if final else None
            prev_dma = E_.dma("sync", dst[oc * 128:(oc + 1) * 128, q0:q0 + 1024],
                              stage[oc % 2][:, 0:1024], waits=[ev],
                              out_sem=osem)
            if dbg is not None:
                E_.dma("sync", K.DBG[dbg][oc * 128:(oc + 1) * 128,
                                          q0:q0 + 1024],
                       stage[oc % 2][:, 0:1024], out_sem=K.sO)
        prev_down = prev_dma
    K.pop()
    return prev_down


def ph_mamba1(K, E_):
    K.push()
    top = K.mamba_top()
    acs = top["acs"]
    oh = top["oh"]
    gA = K.a16(18 * 1028)
    GBW = 1028
    dtg = K.a32(1024)
    u4 = K.a16(8 * 1024)
    winb = [K.a16(8 * 128), K.a16(8 * 128)]
    xc2 = K.a16(18 * 512)
    xdtT = K.a16(16 * 256)
    xdd = K.a16(2 * 128)
    btok = K.a16(2 * 128)
    cbt = K.a16(2 * 256)
    mt = K.a16(2 * 256)
    ebc = K.a16(256)
    cexp = K.a16(256)
    carryb = K.a16(128)
    dxs = K.a16(256)
    tail3 = K.a16(18 * 16)
    ysl = K.a32(8 * 256)
    carry = K.a32(DIN)
    dtf = K.a32(256)
    dtm = K.a32(256)
    eb2 = K.a32(512)
    dg4 = K.a16(2 * 512)
    decT = K.a32(32)
    dtTb = K.a32(32)
    haccum = K.a32(8 * 54)
    u3 = K.a16(8 * 16)
    hs54 = K.a32(54)

    sm = K.smalls_sb

    ohld = E_.dma("gpsimd", oh[0:16, :], K.DR["oh"])

    x3 = u3
    x3ld = None
    for dch in range(8):
        x3ld = E_.dma("gpsimd", x3[:, dch * 16:dch * 16 + 3],
                      K.x2[dch * 128:(dch + 1) * 128, T - 3:T])

    def mstat(e):
        for dch in range(8):
            e.matmul(K.ps[0][:, 0:3], K.onesb[:], x3[:, dch * 16:dch * 16 + 3],
                     start=(dch == 0), stop=(dch == 7))
    ms = E_.op("tensor", mstat, waits=[x3ld] + K.wr(("ps", 0)), inc=K.sT)

    def msq(e):
        r = None
        for dch in range(8):
            r = e.activation(u3[:, dch * 16 + 8:dch * 16 + 11],
                             x3[:, dch * 16:dch * 16 + 3], AF.Square)
        return r
    ms2 = E_.op("scalar", msq, waits=[x3ld], inc=K.sA)

    def mstat2(e):
        for dch in range(8):
            e.matmul(K.ps[1][:, 0:3], K.onesb[:],
                     u3[:, dch * 16 + 8:dch * 16 + 11],
                     start=(dch == 0), stop=(dch == 7))
    ms3 = E_.op("tensor", mstat2, waits=[ms2] + K.wr(("ps", 1)), inc=K.sT)

    def mv1(e):
        e.tensor_scalar(decT[:, 0:3], K.ps[0][:, 0:3], 1.0 / 1024, None,
                        ALU.mult)
        e.tensor_scalar(dtTb[:, 0:3], K.ps[1][:, 0:3], 1.0 / 1024, None,
                        ALU.mult)
    mv = E_.op("vector", mv1, waits=[ms, ms3], inc=K.sV)
    K.rd(("ps", 0), mv)
    K.rd(("ps", 1), mv)
    mv2 = E_.op("scalar", lambda e: e.activation(
        dtf[:, 0:3], decT[:, 0:3], AF.Square), waits=[mv], inc=K.sA)
    mv3 = E_.op("vector", lambda e: e.tensor_sub(
        dtTb[:, 0:3], dtTb[:, 0:3], dtf[:, 0:3]), waits=[mv2], inc=K.sV)
    mv4 = E_.op("scalar", lambda e: e.activation(
        dtTb[:, 0:3], dtTb[:, 0:3], AF.Sqrt, bias=EPS), waits=[mv3],
        inc=K.sA)

    def mv5a(e):
        for dch in range(8):
            e.tensor_sub(eb2[:, dch * 16:dch * 16 + 3],
                         x3[:, dch * 16:dch * 16 + 3], decT[:, 0:3])
    sub_d = E_.op("gpsimd", mv5a, waits=[mv, x3ld], inc=K.sG)

    def mv5(e):
        e.reciprocal(dtTb[:, 0:3], dtTb[:, 0:3])
        K.rtrip(e)
        for dch in range(8):
            e.tensor_mul(u3[:, dch * 16 + 4:dch * 16 + 7],
                         eb2[:, dch * 16:dch * 16 + 3], dtTb[:, 0:3])
    u3d = E_.op("vector", mv5, waits=[mv4, sub_d], inc=K.sV)

    hprev = None
    for oc in range(18):
        wl = None
        for dch in range(8):
            wl = E_.dma("gpsimd",
                        winb[oc % 2][:, dch * 128:dch * 128 + 128],
                        K.DR["winT"][dch * 128:(dch + 1) * 128,
                                     DIN + oc * 128:DIN + oc * 128 + 128],
                        waits=[hprev] if dch == 0 else ())

        def hmf(e, oc=oc):
            for dch in range(8):
                e.matmul(K.ps[2][0:128, 0:3],
                         winb[oc % 2][:, dch * 128:dch * 128 + 128],
                         u3[:, dch * 16 + 4:dch * 16 + 7],
                         start=(dch == 0), stop=(dch == 7))
        hm = E_.op("tensor", hmf, waits=[wl, u3d] + K.wr(("ps", 2)),
                   inc=K.sT)
        hprev = E_.op("vector", lambda e, oc=oc: e.tensor_copy(
            hs54[:, oc * 3:oc * 3 + 3], K.ps[2][0:128, 0:3]),
            waits=[hm], inc=K.sV)
        K.rd(("ps", 2), hprev)
    hdma = E_.dma("sync", K.ch_in[:, :], hs54[:, 0:54], waits=[hprev])

    def ccf(e):
        return e.collective_compute(
            "AllGather", ALU.bypass, replica_groups=[list(range(8))],
            ins=[K.ch_in[:, :].opt()], outs=[K.ch_out[:, :].opt()])
    cc1 = E_.op("gpsimd", ccf, waits=[hdma], inc=K.sP)
    hl = None
    for j in range(8):
        hl = E_.dma("sync", haccum[:, j * 54:(j + 1) * 54],
                    K.ch_out[j * 128:(j + 1) * 128, :], waits=[cc1])

    def hmask(e):
        e.tensor_scalar(K.halo3[:], haccum[:, 0:54], K.rdmask_sb[:, 0:1],
                        None, ALU.mult)
        for j in range(1, 8):
            K.rtrip(e)
            e.scalar_tensor_tensor(K.halo3[:], haccum[:, j * 54:(j + 1) * 54],
                                   K.rdmask_sb[:, j:j + 1], K.halo3[:],
                                   ALU.mult, ALU.add)
    halo_done = E_.op("vector", hmask, waits=[hl], inc=K.sV)

    cz = E_.op("vector", lambda e: e.memset(carry[:, :], 0.0), inc=K.sV)

    mm_of_oc = {}
    group_done = {}
    chunk_state = {}

    def emit_inproj(g, gb_free, winb_free=None):
        lnd = emit_ln(K, E_, K.x2, g * 1024, 1024, u4,
                      extra=[mm_of_oc.get(("last", 0))])
        prev_ev = None
        for oc in range(19):
            ow = 128 if oc < 18 else 16
            wl = None
            for dch in range(8):
                wl = E_.dma("gpsimd",
                            winb[oc % 2][:, dch * 128:dch * 128 + ow],
                            K.DR["winT"][dch * 128:(dch + 1) * 128,
                                         DIN + oc * 128:DIN + oc * 128 + ow],
                            waits=[mm_of_oc.get((g, oc - 2),
                                                mm_of_oc.get(("last", 0))),
                                   winb_free if oc < 2 else None]
                            if dch == 0 else ())

            def mf(e, oc=oc, ow=ow):
                op_ = min(ow, 128)
                for dch in range(8):
                    for st in range(2):
                        e.matmul(K.ps[st][0:op_, 0:512],
                                 winb[oc % 2][:, dch * 128:dch * 128 + ow],
                                 u4[:, dch * 1024 + st * 512:
                                    dch * 1024 + st * 512 + 512],
                                 start=(dch == 0), stop=(dch == 7))
            w = [lnd, wl, prev_ev] + K.wr(("ps", 0), ("ps", 1))
            mm = E_.op("tensor", mf, waits=w, inc=K.sT)
            mm_of_oc[(g, oc)] = mm
            mm_of_oc[("last", 0)] = mm

            def vf(e, oc=oc, ow=ow):
                for st in range(2):
                    if ow == 128:
                        e.tensor_copy(gA[:, oc * GBW + 4 + st * 512:
                                         oc * GBW + 4 + st * 512 + 512],
                                      K.ps[st][0:128, 0:512])
                    else:
                        e.tensor_copy(dtg[0:16, st * 512:st * 512 + 512],
                                      K.ps[st][0:16, 0:512])
            prev_ev = E_.op("vector", vf, waits=[mm, gb_free], inc=K.sV)
            K.rd(("ps", 0), prev_ev)
            K.rd(("ps", 1), prev_ev)
        return prev_ev

    W_ETM = K.etm

    def emit_chunk(j):
        g = j // 4
        jr = j % 4
        gdone = group_done[g]

        po = jr % 2
        if po == 0:
            if jr == 0:
                def ctxf(e, j=j):
                    e.tensor_copy(
                        gA[:, 0:18 * GBW].rearrange("p (a k) -> p a k", a=18)
                        [:, :, 1:4],
                        (K.halo3 if j == 0 else tail3)
                        [:, 0:18 * (3 if j == 0 else 16)]
                        .rearrange("p (a k) -> p a k", a=18)[:, :, 0:3])
                ctx = E_.op("vector", ctxf, waits=[gdone,
                                                   halo_done if j == 0
                                                   else None], inc=K.sV)
            else:
                ctx = None
            conv_sil = []
            dg_mm = {}
            for cch in range(18):
                slot = cch % 2

                def dgf(e, cch=cch, slot=slot):
                    for k in range(4):
                        e.tensor_scalar(
                            dg4[:, slot * 512 + k * 128:
                                slot * 512 + k * 128 + 128],
                            K.ident[:],
                            K.convw_sb[:, cch * 4 + k:cch * 4 + k + 1],
                            None, ALU.mult)
                dg = E_.op("vector", dgf,
                           waits=[ctx, dg_mm.get(cch - 2),
                                  chunk_state.get((j - 1, "ye15"))],
                           inc=K.sV)

                bank = 4 + (cch % 2)

                def cmf(e, cch=cch, slot=slot, jr=jr, bank=bank):
                    bb = cch * GBW
                    for k in range(4):
                        e.matmul(K.ps[bank][0:128, 0:512],
                                 dg4[:, slot * 512 + k * 128:
                                     slot * 512 + k * 128 + 128],
                                 gA[:, bb + 1 + jr * 256 + k:
                                    bb + 1 + jr * 256 + k + 512],
                                 start=(k == 0), stop=(k == 3))
                cm = E_.op("tensor", cmf,
                           waits=[dg, gdone] + K.wr(("ps", bank)), inc=K.sT)
                dg_mm[cch] = cm

                si = E_.op("scalar", lambda e, cch=cch, bank=bank:
                           e.activation(xc2[:, cch * 512:(cch + 1) * 512],
                                        K.ps[bank][0:128, 0:512], AF.Silu,
                                        bias=K.convb_sb[:, cch:cch + 1]),
                           waits=[cm, chunk_state.get((j - 1, "ye15"))],
                           inc=K.sA)
                K.rd(("ps", bank), si)
                conv_sil.append(si)
            chunk_state[(j, "convdone")] = conv_sil[-1]
        conv_done = chunk_state[(j - po, "convdone")]
        xc = xc2[:, po * 256:]

        def t3f(e, jr=jr):
            e.tensor_copy(
                tail3[:, 0:18 * 16].rearrange("p (a k) -> p a k", a=18)
                [:, :, 0:3],
                gA[:, 0:18 * GBW].rearrange("p (a k) -> p a k", a=18)
                [:, :, 4 + jr * 256 + 253:4 + jr * 256 + 256])
        t3 = E_.op("vector", t3f, waits=[gdone, conv_done], inc=K.sV)

        def dtc(e, jr=jr):
            e.tensor_scalar(dtf[0:16, :], dtg[0:16, jr * 256:(jr + 1) * 256],
                            sm[0:16, 0:1], None, ALU.add)
        d1 = E_.op("vector", dtc,
                   waits=[gdone, chunk_state.get((j - 1, "l2")),
                          chunk_state.get((j - 1, "dtT_done"))], inc=K.sV)
        d2 = E_.op("scalar", lambda e: [
            e.activation(dtm[0:16, 0:256], dtf[0:16, 0:256], AF.Abs),
            e.activation(dtm[0:16, 0:256], dtm[0:16, 0:256], AF.Exp,
                         scale=-1.0)],
            waits=[d1], inc=K.sA)
        d3 = E_.op("scalar", lambda e: e.activation(
            dtm[0:16, 0:256], dtm[0:16, 0:256], AF.Ln, bias=1.0),
            waits=[d2], inc=K.sA)

        def dtc2(e):
            e.tensor_relu(dtf[0:16, :], dtf[0:16, :])
            e.tensor_add(dtf[0:16, :], dtf[0:16, :], dtm[0:16, :])
            e.tensor_scalar(dtm[0:16, :], dtf[0:16, :], sm[0:16, 3:4],
                            None, ALU.mult)
        d4 = E_.op("vector", dtc2, waits=[d3], inc=K.sV)

        def scanf(e, j=j):
            e.tensor_tensor_scan(acs[0:16, j * 256:(j + 1) * 256],
                                 dtm[0:16, :], K.zeros16[0:16, :], 0.0,
                                 ALU.add, ALU.add)
        d5 = E_.op("vector", scanf, waits=[d4], inc=K.sV)
        d5b = E_.op("gpsimd", lambda e, j=j: e.tensor_copy(
            K.lasts[0:16, j:j + 1],
            acs[0:16, j * 256 + 255:j * 256 + 256]), waits=[d5], inc=K.sG)

        acsT_toks = []
        for sh in range(2):
            tt = E_.op("tensor", lambda e, j=j, sh=sh: e.transpose(
                K.ps[2][0:128, 0:16],
                acs[0:16, j * 256 + sh * 128:j * 256 + sh * 128 + 128],
                K.identf[0:16, 0:16]),
                waits=[d5] + K.wr(("ps", 2)), inc=K.sT)
            ev = E_.op("vector", lambda e, j=j, sh=sh: e.tensor_copy(
                K.acsT_g[:, j * 32 + sh * 16:j * 32 + sh * 16 + 16],
                K.ps[2][0:128, 0:16]), waits=[tt], inc=K.sV)
            K.rd(("ps", 2), ev)
            acsT_toks.append(ev)
        acsT_done = acsT_toks[-1]
        dtT_toks = []
        for sh in range(2):
            tt = E_.op("tensor", lambda e, sh=sh: e.transpose(
                K.ps[2][0:128, 0:16], dtf[0:16, sh * 128:sh * 128 + 128],
                K.identf[0:16, 0:16]),
                waits=[d4] + K.wr(("ps", 2)), inc=K.sT)
            ev = E_.op("vector", lambda e, sh=sh: e.tensor_copy(
                dtTb[:, sh * 16:sh * 16 + 16], K.ps[2][0:128, 0:16]),
                waits=[tt, chunk_state.get((j - 1, "xdt31"))], inc=K.sV)
            K.rd(("ps", 2), ev)
            dtT_toks.append(ev)
        dtT_done = dtT_toks[-1]

        l1 = E_.op("vector", lambda e, j=j: e.tensor_scalar(
            dtm[0:16, 0:32], K.e16_sb[0:16, 0:32], K.lasts[0:16, j:j + 1],
            None, ALU.mult), waits=[d5b], inc=K.sV)
        l2 = E_.op("tensor", lambda e: e.matmul(
            K.ps[3][0:128, 0:32], K.ones16f[0:16, :], dtm[0:16, 0:32],
            start=True, stop=True), waits=[l1] + K.wr(("ps", 3)), inc=K.sT)
        l3 = E_.op("vector", lambda e, j=j: e.tensor_sub(
            decT[:, 0:32], K.ps[3][0:128, 0:32],
            K.acsT_g[:, j * 32:j * 32 + 32]),
            waits=[l2, acsT_done, chunk_state.get((j - 1, "xdd15"))],
            inc=K.sV)

        def l4f(e, j=j):
            e.activation(decT[:, 0:32], decT[:, 0:32], AF.Exp)
            return e.activation(K.cdecb[:, j * 16:(j + 1) * 16],
                                K.ps[3][0:128, 0:16], AF.Exp)
        l4 = E_.op("scalar", l4f, waits=[l3], inc=K.sA)
        K.rd(("ps", 3), l4)

        xt_ev = {}
        for h in range(16):
            for sh in range(2):
                i = h * 2 + sh
                bank = i % 2

                def tf(e, h=h, sh=sh, bank=bank):
                    e.transpose(K.pbf[bank][0:128, 0:128],
                                xc[:, h * 512 + sh * 128:
                                   h * 512 + sh * 128 + 128],
                                K.ident[:])
                tt = E_.op("tensor", tf,
                           waits=[conv_done] + K.wr(("pbf", bank)), inc=K.sT)

                def ef(e, h=h, sh=sh, bank=bank):
                    e.tensor_scalar(
                        xdtT[:, h * 256 + sh * 128:h * 256 + sh * 128 + 128],
                        K.pbf[bank][0:128, 0:128],
                        dtTb[:, sh * 16 + h:sh * 16 + h + 1], None, ALU.mult)
                ev = E_.op("vector", ef, waits=[tt, dtT_done], inc=K.sV)
                K.rd(("pbf", bank), ev)
                xt_ev[i] = ev
        xdt_done = xt_ev[31]

        bt_toks = []
        for sh in range(2):
            bank = sh % 2
            tt = E_.op("tensor", lambda e, sh=sh, bank=bank: e.transpose(
                K.pbf[bank][0:128, 128:256],
                xc[:, 16 * 512 + sh * 128:16 * 512 + sh * 128 + 128],
                K.ident[:]),
                waits=[conv_done] + K.wr(("pbf", bank)), inc=K.sT)
            ev = E_.op("vector", lambda e, sh=sh, bank=bank: e.tensor_copy(
                btok[:, sh * 128:sh * 128 + 128],
                K.pbf[bank][0:128, 128:256]),
                waits=[tt, chunk_state.get((j - 1, "stm15"))], inc=K.sV)
            K.rd(("pbf", bank), ev)
            bt_toks.append(ev)
        btok_done = bt_toks[-1]

        E_.dma("sync", K.cfms[:, j * 256:(j + 1) * 256],
               xc[:, 17 * 512:17 * 512 + 256], waits=[conv_done])

        cb_toks = []
        for sh in range(2):
            mmc = E_.op("tensor", lambda e, sh=sh: e.matmul(
                K.ps[sh][0:128, 0:256],
                xc[:, 16 * 512 + sh * 128:16 * 512 + sh * 128 + 128],
                xc[:, 17 * 512:17 * 512 + 256], start=True, stop=True),
                waits=[conv_done] + K.wr(("ps", sh)), inc=K.sT)
            ev = E_.op("vector", lambda e, sh=sh: e.tensor_copy(
                cbt[:, sh * 256:(sh + 1) * 256], K.ps[sh][0:128, 0:256]),
                waits=[mmc, chunk_state.get((j - 1, "mt15"))], inc=K.sV)
            K.rd(("ps", sh), ev)
            cb_toks.append(ev)
        cbt_done = cb_toks[-1]

        prev = {}
        for kk2 in ("ym", "cu", "ye", "stm", "mt", "cexp"):
            prev[(kk2, -1)] = chunk_state.get((j - 1, kk2 + "15"))
        for h in range(16):
            ab = 2 + (h % 2)
            yb = 4 + (h % 2)
            stb = h % 2

            bca = E_.op("tensor", lambda e, h=h, j=j, ab=ab: e.matmul(
                K.ps[ab][0:128, 0:256],
                oh[0:16, h * 128:(h + 1) * 128],
                acs[0:16, j * 256:(j + 1) * 256],
                start=True, stop=True),
                waits=[d5, ohld] + K.wr(("ps", ab)), inc=K.sT)

            def ebf(e, h=h, j=j, ab=ab):
                for sh in range(2):
                    e.scalar_tensor_tensor(
                        eb2[:, sh * 256:(sh + 1) * 256],
                        K.ps[ab][0:128, 0:256],
                        K.acsT_g[:, j * 32 + sh * 16 + h:
                                 j * 32 + sh * 16 + h + 1],
                        W_ETM[:, sh * 256:(sh + 1) * 256],
                        ALU.subtract, ALU.add)
            eb = E_.op("vector", ebf,
                       waits=[bca, acsT_done,
                              prev.get(("mt", h - 1), prev[("mt", -1)])],
                       inc=K.sV)

            def exf(e, ab=ab):
                e.activation(ebc[:, 0:256], K.ps[ab][0:128, 0:256], AF.Exp)
                return e.activation(eb2[:, 0:512], eb2[:, 0:512], AF.Exp)
            ex = E_.op("scalar", exf,
                       waits=[eb, prev.get(("cexp", h - 1),
                                           prev[("cexp", -1)])], inc=K.sA)
            K.rd(("ps", ab), ex)

            def mtf(e, h=h):
                for sh in range(2):
                    e.tensor_mul(mt[:, sh * 256:(sh + 1) * 256],
                                 eb2[:, sh * 256:(sh + 1) * 256],
                                 cbt[:, sh * 256:(sh + 1) * 256])
                e.tensor_mul(cexp[:, 0:256], xc[:, 17 * 512:17 * 512 + 256],
                             ebc[:, 0:256])
            mtt = E_.op("vector", mtf,
                        waits=[ex, cbt_done,
                               prev.get(("ym", h - 1), prev[("ym", -1)])],
                        inc=K.sV)
            prev[("mt", h)] = mtt
            prev[("cexp", h)] = mtt

            def xdf(e, h=h):
                for sh in range(2):
                    e.tensor_scalar(xdd[:, sh * 128:(sh + 1) * 128],
                                    xdtT[:, h * 256 + sh * 128:
                                         h * 256 + sh * 128 + 128],
                                    decT[:, sh * 16 + h:sh * 16 + h + 1],
                                    None, ALU.mult)
            xdt2 = E_.op("vector", xdf,
                         waits=[xdt_done, l4,
                                prev.get(("stm", h - 1), prev[("stm", -1)])],
                         inc=K.sV)

            stm = E_.op("tensor", lambda e, stb=stb: [
                e.matmul(K.ps[stb][0:128, 0:128], btok[:, 0:128],
                         xdd[:, 0:128], start=True, stop=False),
                e.matmul(K.ps[stb][0:128, 0:128], btok[:, 128:256],
                         xdd[:, 128:256], start=False, stop=True)],
                waits=[btok_done, xdt2] + K.wr(("ps", stb)), inc=K.sT)
            prev[("stm", h)] = stm

            cbk = E_.op("vector", lambda e, h=h: e.tensor_copy(
                carryb[:, 0:128], carry[:, h * 128:(h + 1) * 128]),
                waits=[cz, prev.get(("cu", h - 1), prev[("cu", -1)]),
                       prev.get(("ym", h - 1), prev[("ym", -1)])],
                inc=K.sV)

            ym = E_.op("tensor", lambda e, h=h, yb=yb: [
                e.matmul(K.ps[yb][0:128, 0:256], carryb[:, 0:128],
                         cexp[:, 0:256], start=True, stop=False),
                e.matmul(K.ps[yb][0:128, 0:256],
                         xdtT[:, h * 256:h * 256 + 128],
                         mt[:, 0:256], start=False, stop=False),
                e.matmul(K.ps[yb][0:128, 0:256],
                         xdtT[:, h * 256 + 128:h * 256 + 256],
                         mt[:, 256:512], start=False, stop=True)],
                waits=[cbk, mtt] + K.wr(("ps", yb)), inc=K.sT)
            prev[("ym", h)] = ym

            dx = E_.op("vector", lambda e, h=h: e.tensor_scalar(
                dxs[:, 0:256], xc[:, h * 512:h * 512 + 256],
                K.dcols[:, h:h + 1], None, ALU.mult),
                waits=[conv_done,
                       prev.get(("ye", h - 1), prev[("ye", -1)])], inc=K.sV)

            def yef(e, h=h, yb=yb, j=j):
                e.scalar_tensor_tensor(
                    ysl[:, (h % 8) * 256:(h % 8) * 256 + 256],
                    K.ps[yb][0:128, 0:256], 1.0, dxs[:, 0:256],
                    ALU.mult, ALU.add)
            ye = E_.op("vector", yef,
                       waits=[ym, dx,
                              chunk_state.get((j, "ysl0")) if h >= 8 else
                              chunk_state.get((j - 1, "ysl1"))],
                       inc=K.sV)
            K.rd(("ps", yb), ye)
            prev[("ye", h)] = ye
            if h == 7:
                chunk_state[(j, "ysl0")] = E_.dma(
                    "sync",
                    K.ysp[:, :].rearrange("p (a t) -> p a t", a=16)
                    [:, 0:8, j * 256:(j + 1) * 256],
                    ysl[:, 0:2048].rearrange("p (a t) -> p a t", a=8),
                    waits=[ye])
            if h == 15:
                chunk_state[(j, "ysl1")] = E_.dma(
                    "sync",
                    K.ysp[:, :].rearrange("p (a t) -> p a t", a=16)
                    [:, 8:16, j * 256:(j + 1) * 256],
                    ysl[:, 0:2048].rearrange("p (a t) -> p a t", a=8),
                    waits=[ye])

            def cuf(e, h=h, stb=stb, j=j):
                e.scalar_tensor_tensor(
                    carry[:, h * 128:(h + 1) * 128],
                    carry[:, h * 128:(h + 1) * 128],
                    K.cdecb[:, j * 16 + h:j * 16 + h + 1],
                    K.ps[stb][0:128, 0:128], ALU.mult, ALU.add)
            cu = E_.op("vector", cuf, waits=[stm, cbk, l4], inc=K.sV)
            K.rd(("ps", stb), cu)
            prev[("cu", h)] = cu

        for kk2 in ("ym", "cu", "ye", "stm", "mt", "cexp"):
            chunk_state[(j, kk2 + "15")] = prev[(kk2, 15)]
        chunk_state[(j, "carry")] = prev[("cu", 15)]
        chunk_state[(j, "free")] = t3
        chunk_state[(j, "l2")] = l2
        chunk_state[(j, "dtT_done")] = dtT_done
        chunk_state[(j, "xdt31")] = xt_ev[31]
        chunk_state[(j, "xdd15")] = prev[("stm", 15)]
        chunk_state[(j, "stm15")] = prev[("stm", 15)]
        chunk_state[(j, "mt15")] = prev[("mt", 15)]

    for g in range(4):
        gfree = chunk_state.get((g * 4 - 1, "free"))
        gd = emit_inproj(g, gfree, winb_free=hprev if g == 0 else None)
        group_done[g] = gd
        for j in range(g * 4, g * 4 + 4):
            emit_chunk(j)

    st_dma = E_.dma("sync", K.cc_in[:, :], carry[:, :],
                    waits=[chunk_state[(15, "carry")]])

    def ccf2(e):
        return e.collective_compute(
            "AllGather", ALU.bypass, replica_groups=[list(range(8))],
            ins=[K.cc_in[:, :].opt()], outs=[K.cc_out[:, :].opt()])
    cc2 = E_.op("gpsimd", ccf2, waits=[st_dma], inc=K.sP)

    sbb = [u4.bitcast(F32)[:, 0:2048], u4.bitcast(F32)[:, 2048:4096]]
    acc = None
    for jj in range(8):
        ldt = E_.dma("sync", sbb[jj % 2][:, 0:2048],
                     K.cc_out[jj * 128:(jj + 1) * 128, :],
                     waits=[cc2, acc, mm_of_oc.get(("last", 0))])
        if jj == 0:
            acc = E_.op("vector", lambda e: e.tensor_scalar(
                top["sin"][:, :], sbb[0][:, 0:2048], K.rdmask_sb[:, 0:1],
                None, ALU.mult), waits=[ldt], inc=K.sV)
        else:
            acc = E_.op("vector", lambda e, jj=jj: e.scalar_tensor_tensor(
                top["sin"][:, :], sbb[jj % 2][:, 0:2048],
                K.rdmask_sb[:, jj:jj + 1], top["sin"][:, :],
                ALU.mult, ALU.add), waits=[ldt], inc=K.sV)
    sin_done = E_.op("vector", lambda e: e.tensor_copy(
        top["sinb"][:, :], top["sin"][:, :]), waits=[acc], inc=K.sV)

    pj1 = E_.op("vector", lambda e: e.tensor_tensor_scan(
        K.lasts[0:16, 20:36], K.lasts[0:16, 0:16],
        K.zeros16[0:16, 0:16], 0.0, ALU.add, ALU.add),
        waits=[chunk_state[(15, "carry")]], inc=K.sV)

    def pjshift(e):
        e.memset(K.lasts[0:16, 17:18], 0.0)
        e.tensor_copy(K.lasts[0:16, 18:18 + 15], K.lasts[0:16, 20:35])
    pj2 = E_.op("gpsimd", pjshift, waits=[pj1], inc=K.sG)
    pj3 = E_.op("scalar", lambda e: e.activation(
        K.pj_sb[0:16, 0:16], K.lasts[0:16, 17:33], AF.Exp),
        waits=[pj2], inc=K.sA)

    pjb_last = None
    for j in range(16):
        pd = E_.op("vector", lambda e, j=j: e.tensor_scalar(
            dtm[0:16, 0:16], K.e16_sb[0:16, 0:16], K.pj_sb[0:16, j:j + 1],
            None, ALU.mult), waits=[pj3, pjb_last], inc=K.sV)
        pm = E_.op("tensor", lambda e: e.matmul(
            K.ps[2][0:128, 0:16], K.ones16f[0:16, :], dtm[0:16, 0:16],
            start=True, stop=True), waits=[pd] + K.wr(("ps", 2)), inc=K.sT)
        pe = E_.op("vector", lambda e, j=j: e.tensor_copy(
            K.pj128[:, j * 16:(j + 1) * 16], K.ps[2][0:128, 0:16]),
            waits=[pm], inc=K.sV)
        K.rd(("ps", 2), pe)
        pjb_last = pe

    K.pop()
    return dict(sin=sin_done, pj=pjb_last,
                ydone=chunk_state[(15, "ysl1")])


def ph_mamba2(K, E_, m1):
    K.push()
    top = K.mamba_top()
    acs = top["acs"]
    oh = top["oh"]
    sz = K.a16(16 * 1024)
    wout = K.a16(16 * 1024)
    u4 = K.a16(8 * 1024)
    wzb = [K.a16(8 * 128), K.a16(8 * 128)]
    ybuf = K.a32(16 * 256)
    cfb = K.a16(256)
    ebc2 = K.a16(2 * 256)
    g2b = K.a16(16 * 256)
    xst = [K.a32(1024), K.a32(1024)]
    stage = [K.a32(1024), K.a32(1024)]
    rbc = K.a32(256)

    wol = None
    for kc in range(16):
        wol = E_.dma("gpsimd", wout[:, kc * 1024:(kc + 1) * 1024],
                     K.DR["woutT"][kc * 128:(kc + 1) * 128, :])

    zmm_of = {}
    prev_out_dma = None
    for tg in range(4):
        q0 = tg * 1024
        lnd = emit_ln(K, E_, K.x2, q0, 1024, u4,
                      extra=[zmm_of.get((tg - 1, 15))])

        zprev = None
        for zc in range(16):
            wl = None
            for dch in range(8):
                wl = E_.dma("gpsimd",
                            wzb[zc % 2][:, dch * 128:(dch + 1) * 128],
                            K.DR["winT"][dch * 128:(dch + 1) * 128,
                                         zc * 128:(zc + 1) * 128],
                            waits=[zmm_of.get((tg, zc - 2),
                                              zmm_of.get((tg - 1, 14 + zc)))]
                            if dch == 0 else ())

            def zmf(e, zc=zc):
                for dch in range(8):
                    for st in range(2):
                        e.matmul(K.ps[st][0:128, 0:512],
                                 wzb[zc % 2][:, dch * 128:(dch + 1) * 128],
                                 u4[:, dch * 1024 + st * 512:
                                    dch * 1024 + st * 512 + 512],
                                 start=(dch == 0), stop=(dch == 7))
            zm = E_.op("tensor", zmf,
                       waits=[lnd, wl, prev_out_dma] +
                       K.wr(("ps", 0), ("ps", 1)), inc=K.sT)
            zmm_of[(tg, zc)] = zm
            zprev = E_.op("scalar", lambda e, zc=zc: [
                e.activation(sz[:, zc * 1024:zc * 1024 + 512],
                             K.ps[0][0:128, 0:512], AF.Silu),
                e.activation(sz[:, zc * 1024 + 512:(zc + 1) * 1024],
                             K.ps[1][0:128, 0:512], AF.Silu)],
                waits=[zm, prev_out_dma], inc=K.sA)
            K.rd(("ps", 0), zprev)
            K.rd(("ps", 1), zprev)
        z_done = zprev

        gn_done = []
        prev_ctok = [None]
        for jj in range(4):
            j = tg * 4 + jj
            yl = E_.dma("sync",
                        ybuf[:, 0:4096].rearrange("p (a t) -> p a t", a=16),
                        K.ysp[:, :].rearrange("p (a t) -> p a t", a=16)
                        [:, :, j * 256:(j + 1) * 256],
                        waits=[m1["ydone"],
                               gn_done[-1] if gn_done else None])
            cl = E_.dma("sync", cfb[:, 0:256],
                        K.cfms[:, j * 256:(j + 1) * 256],
                        waits=[prev_ctok[0]])
            ctoks = []
            for h in range(16):
                ab = 2 + (h % 2)
                yb2 = 4 + (h % 2)
                eo = (h % 2) * 256
                bca = E_.op("tensor", lambda e, h=h, j=j, ab=ab: e.matmul(
                    K.ps[ab][0:128, 0:256],
                    oh[0:16, h * 128:(h + 1) * 128],
                    acs[0:16, j * 256:(j + 1) * 256],
                    start=True, stop=True),
                    waits=[m1["sin"], m1["pj"], yl, cl] + K.wr(("ps", ab)),
                    inc=K.sT)
                exb = E_.op("scalar", lambda e, ab=ab, eo=eo: e.activation(
                    ebc2[:, eo:eo + 256], K.ps[ab][0:128, 0:256], AF.Exp),
                    waits=[bca, ctoks[-2] if len(ctoks) >= 2 else None],
                    inc=K.sA)
                K.rd(("ps", ab), exb)
                cx = E_.op("vector", lambda e, eo=eo: e.tensor_mul(
                    ebc2[:, eo:eo + 256], cfb[:, 0:256], ebc2[:, eo:eo + 256]),
                    waits=[exb, cl], inc=K.sV)
                ym = E_.op("tensor", lambda e, h=h, yb2=yb2, eo=eo: e.matmul(
                    K.ps[yb2][0:128, 0:256],
                    top["sinb"][:, h * 128:(h + 1) * 128],
                    ebc2[:, eo:eo + 256],
                    start=True, stop=True),
                    waits=[cx, m1["sin"]] + K.wr(("ps", yb2)), inc=K.sT)
                ce = E_.op("vector",
                           lambda e, h=h, j=j, yb2=yb2:
                           e.scalar_tensor_tensor(
                               ybuf[:, h * 256:(h + 1) * 256],
                               K.ps[yb2][0:128, 0:256],
                               K.pj128[:, j * 16 + h:j * 16 + h + 1],
                               ybuf[:, h * 256:(h + 1) * 256],
                               ALU.mult, ALU.add),
                           waits=[ym, yl], inc=K.sV)
                K.rd(("ps", yb2), ce)
                ctoks.append(ce)
            corr_done = ctoks[-1]
            prev_ctok[0] = corr_done

            def gf(e, jj=jj):
                e.tensor_mul(
                    ybuf[:, 0:4096].rearrange("p (a t) -> p a t", a=16),
                    ybuf[:, 0:4096].rearrange("p (a t) -> p a t", a=16),
                    sz[:, 0:16 * 1024].rearrange("p (a t) -> p a t", a=16)
                    [:, :, jj * 256:(jj + 1) * 256])
            gt = E_.op("vector", gf, waits=[corr_done, z_done], inc=K.sV)

            sq = E_.op("scalar", lambda e: e.activation(
                g2b[:, 0:4096], ybuf[:, 0:4096], AF.Square), waits=[gt],
                inc=K.sA)

            def rmf(e):
                for blk in range(16):
                    e.matmul(K.ps[2][0:128, 0:256], K.onesb[:],
                             g2b[:, blk * 256:(blk + 1) * 256],
                             start=(blk == 0), stop=(blk == 15))
            rm = E_.op("tensor", rmf, waits=[sq] + K.wr(("ps", 2)), inc=K.sT)
            r2 = E_.op("scalar", lambda e: e.activation(
                rbc[:, 0:256], K.ps[2][0:128, 0:256], AF.Sqrt,
                scale=1.0 / DIN, bias=EPS), waits=[rm], inc=K.sA)
            K.rd(("ps", 2), r2)

            def gnf(e, jj=jj):
                e.reciprocal(rbc[:, 0:256], rbc[:, 0:256])
                K.rtrip(e)
                for blk in range(16):
                    e.scalar_tensor_tensor(
                        sz[:, blk * 1024 + jj * 256:
                           blk * 1024 + jj * 256 + 256],
                        ybuf[:, blk * 256:(blk + 1) * 256],
                        K.normw_sb[:, blk:blk + 1], rbc[:, 0:256],
                        ALU.mult, ALU.mult)
            g4 = E_.op("vector", gnf, waits=[r2, gt], inc=K.sV)
            gn_done.append(g4)

        prev_dma = None
        ev_of = {}
        for oc in range(8):
            xl = E_.dma("sync", xst[oc % 2][:, 0:1024],
                        K.x2[oc * 128:(oc + 1) * 128, q0:q0 + 1024],
                        waits=[ev_of.get(oc - 2, prev_out_dma)])

            def omf(e, oc=oc):
                for kc in range(16):
                    for st in range(2):
                        e.matmul(K.ps[st][0:128, 0:512],
                                 wout[:, kc * 1024 + oc * 128:
                                      kc * 1024 + oc * 128 + 128],
                                 sz[:, kc * 1024 + st * 512:
                                    kc * 1024 + st * 512 + 512],
                                 start=(kc == 0), stop=(kc == 15))
            om = E_.op("tensor", omf, waits=[gn_done[-1], wol] +
                       K.wr(("ps", 0), ("ps", 1)), inc=K.sT)

            def oef(e, oc=oc):
                for st in range(2):
                    e.tensor_add(stage[oc % 2][:, st * 512:st * 512 + 512],
                                 K.ps[st][0:128, 0:512],
                                 xst[oc % 2][:, st * 512:st * 512 + 512])
            ev = E_.op("vector", oef, waits=[om, xl, prev_dma], inc=K.sV)
            ev_of[oc] = ev
            K.rd(("ps", 0), ev)
            K.rd(("ps", 1), ev)
            prev_dma = E_.dma("sync",
                              K.x3[oc * 128:(oc + 1) * 128, q0:q0 + 1024],
                              stage[oc % 2][:, 0:1024], waits=[ev])
            if DEBUG:
                E_.dma("sync", K.DBG["dbg_x3"][oc * 128:(oc + 1) * 128,
                                               q0:q0 + 1024],
                       stage[oc % 2][:, 0:1024], out_sem=K.sO)
        prev_out_dma = prev_dma
    K.pop()


def build_program(upto=7):
    nc = bass.Bass("TRN2", target_bir_lowering=False)
    K = KB(nc)

    e0 = E(K)
    ph_init(K, e0)
    e0.emit("init")

    if upto >= 1:
        e1 = E(K)
        ph_qkv(K, e1)
        e1.emit("qkv")

    if upto >= 2:
        K.push()
        al = {"gp": K.a16(3 * NH * 384), "wo": K.a16(8 * 1024), "load": None}
        for ap in range(4):
            ei = E(K)
            ph_attn(K, ei, ap, al)
            ei.emit(f"attn{ap}")
            if upto == 2 and ap == 0:
                break
        K.pop()

    if upto >= 3:
        e2 = E(K)
        ph_ffn(K, e2, K.x1, K.x2, final=False,
               dbg="dbg_x2" if DEBUG else None)
        e2.emit("ffn1")

    if upto >= 4:
        e3 = E(K)
        m1 = ph_mamba1(K, e3)
        e3.emit("mamba1")

    if upto >= 5:
        e4 = E(K)
        ph_mamba2(K, e4, m1)
        e4.emit("mamba2")

    if upto >= 6:
        e5 = E(K)
        ph_ffn(K, e5, K.x3, K.OUT, final=True, dbg=None)

        def fin(e):
            e.wait_ge(K.sO.h, K.sO.v)
        e5.tape["sync"].append(fin)
        e5.emit("ffn2")
    return nc, K


def kernel(**inputs):
    full = {k: np.asarray(v) for k, v in inputs.items()}
    nc, K = build_program()
    in_maps = [make_core_inputs(c, full) for c in range(8)]
    res = run_bass_kernel_spmd(nc, in_maps, core_ids=list(range(8)))
    out = np.empty((B, S, D), np.float32)
    for c in range(8):
        b, hf = c // 2, c % 2
        out[b, hf * T:(hf + 1) * T] = np.asarray(res.results[c]["out"]).T
    return out


if __name__ == "__main__":
    nc, K = build_program()
    try:
        print("built ok; instructions:", nc.n_instructions())
    except Exception:
        print("built ok")


# version 08
# speedup vs baseline: 1.0525x; 1.0041x over previous
import numpy as np
import ml_dtypes
import concourse.bass as bass
import concourse.mybir as mybir
from concourse.bass_utils import run_bass_kernel_spmd

F32 = mybir.dt.float32
BF16 = mybir.dt.bfloat16
I32 = mybir.dt.int32
AF = mybir.ActivationFunctionType
ALU = mybir.AluOpType
AX = mybir.AxisListType

B, S, D = 4, 8192, 1024
T = 4096
HALO = 128
XT = T + 2 * HALO
NH, HD = 16, 64
HID = 3072
DIN, DST, NCH, CH = 2048, 128, 16, 256
CONV_DIM = DIN + 2 * DST
NPROJ = 2 * DIN + 2 * DST + NH
XBCDT = CONV_DIM + NH
EPS = 1e-5
NEG = -1e30
Q127 = 1.0 / 127.0

DEBUG = False
ENGS = ("sync", "tensor", "vector", "scalar", "gpsimd")


class Sem:
    def __init__(self, nc, name):
        self.h = nc.alloc_semaphore(name)
        self.v = 0


def _pad_slice_T(x_b, lo, hi):
    n = hi - lo
    out = np.zeros((D, n), np.float32)
    a, b2 = max(lo, 0), min(hi, S)
    if b2 > a:
        out[:, a - lo:b2 - lo] = np.ascontiguousarray(x_b[a:b2]).T
    return out


def _build_gp(rel, hf):
    p = np.arange(128)[:, None]
    f = np.arange(384)[None, :]
    idx = f - p
    band = (idx >= 0) & (idx <= 256)
    g = np.where(band[None], np.take(rel, np.clip(idx, 0, 256), axis=1), NEG)
    g = np.ascontiguousarray(g.astype(np.float32))
    g_first = g.copy()
    if hf == 0:
        g_first[:, :, :128] = NEG
    g_last = g.copy()
    if hf == 1:
        g_last[:, :, 256:] = NEG
    return g, g_first, g_last


def make_core_inputs(c, full):
    b, hf = c // 2, c % 2
    x = full["L0_attn_a_x"][b]
    lo = hf * T - HALO
    d = {}
    d["xin"] = _pad_slice_T(x, lo, lo + XT)
    d["wq"] = np.ascontiguousarray(full["L0_attn_a_Wq"], np.float32)
    d["wk"] = np.ascontiguousarray(full["L0_attn_a_Wk"], np.float32)
    d["wv"] = np.ascontiguousarray(full["L0_attn_a_Wv"], np.float32)
    d["wo"] = np.ascontiguousarray(full["L0_attn_a_Wo"], np.float32)
    for nm in ("bq", "bk", "bv", "bo"):
        v = np.asarray(full[f"L0_attn_a_{nm}"], np.float32)
        d[nm + "c"] = np.ascontiguousarray(v.reshape(8, 128).T)
        d[nm + "r"] = np.ascontiguousarray(v.reshape(1, D))
    g, gf, gl = _build_gp(np.asarray(full["L0_attn_a_rel_pos_bias"],
                                     np.float32), hf)
    d["gp_int"], d["gp_first"], d["gp_last"] = g, gf, gl
    d["wgT"] = np.ascontiguousarray(full["L0_ffn_w_gate"].T.astype(np.int32))
    d["wuT"] = np.ascontiguousarray(full["L0_ffn_w_up"].T.astype(np.int32))
    d["wdT"] = np.ascontiguousarray(full["L0_ffn_w_down"].T.astype(np.int32))
    d["winT"] = np.ascontiguousarray(full["L0_attn_b_W_in"].T.astype(np.float32))
    d["woutT"] = np.ascontiguousarray(full["L0_attn_b_W_out"].T.astype(np.float32))
    d["convw"] = np.ascontiguousarray(
        np.asarray(full["L0_attn_b_conv_w"], np.float32).reshape(CONV_DIM, 4))
    d["convbc"] = np.ascontiguousarray(
        np.asarray(full["L0_attn_b_conv_b"], np.float32).reshape(18, 128).T)
    d["smalls"] = np.ascontiguousarray(np.stack(
        [np.asarray(full["L0_attn_b_dt_bias"], np.float32),
         np.asarray(full["L0_attn_b_A_log"], np.float32),
         np.asarray(full["L0_attn_b_D"], np.float32)], axis=1))
    d["drow"] = np.ascontiguousarray(
        np.asarray(full["L0_attn_b_D"], np.float32).reshape(1, NH))
    d["normw"] = np.ascontiguousarray(
        np.asarray(full["L0_attn_b_norm_weight"], np.float32).reshape(16, 128).T)
    oh = np.zeros((16, 2048), np.float32)
    e16 = np.zeros((16, 256), np.float32)
    for h in range(16):
        oh[h, h * 128:(h + 1) * 128] = 1.0
        e16[h, h::16] = 1.0
    d["oh"] = oh
    d["e16"] = e16
    rm = np.zeros((128, 8), np.float32)
    if hf == 1:
        rm[:, c - 1] = 1.0
    d["rdmask"] = rm
    return d


class KB:
    def __init__(self, nc):
        self.nc = nc
        DR = {}

        def din(name, shape, dt=F32):
            DR[name] = nc.dram_tensor(name, list(shape), dt,
                                      kind="ExternalInput").ap()
        din("xin", [D, XT])
        for w in ("wq", "wk", "wv", "wo"):
            din(w, [D, D])
        for bn in ("bq", "bk", "bv", "bo"):
            din(bn + "c", [128, 8])
            din(bn + "r", [1, D])
        din("gp_int", [NH, 128, 384])
        din("gp_first", [NH, 128, 384])
        din("gp_last", [NH, 128, 384])
        din("wgT", [D, HID], I32)
        din("wuT", [D, HID], I32)
        din("wdT", [HID, D], I32)
        din("winT", [D, NPROJ])
        din("woutT", [DIN, D])
        din("convw", [CONV_DIM, 4])
        din("convbc", [128, 18])
        din("smalls", [NH, 3])
        din("drow", [1, NH])
        din("normw", [128, 16])
        din("oh", [16, 2048])
        din("e16", [16, 256])
        din("rdmask", [128, 8])
        self.DR = DR

        self.OUT = nc.dram_tensor("out", [D, T], F32, kind="ExternalOutput").ap()
        self.DBG = {}
        if DEBUG:
            for nm in ("dbg_x1", "dbg_x2", "dbg_x3"):
                self.DBG[nm] = nc.dram_tensor(nm, [D, T], F32,
                                              kind="ExternalOutput").ap()
            self.DBG["dbg_ao"] = nc.dram_tensor(
                "dbg_ao", [128, 8 * 1024], BF16, kind="ExternalOutput").ap()

        self.x1 = nc.dram_tensor("x1s", [D, T], F32).ap()
        self.x2 = nc.dram_tensor("x2s", [D, T], F32).ap()
        self.x3 = nc.dram_tensor("x3s", [D, T], F32).ap()
        kw = dict(kind="ExternalOutput") if DEBUG else {}
        self.qsc = nc.dram_tensor("qscr", [D, T], BF16, **kw).ap()
        self.ksc = nc.dram_tensor("kscr", [D, XT], BF16, **kw).ap()
        self.vsc = nc.dram_tensor("vscr", [XT, D], BF16, **kw).ap()
        self.ysp = nc.dram_tensor("yspill", [128, 16 * T], F32, **kw).ap()
        self.cfms = nc.dram_tensor("cfms", [DST, T], BF16, **kw).ap()
        self.cc_in = nc.dram_tensor("cc_in", [128, DIN], F32).ap()
        self.cc_out = nc.dram_tensor("cc_out", [8 * 128, DIN], F32).ap()
        self.ch_in = nc.dram_tensor("ch_in", [128, 54], F32).ap()
        self.ch_out = nc.dram_tensor("ch_out", [8 * 128, 54], F32).ap()

        self.PB = nc.alloc_sbuf_tensor("PB", [128, 96 * 1024 + 512], BF16)
        self.off = 0
        self.marks = []

        A = nc.alloc_sbuf_tensor
        self.ident = A("ident", [128, 128], BF16)
        self.identf = A("identf", [128, 128], F32)
        self.onesb = A("onesb", [128, 128], BF16)
        self.onesf = A("onesf", [128, 128], F32)
        self.ones16f = A("ones16f", [16, 128], F32)
        self.ones1 = A("ones1", [1, 512], BF16)
        self.biasc = A("biasc", [128, 32], F32)
        self.biasr = A("biasr", [1, D], BF16)
        self.convw_sb = A("convw_sb", [128, 72], F32)
        self.convb_sb = A("convb_sb", [128, 18], F32)
        self.smalls_sb = A("smalls_sb", [16, 8], F32)
        self.drow_sb = A("drow_sb", [1, NH], BF16)
        self.dcols = A("dcols", [128, NH], F32)
        self.normw_sb = A("normw_sb", [128, 16], F32)
        self.rdmask_sb = A("rdmask_sb", [128, 8], F32)
        self.etm = A("etm", [128, 512], F32)
        self.zeros16 = A("zeros16", [16, 256], F32)
        self.acsT_g = A("acsT_g", [128, NCH * 32], F32)
        self.lasts = A("lasts", [16, 40], F32)
        self.pj_sb = A("pj_sb", [16, NCH], F32)
        self.pj128 = A("pj128", [128, NCH * 16], F32)
        self.cdecb = A("cdecb", [128, NCH * 16], F32)
        self.e16_sb = A("e16_sb", [16, 256], F32)
        self.halo3 = A("halo3", [128, 54], F32)
        self.cvals = A("cvals", [128, 8], F32)
        self.scr_nop = A("scrnop", [1, 8], F32)

        self.ps = [nc.alloc_psum_tensor(f"ps{i}", [128, 512], F32)
                   for i in range(6)]
        self.pbf = [nc.alloc_psum_tensor(f"pbf{i}", [128, 1024], BF16)
                    for i in range(2)]
        self.ptp = [t[:, :].bitcast(F32) for t in self.pbf]

        for s in ("sS", "sG", "sC", "sT", "sV", "sA", "sP", "sO", "sRT",
                  "sRP"):
            setattr(self, s, Sem(nc, s))

        self.blr = {}

        self.CONSTS = [EPS, 1.0, -1.0, Q127, 1.0 / DIN, 0.5, 2.0, 0.0]
        for i, v in enumerate(self.CONSTS):
            nc.const_aps.register(F32, v, self.cvals[:, i:i + 1])

    def a16(self, cols):
        cols = (cols + 15) // 16 * 16
        v = self.PB[:, self.off:self.off + cols]
        self.off += cols
        assert self.off <= 96 * 1024 + 512, f"SBUF pool overflow: {self.off}"
        return v

    def a32(self, cols):
        return self.a16(2 * cols).bitcast(F32)

    def push(self):
        self.marks.append(self.off)

    def pop(self):
        self.off = self.marks.pop()

    def rtrip(self, e, eng="vector"):
        s = self.sRT if eng == "vector" else self.sRP
        s.v += 1
        e.engine_nop().then_inc(s.h, 1)
        e.wait_ge(s.h, s.v)

    def rd(self, bank, tok):
        self.blr[bank] = tok

    def wr(self, *banks):
        return [self.blr.get(b) for b in banks]

    def mamba_top(self):
        base = self.off
        t = dict(acs=self.a32(T), oh=self.a32(2048), sin=self.a32(2048),
                 sinb=self.a16(2048))
        if getattr(self, "_mtop_base", None) is None:
            self._mtop_base = base
        else:
            assert self._mtop_base == base, (self._mtop_base, base)
        return t


PE_DELAY = 500


class E:

    def __init__(self, K):
        self.K = K
        self.tape = {e: [] for e in ENGS}

    def op(self, eng, fn, waits=(), inc=None, incn=1):
        K = self.K
        wl = [(t[0].h, t[1]) for t in waits if t is not None]
        pe_tok = any((t is not None and t[0] is K.sT) for t in waits)
        if inc is not None:
            inc.v += incn

        def run(e, fn=fn, wl=wl, inc=inc, incn=incn, eng=eng, pe_tok=pe_tok):
            for h, v in wl:
                e.wait_ge(h, v)
            if pe_tok and eng != "tensor":
                e.nop(cycle_cnt=PE_DELAY)
            r = fn(e)
            if inc is not None:
                if isinstance(r, list):
                    r = r[-1] if r else None
                if eng == "tensor":
                    e.drain()
                    r = e.nop()
                elif r is None:
                    if eng in ("vector", "gpsimd"):
                        r = e.engine_nop()
                    elif eng == "scalar":
                        r = e.copy(K.scr_nop[0:1, 0:1], K.scr_nop[0:1, 1:2])
                    else:
                        r = e.nop()
                r.then_inc(inc.h, incn)
        self.tape[eng].append(run)
        return (inc, inc.v) if inc is not None else None

    def dma(self, eng, dst, src, waits=(), out_sem=None):
        K = self.K
        sem = out_sem if out_sem is not None else (
            K.sG if eng == "gpsimd" else (K.sC if eng == "scalar" else K.sS))
        return self.op(eng, lambda e: e.dma_start(out=dst, in_=src),
                       waits=waits, inc=sem, incn=16)

    def emit(self, name):
        with self.K.nc.Block(name) as blk:
            for eng in ENGS:
                lst = self.tape[eng]
                if not lst:
                    continue

                def body(e, lst=lst):
                    for fn in lst:
                        fn(e)
                getattr(blk, eng)(body)


def emit_ln(K, E_, src, col0, ncols, u_dst, extra=()):
    import os
    lnp = int(os.environ.get("LN_PART", 9))
    nt = (ncols + 511) // 512
    prev_ln = getattr(K, "_ln_prev", [])
    K.push()
    xc = [K.a16(8 * 512), K.a16(8 * 512)]
    x2b = K.a16(8 * 512)
    mbc = K.a32(512)
    rbc = K.a32(512)
    tsc = K.a32(512)
    done = []
    lastmm = None
    for t in range(nt):
        w = min(512, ncols - t * 512)
        bb = xc[t % 2]
        reuse = [done[t - 2]] if t >= 2 else list(prev_ln) + list(extra)
        ld = None
        for dch in range(8):
            ld = E_.dma("gpsimd", bb[:, dch * 512:dch * 512 + w],
                        src[dch * 128:dch * 128 + 128,
                            col0 + t * 512:col0 + t * 512 + w],
                        waits=reuse if dch == 0 else ())

        if lnp < 2:
            done.append(ld)
            continue

        def sf(e, bb=bb, w=w):
            r = None
            for dch in range(8):
                r = e.activation(x2b[:, dch * 512:dch * 512 + w],
                                 bb[:, dch * 512:dch * 512 + w], AF.Square)
            return r
        sq = E_.op("scalar", sf, waits=[ld] + ([done[t - 1]] if t >= 1
                                               else list(prev_ln)), inc=K.sA)
        if lnp < 3:
            done.append(sq)
            continue

        def mf(e, bb=bb, w=w):
            for dch in range(8):
                e.matmul(K.ps[0][:, 0:w], K.onesb[:],
                         bb[:, dch * 512:dch * 512 + w],
                         start=(dch == 0), stop=(dch == 7))
            for dch in range(8):
                e.matmul(K.ps[1][:, 0:w], K.onesb[:],
                         x2b[:, dch * 512:dch * 512 + w],
                         start=(dch == 0), stop=(dch == 7))
        mm = E_.op("tensor", mf,
                   waits=[ld, sq] + K.wr(("ps", 0), ("ps", 1)), inc=K.sT)
        lastmm = mm
        if lnp < 4:
            done.append(mm)
            continue

        def v1(e, w=w):
            e.tensor_scalar(mbc[:, 0:w], K.ps[0][:, 0:w], 1.0 / 1024, None,
                            ALU.mult)
            e.tensor_scalar(tsc[:, 0:w], K.ps[1][:, 0:w], 1.0 / 1024, None,
                            ALU.mult)
        t1 = E_.op("vector", v1, waits=[mm], inc=K.sV)
        K.rd(("ps", 0), t1)
        K.rd(("ps", 1), t1)
        if lnp < 5:
            done.append(t1)
            continue

        t2 = E_.op("scalar", lambda e, w=w: e.activation(
            rbc[:, 0:w], mbc[:, 0:w], AF.Square), waits=[t1], inc=K.sA)
        t3 = E_.op("vector", lambda e, w=w: e.tensor_sub(
            tsc[:, 0:w], tsc[:, 0:w], rbc[:, 0:w]), waits=[t2], inc=K.sV)
        t4 = E_.op("scalar", lambda e, w=w: e.activation(
            tsc[:, 0:w], tsc[:, 0:w], AF.Sqrt, bias=EPS), waits=[t3], inc=K.sA)

        def v2(e, bb=bb, w=w, t=t):
            e.reciprocal(rbc[:, 0:w], tsc[:, 0:w])
            for dch in range(8):
                e.tensor_sub(x2b[:, dch * 512:dch * 512 + w],
                             bb[:, dch * 512:dch * 512 + w], mbc[:, 0:w])
                e.tensor_mul(u_dst[:, dch * ncols + t * 512:
                                   dch * ncols + t * 512 + w],
                             x2b[:, dch * 512:dch * 512 + w], rbc[:, 0:w])
        t5 = E_.op("vector", v2, waits=[t4] + list(extra), inc=K.sV)
        done.append(t5)
    K.pop()
    K._ln_prev = [done[-1], lastmm]
    return done[-1]


def ph_init(K, E_):
    DR = K.DR

    def g0(e):
        def d(dst, src):
            e.dma_start(out=dst, in_=src).then_inc(K.sG.h, 16)
        d(K.biasc[:, 0:8], DR["bqc"])
        d(K.biasc[:, 8:16], DR["bkc"])
        d(K.biasc[:, 16:24], DR["bvc"])
        d(K.biasc[:, 24:32], DR["boc"])
        d(K.biasr[:, 0:D], DR["bor"])
        for a in range(18):
            d(K.convw_sb[:, a * 4:(a + 1) * 4],
              DR["convw"][a * 128:(a + 1) * 128, :])
        d(K.convb_sb[:], DR["convbc"])
        d(K.smalls_sb[0:16, 0:3], DR["smalls"])
        d(K.drow_sb[:], DR["drow"])
        d(K.normw_sb[:], DR["normw"])
        d(K.rdmask_sb[:], DR["rdmask"])
        d(K.e16_sb[:], DR["e16"])
        for i, v in enumerate(K.CONSTS):
            e.memset(K.cvals[:, i:i + 1], v)
        e.memset(K.onesb[:], 1.0)
        e.memset(K.onesf[:], 1.0)
        e.memset(K.ones16f[:], 1.0)
        e.memset(K.ones1[:], 1.0)
        e.memset(K.zeros16[:], 0.0)
        e.memset(K.scr_nop[:], 0.0)
        e.memset(K.etm[:], 0.0)
        K.rtrip(e, "gpsimd")
        e.affine_select(K.ident[:], K.onesb[:], pattern=[[-1, 128]],
                        compare_op=ALU.is_equal, fill=0.0, base=0,
                        channel_multiplier=1)
        e.affine_select(K.identf[:], K.onesf[:], pattern=[[-1, 128]],
                        compare_op=ALU.is_equal, fill=0.0, base=0,
                        channel_multiplier=1)
        e.affine_select(K.etm[:, 0:256], K.etm[:, 0:256], pattern=[[1, 256]],
                        compare_op=ALU.is_ge, fill=NEG, base=0,
                        channel_multiplier=-1)
        e.affine_select(K.etm[:, 256:512], K.etm[:, 256:512],
                        pattern=[[1, 256]], compare_op=ALU.is_ge, fill=NEG,
                        base=-128, channel_multiplier=-1)
    K.sG.v += 16 * 29
    ld = E_.op("gpsimd", g0, inc=K.sP)
    ldt = (K.sG, K.sG.v)

    s0a = E_.op("scalar", lambda e: e.activation(
        K.smalls_sb[0:16, 3:4], K.smalls_sb[0:16, 1:2], AF.Exp),
        waits=[ldt, ld], inc=K.sA)
    E_.op("vector", lambda e: e.tensor_scalar(
        K.smalls_sb[0:16, 3:4], K.smalls_sb[0:16, 3:4], -1.0, None,
        ALU.mult), waits=[s0a], inc=K.sV)

    mm = E_.op("tensor", lambda e: e.matmul(
        K.ps[0][0:128, 0:NH], K.ones1[0:1, 0:128], K.drow_sb[:],
        start=True, stop=True), waits=[ldt, ld], inc=K.sT)
    ev = E_.op("vector", lambda e: e.tensor_copy(K.dcols[:],
                                                 K.ps[0][0:128, 0:NH]),
               waits=[mm], inc=K.sV)
    K.rd(("ps", 0), ev)


def ph_qkv(K, E_):
    import os
    part = int(os.environ.get("QKV_PART", 9))
    K.push()
    u = K.a16(8 * XT)
    wbuf = [K.a16(8 * 1024), K.a16(8 * 1024)]
    stage = [K.a16(4 * 512), K.a16(4 * 512)]
    bvr = K.a16(1024)[0:1, :]

    bv_ld = E_.dma("gpsimd", bvr, K.DR["bvr"])
    ln_done = emit_ln(K, E_, K.DR["xin"], 0, XT, u)

    wname = ("wq", "wk", "wv")
    wld = [None] * 3

    def load_w(wi, waits):
        t = None
        for dch in range(8):
            t = E_.dma("gpsimd", wbuf[wi % 2][:, dch * 1024:(dch + 1) * 1024],
                       K.DR[wname[wi]][dch * 128:(dch + 1) * 128, :],
                       waits=waits if dch == 0 else ())
        return t

    if part < 1:
        K.pop()
        return
    wld[0] = load_w(0, [])
    wld[1] = load_w(1, [])

    def proj(wi, outdram, c0, nst, length, biasoff, qmul):
        groups = [list(range(i, min(i + 4, nst))) for i in range(0, nst, 4)]
        last_dma = None
        for grp in groups:
            prev_ev = None
            for oc in range(8):
                def mf(e, grp=grp, oc=oc, wi=wi, c0=c0, length=length):
                    for dch in range(8):
                        for si, st in enumerate(grp):
                            w0 = c0 + st * 512
                            wdt = min(512, length - st * 512)
                            e.matmul(K.ps[si][0:128, 0:wdt],
                                     wbuf[wi % 2][:, dch * 1024 + oc * 128:
                                                  dch * 1024 + oc * 128 + 128],
                                     u[:, dch * XT + w0:dch * XT + w0 + wdt],
                                     start=(dch == 0), stop=(dch == 7))
                w = [ln_done, wld[wi], prev_ev]
                w += K.wr(*[("ps", si) for si in range(len(grp))])
                mm = E_.op("tensor", mf, waits=w, inc=K.sT)

                def vf(e, grp=grp, oc=oc, biasoff=biasoff, qmul=qmul,
                       length=length):
                    for si, st in enumerate(grp):
                        wdt = min(512, length - st * 512)
                        if qmul is None:
                            e.tensor_scalar(
                                stage[oc % 2][:, si * 512:si * 512 + wdt],
                                K.ps[si][0:128, 0:wdt],
                                K.biasc[:, biasoff + oc:biasoff + oc + 1],
                                None, ALU.add)
                        else:
                            e.tensor_scalar(
                                stage[oc % 2][:, si * 512:si * 512 + wdt],
                                K.ps[si][0:128, 0:wdt],
                                K.biasc[:, biasoff + oc:biasoff + oc + 1],
                                qmul, ALU.add, ALU.mult)
                ev = E_.op("vector", vf, waits=[mm, last_dma], inc=K.sV)
                for si in range(len(grp)):
                    K.rd(("ps", si), ev)
                prev_ev = ev
                for si, st in enumerate(grp):
                    w0 = c0 + st * 512
                    wdt = min(512, length - st * 512)
                    last_dma = E_.dma(
                        "sync",
                        outdram[oc * 128:(oc + 1) * 128, w0 - c0:w0 - c0 + wdt],
                        stage[oc % 2][:, si * 512:si * 512 + wdt],
                        waits=[ev] if si == 0 else ())
        return last_dma

    if part < 2:
        K.pop()
        return
    qd = proj(0, K.qsc, HALO, 8, T, 0, 0.125)
    if part < 3:
        K.pop()
        return
    kd = proj(1, K.ksc, 0, 9, XT, 8, None)
    if part < 4:
        K.pop()
        return

    wld[2] = load_w(2, [qd])
    prevv = None
    for tc in range(34):
        def mf(e, tc=tc):
            for dch in range(8):
                for half in range(2):
                    e.matmul(K.ps[(tc % 2) * 2 + half][0:128, 0:512],
                             u[:, dch * XT + tc * 128:dch * XT + tc * 128 + 128],
                             wbuf[0][:, dch * 1024 + half * 512:
                                     dch * 1024 + half * 512 + 512],
                             start=(dch == 0), stop=False)
            for half in range(2):
                e.matmul(K.ps[(tc % 2) * 2 + half][0:128, 0:512],
                         K.ones1[0:1, 0:128],
                         bvr[0:1, half * 512:half * 512 + 512],
                         start=False, stop=True)
        b0, b1 = (tc % 2) * 2, (tc % 2) * 2 + 1
        mm = E_.op("tensor", mf, waits=[wld[2], bv_ld, ln_done] +
                   K.wr(("ps", b0), ("ps", b1)), inc=K.sT)

        def vf(e, tc=tc):
            for half in range(2):
                e.tensor_copy(stage[tc % 2][:, half * 512:half * 512 + 512],
                              K.ps[(tc % 2) * 2 + half][0:128, 0:512])
        ev = E_.op("vector", vf, waits=[mm, prevv], inc=K.sV)
        K.rd(("ps", b0), ev)
        K.rd(("ps", b1), ev)
        prevv = E_.dma("sync", K.vsc[tc * 128:(tc + 1) * 128, :],
                       stage[tc % 2][:, 0:1024], waits=[ev])
    K.pop()


def ph_attn(K, E_, ap, al):
    qs = ap * 1024
    K.push()
    qb = K.a16(8 * 1024)
    kb = K.a16(8 * 1280)
    vb = K.a16(10 * 1024)
    ao = K.a16(8 * 1024)
    ptg = K.a16(2 * 3072)
    pn = K.a16(4 * 768)
    nml = K.a32(16)
    dg = K.a16(4 * 256)
    xr = K.a32(8 * 1024)
    stage = [K.a32(1024), K.a32(1024)]

    if ap == 0:
        t = None
        for vi, vn in enumerate(("gp_int", "gp_first", "gp_last")):
            for h in range(NH):
                t = E_.dma("gpsimd",
                           al["gp"][:, (vi * NH + h) * 384:
                                    (vi * NH + h) * 384 + 384],
                           K.DR[vn][h])
        def expgp(e):
            r = None
            for ch in range(9):
                r = e.activation(al["gp"][:, ch * 2048:(ch + 1) * 2048],
                                 al["gp"][:, ch * 2048:(ch + 1) * 2048],
                                 AF.Exp)
            return r
        t = E_.op("scalar", expgp, waits=[t], inc=K.sA)
        for dch in range(8):
            t = E_.dma("gpsimd", al["wo"][:, dch * 1024:(dch + 1) * 1024],
                       K.DR["wo"][dch * 128:(dch + 1) * 128, :],
                       waits=[t] if dch == 0 else ())
        al["load"] = t

    spl = None
    for dch in range(8):
        spl = E_.dma("scalar", qb[:, dch * 1024:(dch + 1) * 1024],
                     K.qsc[dch * 128:(dch + 1) * 128, qs:qs + 1024])
        spl = E_.dma("scalar", kb[:, dch * 1280:(dch + 1) * 1280],
                     K.ksc[dch * 128:(dch + 1) * 128, qs:qs + 1280])
    for tc in range(10):
        spl = E_.dma("scalar", vb[:, tc * 1024:(tc + 1) * 1024],
                     K.vsc[qs + tc * 128:qs + (tc + 1) * 128, :])
    xrl = None
    for dch in range(8):
        xrl = E_.dma("sync", xr[:, dch * 1024:(dch + 1) * 1024],
                     K.DR["xin"][dch * 128:(dch + 1) * 128,
                                 HALO + qs:HALO + qs + 1024])

    def gpsel(qtg, h):
        vi = 1 if qtg == 0 else (2 if qtg == 31 else 0)
        return al["gp"][:, (vi * NH + h) * 384:(vi * NH + h) * 384 + 384]

    PVRANGE = {0: (0, 128), 1: (0, 256), 2: (0, 384), 3: (128, 512),
               4: (256, 512), 5: (384, 512)}

    exp_toks = {}
    nm_toks = {}
    dg_toks = {}
    tr_ev = {}
    pv_ev = {}

    for c in range(8):
        for sp in range(2):
            for qtj in range(4):
                qt = sp * 4 + qtj
                qtg = ap * 8 + qt
                bA = (qt % 2) * 2
                idx = c * 8 + qt

                sl = idx % 4

                def smf(e, c=c, qt=qt, bA=bA):
                    e.matmul(K.ps[bA][0:128, 0:384],
                             qb[0:64, c * 1024 + qt * 128:
                                c * 1024 + qt * 128 + 128],
                             kb[0:64, c * 1280 + qt * 128:
                                c * 1280 + qt * 128 + 384],
                             start=True, stop=True, tile_position=(0, 0))
                    e.matmul(K.ps[bA + 1][0:128, 0:384],
                             qb[64:128, c * 1024 + qt * 128:
                                c * 1024 + qt * 128 + 128],
                             kb[64:128, c * 1280 + qt * 128:
                                c * 1280 + qt * 128 + 384],
                             start=True, stop=True, tile_position=(64, 0))
                w = [spl] + K.wr(("ps", bA), ("ps", bA + 1))
                mm = E_.op("tensor", smf, waits=w, inc=K.sT)

                def emf(e, bA=bA, sl=sl):
                    o = sl * 768
                    e.activation(pn[:, o:o + 384], K.ps[bA][0:128, 0:384],
                                 AF.Exp)
                    return e.activation(pn[:, o + 384:o + 768],
                                        K.ps[bA + 1][0:128, 0:384], AF.Exp)
                w = [mm]
                if idx >= 4:
                    w.append(tr_ev[(idx - 4, 5)])
                ex = E_.op("scalar", emf, waits=w, inc=K.sA)
                exp_toks[idx] = ex
                K.rd(("ps", bA), ex)
                K.rd(("ps", bA + 1), ex)

                def lmf(e, c=c, qtg=qtg, sl=sl):
                    o = sl * 768
                    e.scalar_tensor_tensor(
                        pn[:, o:o + 384], pn[:, o:o + 384], 1.0,
                        gpsel(qtg, 2 * c), ALU.mult, ALU.mult,
                        accum_out=nml[:, sl * 4:sl * 4 + 1])
                    return e.scalar_tensor_tensor(
                        pn[:, o + 384:o + 768], pn[:, o + 384:o + 768], 1.0,
                        gpsel(qtg, 2 * c + 1), ALU.mult, ALU.mult,
                        accum_out=nml[:, sl * 4 + 1:sl * 4 + 2])
                w = [ex, al["load"]]
                if idx >= 4:
                    w.append(dg_toks[idx - 4])
                lm = E_.op("vector", lmf, waits=w, inc=K.sV)

                def nrm(e, sl=sl):
                    K.rtrip(e)
                    return e.reciprocal(nml[:, sl * 4 + 2:sl * 4 + 4],
                                        nml[:, sl * 4:sl * 4 + 2])
                nm = E_.op("vector", nrm, waits=[lm], inc=K.sV)
                nm_toks[idx] = nm

                def dgf(e, sl=sl):
                    p = sl * 256
                    e.tensor_scalar(dg[:, p:p + 128], K.ident[:],
                                    nml[:, sl * 4 + 2:sl * 4 + 3],
                                    None, ALU.mult)
                    return e.tensor_scalar(dg[:, p + 128:p + 256], K.ident[:],
                                           nml[:, sl * 4 + 3:sl * 4 + 4],
                                           None, ALU.mult)
                w = [nm]
                if idx >= 4:
                    w.append(tr_ev[(idx - 4, 5)])
                dgt = E_.op("gpsimd", dgf, waits=w, inc=K.sG)
                dg_toks[idx] = dgt

                def tf6(e, sl=sl):
                    for h2 in range(2):
                        for kk in range(3):
                            o = sl * 768
                            p = sl * 256
                            e.matmul(K.ptp[h2][0:128, kk * 128:kk * 128 + 128],
                                     pn[:, o + h2 * 384 + kk * 128:
                                        o + h2 * 384 + kk * 128 + 128],
                                     dg[:, p + h2 * 128:p + h2 * 128 + 128],
                                     start=True, stop=True)
                w = [dgt] + K.wr(("pbf", 0), ("pbf", 1))
                tt = E_.op("tensor", tf6, waits=w, inc=K.sT)

                def ef6(e, qtj=qtj):
                    r = None
                    for h2 in range(2):
                        for kk in range(3):
                            rr = qtj + kk
                            r = e.tensor_copy(
                                ptg[:, h2 * 3072 + rr * 512 + qtj * 128:
                                    h2 * 3072 + rr * 512 + qtj * 128 + 128],
                                K.ptp[h2][0:128, kk * 128:kk * 128 + 128])
                    return r
                te = E_.op("vector", ef6, waits=[tt], inc=K.sV)
                K.rd(("pbf", 0), te)
                K.rd(("pbf", 1), te)
                tr_ev[(idx, 5)] = te

            spi = c * 2 + sp
            bank = 4 + (spi % 2)

            def pvf(e, c=c, sp=sp, bank=bank):
                for r in range(6):
                    c0, c1 = PVRANGE[r]
                    tc = sp * 4 + r
                    e.matmul(K.ps[bank][64:128, c0:c1],
                             vb[:, tc * 1024 + (2 * c + 1) * 64:
                                tc * 1024 + (2 * c + 1) * 64 + 64],
                             ptg[:, 3072 + r * 512 + c0:3072 + r * 512 + c1],
                             start=(r == 0), stop=(r == 5),
                             tile_position=(0, 64))
                    e.matmul(K.ps[bank][0:64, c0:c1],
                             vb[:, tc * 1024 + (2 * c) * 64:
                                tc * 1024 + (2 * c) * 64 + 64],
                             ptg[:, r * 512 + c0:r * 512 + c1],
                             start=(r == 0), stop=(r == 5),
                             tile_position=(0, 0))
            w = [tr_ev[(c * 8 + sp * 4 + j, 5)] for j in range(4)]
            w += K.wr(("ps", bank))
            pv = E_.op("tensor", pvf, waits=w, inc=K.sT)

            def pve(e, c=c, sp=sp, bank=bank):
                e.tensor_copy(ao[:, c * 1024 + sp * 512:
                                 c * 1024 + sp * 512 + 512],
                              K.ps[bank][0:128, 0:512])
            pev = E_.op("vector", pve, waits=[pv], inc=K.sV)
            K.rd(("ps", bank), pev)
            pv_ev[spi] = pev

    ao_done = pv_ev[15]
    if DEBUG and ap == 0:
        ao_done = E_.dma("sync", K.DBG["dbg_ao"], ao[:, 0:8 * 1024],
                         waits=[ao_done])

    prev_dma = None
    for oc in range(8):
        def mf(e, oc=oc):
            for dch in range(8):
                for st in range(2):
                    e.matmul(K.ps[st][0:128, 0:512],
                             al["wo"][:, dch * 1024 + oc * 128:
                                      dch * 1024 + oc * 128 + 128],
                             ao[:, dch * 1024 + st * 512:
                                dch * 1024 + st * 512 + 512],
                             start=(dch == 0), stop=False)
            for st in range(2):
                e.matmul(K.ps[st][0:128, 0:512],
                         K.biasr[0:1, oc * 128:oc * 128 + 128],
                         K.ones1[0:1, 0:512], start=False, stop=True)
        w = [ao_done, xrl, al["load"]] + K.wr(("ps", 0), ("ps", 1))
        mm = E_.op("tensor", mf, waits=w, inc=K.sT)

        def vf(e, oc=oc):
            for st in range(2):
                e.tensor_add(stage[oc % 2][:, st * 512:st * 512 + 512],
                             K.ps[st][0:128, 0:512],
                             xr[:, oc * 1024 + st * 512:
                                oc * 1024 + st * 512 + 512])
        ev = E_.op("vector", vf, waits=[mm, prev_dma], inc=K.sV)
        K.rd(("ps", 0), ev)
        K.rd(("ps", 1), ev)
        prev_dma = E_.dma("sync", K.x1[oc * 128:(oc + 1) * 128, qs:qs + 1024],
                          stage[oc % 2][:, 0:1024], waits=[ev])
        if DEBUG:
            E_.dma("sync", K.DBG["dbg_x1"][oc * 128:(oc + 1) * 128,
                                           qs:qs + 1024],
                   stage[oc % 2][:, 0:1024], out_sem=K.sO)
    K.pop()


def ph_ffn(K, E_, src, dst, final, dbg=None):
    K.push()
    wd = K.a16(24 * 1024)
    hb = K.a16(24 * 1024)
    u4 = K.a16(8 * 1024)
    wg = K.a16(8 * 768)
    wu = K.a16(8 * 768)
    sg = K.a16(1024)
    xst = [K.a32(1024), K.a32(1024)]
    stage = [K.a32(1024), K.a32(1024)]

    wdl = None
    for kc in range(24):
        wdl = E_.dma("gpsimd", wd[:, kc * 1024:(kc + 1) * 1024],
                     K.DR["wdT"][kc * 128:(kc + 1) * 128, :])

    prev_down = None
    last_gm = None
    last_um = None
    for tq in range(4):
        q0 = tq * 1024
        lnd = emit_ln(K, E_, src, q0, 1024, u4, extra=[last_gm, last_um])
        hdone = []
        for hs in range(4):
            wgl = None
            for dch in range(8):
                wgl = E_.dma("gpsimd", wg[:, dch * 768:(dch + 1) * 768],
                             K.DR["wgT"][dch * 128:(dch + 1) * 128,
                                         hs * 768:(hs + 1) * 768],
                             waits=[last_gm] if dch == 0 else ())
            wul = None
            for dch in range(8):
                wul = E_.dma("gpsimd", wu[:, dch * 768:(dch + 1) * 768],
                             K.DR["wuT"][dch * 128:(dch + 1) * 128,
                                         hs * 768:(hs + 1) * 768],
                             waits=[last_um] if dch == 0 else ())
            for oc in range(6):
                def gmf(e, oc=oc):
                    for dch in range(8):
                        for st in range(2):
                            e.matmul(K.ps[st][0:128, 0:512],
                                     wg[:, dch * 768 + oc * 128:
                                        dch * 768 + oc * 128 + 128],
                                     u4[:, dch * 1024 + st * 512:
                                        dch * 1024 + st * 512 + 512],
                                     start=(dch == 0), stop=(dch == 7))
                gm = E_.op("tensor", gmf,
                           waits=[lnd, wgl] + K.wr(("ps", 0), ("ps", 1)),
                           inc=K.sT)
                last_gm = gm

                def umf(e, oc=oc):
                    for dch in range(8):
                        for st in range(2):
                            e.matmul(K.ps[2 + st][0:128, 0:512],
                                     wu[:, dch * 768 + oc * 128:
                                        dch * 768 + oc * 128 + 128],
                                     u4[:, dch * 1024 + st * 512:
                                        dch * 1024 + st * 512 + 512],
                                     start=(dch == 0), stop=(dch == 7))
                um = E_.op("tensor", umf,
                           waits=[lnd, wul] + K.wr(("ps", 2), ("ps", 3)),
                           inc=K.sT)
                last_um = um

                silu = E_.op("scalar", lambda e: [
                    e.activation(sg[:, 0:512], K.ps[0][0:128, 0:512],
                                 AF.Silu, scale=Q127),
                    e.activation(sg[:, 512:1024], K.ps[1][0:128, 0:512],
                                 AF.Silu, scale=Q127)],
                    waits=[gm, hdone[-1] if hdone else None], inc=K.sA)
                K.rd(("ps", 0), silu)
                K.rd(("ps", 1), silu)

                def hf(e, oc=oc, hs=hs):
                    blk = hs * 6 + oc
                    for st in range(2):
                        e.scalar_tensor_tensor(
                            hb[:, blk * 1024 + st * 512:
                               blk * 1024 + st * 512 + 512],
                            K.ps[2 + st][0:128, 0:512], Q127,
                            sg[:, st * 512:st * 512 + 512],
                            ALU.mult, ALU.mult)
                ht = E_.op("vector", hf, waits=[um, silu], inc=K.sV)
                K.rd(("ps", 2), ht)
                K.rd(("ps", 3), ht)
                hdone.append(ht)

        prev_dma = None
        ev_of = {}
        for oc in range(8):
            xl = E_.dma("sync", xst[oc % 2][:, 0:1024],
                        src[oc * 128:(oc + 1) * 128, q0:q0 + 1024],
                        waits=[ev_of.get(oc - 2, prev_down)])

            def dmf(e, oc=oc):
                for kc in range(24):
                    for st in range(2):
                        e.matmul(K.ps[4 + st][0:128, 0:512],
                                 wd[:, kc * 1024 + oc * 128:
                                    kc * 1024 + oc * 128 + 128],
                                 hb[:, kc * 1024 + st * 512:
                                    kc * 1024 + st * 512 + 512],
                                 start=(kc == 0), stop=(kc == 23))
            dm = E_.op("tensor", dmf, waits=[hdone[-1], wdl] +
                       K.wr(("ps", 4), ("ps", 5)), inc=K.sT)

            def ef(e, oc=oc):
                for st in range(2):
                    e.scalar_tensor_tensor(
                        stage[oc % 2][:, st * 512:st * 512 + 512],
                        K.ps[4 + st][0:128, 0:512], Q127,
                        xst[oc % 2][:, st * 512:st * 512 + 512],
                        ALU.mult, ALU.add)
            ev = E_.op("vector", ef, waits=[dm, xl, prev_dma], inc=K.sV)
            ev_of[oc] = ev
            K.rd(("ps", 4), ev)
            K.rd(("ps", 5), ev)
            osem = K.sO if final else None
            prev_dma = E_.dma("sync", dst[oc * 128:(oc + 1) * 128, q0:q0 + 1024],
                              stage[oc % 2][:, 0:1024], waits=[ev],
                              out_sem=osem)
            if dbg is not None:
                E_.dma("sync", K.DBG[dbg][oc * 128:(oc + 1) * 128,
                                          q0:q0 + 1024],
                       stage[oc % 2][:, 0:1024], out_sem=K.sO)
        prev_down = prev_dma
    K.pop()
    return prev_down


def ph_mamba1(K, E_):
    K.push()
    top = K.mamba_top()
    acs = top["acs"]
    oh = top["oh"]
    gA = K.a16(18 * 1028)
    GBW = 1028
    dtg = K.a32(1024)
    u4 = K.a16(8 * 1024)
    winb = [K.a16(8 * 128), K.a16(8 * 128)]
    xc2 = K.a16(18 * 512)
    xdtT = K.a16(16 * 256)
    xdd = K.a16(2 * 128)
    btok = K.a16(2 * 128)
    cbt = K.a16(2 * 256)
    mt = K.a16(2 * 256)
    ebc = K.a16(256)
    cexp = K.a16(256)
    carryb = K.a16(128)
    dxs = K.a16(256)
    tail3 = K.a16(18 * 16)
    ysl = K.a32(8 * 256)
    carry = K.a32(DIN)
    dtf = K.a32(256)
    dtm = K.a32(256)
    eb2 = K.a32(512)
    dg4 = K.a16(2 * 512)
    decT = K.a32(32)
    dtTb = K.a32(32)
    haccum = K.a32(8 * 54)
    u3 = K.a16(8 * 16)
    hs54 = K.a32(54)

    sm = K.smalls_sb

    ohld = E_.dma("gpsimd", oh[0:16, :], K.DR["oh"])

    x3 = u3
    x3ld = None
    for dch in range(8):
        x3ld = E_.dma("gpsimd", x3[:, dch * 16:dch * 16 + 3],
                      K.x2[dch * 128:(dch + 1) * 128, T - 3:T])

    def mstat(e):
        for dch in range(8):
            e.matmul(K.ps[0][:, 0:3], K.onesb[:], x3[:, dch * 16:dch * 16 + 3],
                     start=(dch == 0), stop=(dch == 7))
    ms = E_.op("tensor", mstat, waits=[x3ld] + K.wr(("ps", 0)), inc=K.sT)

    def msq(e):
        r = None
        for dch in range(8):
            r = e.activation(u3[:, dch * 16 + 8:dch * 16 + 11],
                             x3[:, dch * 16:dch * 16 + 3], AF.Square)
        return r
    ms2 = E_.op("scalar", msq, waits=[x3ld], inc=K.sA)

    def mstat2(e):
        for dch in range(8):
            e.matmul(K.ps[1][:, 0:3], K.onesb[:],
                     u3[:, dch * 16 + 8:dch * 16 + 11],
                     start=(dch == 0), stop=(dch == 7))
    ms3 = E_.op("tensor", mstat2, waits=[ms2] + K.wr(("ps", 1)), inc=K.sT)

    def mv1(e):
        e.tensor_scalar(decT[:, 0:3], K.ps[0][:, 0:3], 1.0 / 1024, None,
                        ALU.mult)
        e.tensor_scalar(dtTb[:, 0:3], K.ps[1][:, 0:3], 1.0 / 1024, None,
                        ALU.mult)
    mv = E_.op("vector", mv1, waits=[ms, ms3], inc=K.sV)
    K.rd(("ps", 0), mv)
    K.rd(("ps", 1), mv)
    mv2 = E_.op("scalar", lambda e: e.activation(
        dtf[:, 0:3], decT[:, 0:3], AF.Square), waits=[mv], inc=K.sA)
    mv3 = E_.op("vector", lambda e: e.tensor_sub(
        dtTb[:, 0:3], dtTb[:, 0:3], dtf[:, 0:3]), waits=[mv2], inc=K.sV)
    mv4 = E_.op("scalar", lambda e: e.activation(
        dtTb[:, 0:3], dtTb[:, 0:3], AF.Sqrt, bias=EPS), waits=[mv3],
        inc=K.sA)

    def mv5a(e):
        for dch in range(8):
            e.tensor_sub(eb2[:, dch * 16:dch * 16 + 3],
                         x3[:, dch * 16:dch * 16 + 3], decT[:, 0:3])
    sub_d = E_.op("gpsimd", mv5a, waits=[mv, x3ld], inc=K.sG)

    def mv5(e):
        e.reciprocal(dtTb[:, 0:3], dtTb[:, 0:3])
        K.rtrip(e)
        for dch in range(8):
            e.tensor_mul(u3[:, dch * 16 + 4:dch * 16 + 7],
                         eb2[:, dch * 16:dch * 16 + 3], dtTb[:, 0:3])
    u3d = E_.op("vector", mv5, waits=[mv4, sub_d], inc=K.sV)

    hprev = None
    for oc in range(18):
        wl = None
        for dch in range(8):
            wl = E_.dma("gpsimd",
                        winb[oc % 2][:, dch * 128:dch * 128 + 128],
                        K.DR["winT"][dch * 128:(dch + 1) * 128,
                                     DIN + oc * 128:DIN + oc * 128 + 128],
                        waits=[hprev] if dch == 0 else ())

        def hmf(e, oc=oc):
            for dch in range(8):
                e.matmul(K.ps[2][0:128, 0:3],
                         winb[oc % 2][:, dch * 128:dch * 128 + 128],
                         u3[:, dch * 16 + 4:dch * 16 + 7],
                         start=(dch == 0), stop=(dch == 7))
        hm = E_.op("tensor", hmf, waits=[wl, u3d] + K.wr(("ps", 2)),
                   inc=K.sT)
        hprev = E_.op("vector", lambda e, oc=oc: e.tensor_copy(
            hs54[:, oc * 3:oc * 3 + 3], K.ps[2][0:128, 0:3]),
            waits=[hm], inc=K.sV)
        K.rd(("ps", 2), hprev)
    hdma = E_.dma("sync", K.ch_in[:, :], hs54[:, 0:54], waits=[hprev])

    def ccf(e):
        return e.collective_compute(
            "AllGather", ALU.bypass, replica_groups=[list(range(8))],
            ins=[K.ch_in[:, :].opt()], outs=[K.ch_out[:, :].opt()])
    cc1 = E_.op("gpsimd", ccf, waits=[hdma], inc=K.sP)
    hl = None
    for j in range(8):
        hl = E_.dma("sync", haccum[:, j * 54:(j + 1) * 54],
                    K.ch_out[j * 128:(j + 1) * 128, :], waits=[cc1])

    def hmask(e):
        e.tensor_scalar(K.halo3[:], haccum[:, 0:54], K.rdmask_sb[:, 0:1],
                        None, ALU.mult)
        for j in range(1, 8):
            K.rtrip(e)
            e.scalar_tensor_tensor(K.halo3[:], haccum[:, j * 54:(j + 1) * 54],
                                   K.rdmask_sb[:, j:j + 1], K.halo3[:],
                                   ALU.mult, ALU.add)
    halo_done = E_.op("vector", hmask, waits=[hl], inc=K.sV)

    cz = E_.op("vector", lambda e: e.memset(carry[:, :], 0.0), inc=K.sV)

    mm_of_oc = {}
    group_done = {}
    chunk_state = {}

    def emit_inproj(g, gb_free, winb_free=None):
        lnd = emit_ln(K, E_, K.x2, g * 1024, 1024, u4,
                      extra=[mm_of_oc.get(("last", 0))])
        prev_ev = None
        for oc in range(19):
            ow = 128 if oc < 18 else 16
            wl = None
            for dch in range(8):
                wl = E_.dma("gpsimd",
                            winb[oc % 2][:, dch * 128:dch * 128 + ow],
                            K.DR["winT"][dch * 128:(dch + 1) * 128,
                                         DIN + oc * 128:DIN + oc * 128 + ow],
                            waits=[mm_of_oc.get((g, oc - 2),
                                                mm_of_oc.get(("last", 0))),
                                   winb_free if oc < 2 else None]
                            if dch == 0 else ())

            def mf(e, oc=oc, ow=ow):
                op_ = min(ow, 128)
                for dch in range(8):
                    for st in range(2):
                        e.matmul(K.ps[st][0:op_, 0:512],
                                 winb[oc % 2][:, dch * 128:dch * 128 + ow],
                                 u4[:, dch * 1024 + st * 512:
                                    dch * 1024 + st * 512 + 512],
                                 start=(dch == 0), stop=(dch == 7))
            w = [lnd, wl, prev_ev] + K.wr(("ps", 0), ("ps", 1))
            mm = E_.op("tensor", mf, waits=w, inc=K.sT)
            mm_of_oc[(g, oc)] = mm
            mm_of_oc[("last", 0)] = mm

            def vf(e, oc=oc, ow=ow):
                for st in range(2):
                    if ow == 128:
                        e.tensor_copy(gA[:, oc * GBW + 4 + st * 512:
                                         oc * GBW + 4 + st * 512 + 512],
                                      K.ps[st][0:128, 0:512])
                    else:
                        e.tensor_copy(dtg[0:16, st * 512:st * 512 + 512],
                                      K.ps[st][0:16, 0:512])
            prev_ev = E_.op("vector", vf, waits=[mm, gb_free], inc=K.sV)
            K.rd(("ps", 0), prev_ev)
            K.rd(("ps", 1), prev_ev)
        return prev_ev

    W_ETM = K.etm

    def emit_chunk(j):
        g = j // 4
        jr = j % 4
        gdone = group_done[g]

        po = jr % 2
        if po == 0:
            if jr == 0:
                def ctxf(e, j=j):
                    e.tensor_copy(
                        gA[:, 0:18 * GBW].rearrange("p (a k) -> p a k", a=18)
                        [:, :, 1:4],
                        (K.halo3 if j == 0 else tail3)
                        [:, 0:18 * (3 if j == 0 else 16)]
                        .rearrange("p (a k) -> p a k", a=18)[:, :, 0:3])
                ctx = E_.op("vector", ctxf, waits=[gdone,
                                                   halo_done if j == 0
                                                   else None], inc=K.sV)
            else:
                ctx = None
            conv_sil = []
            dg_mm = {}
            for cch in range(18):
                slot = cch % 2

                def dgf(e, cch=cch, slot=slot):
                    for k in range(4):
                        e.tensor_scalar(
                            dg4[:, slot * 512 + k * 128:
                                slot * 512 + k * 128 + 128],
                            K.ident[:],
                            K.convw_sb[:, cch * 4 + k:cch * 4 + k + 1],
                            None, ALU.mult)
                dg = E_.op("vector", dgf,
                           waits=[ctx, dg_mm.get(cch - 2),
                                  chunk_state.get((j - 1, "ye15"))],
                           inc=K.sV)

                bank = 4 + (cch % 2)

                def cmf(e, cch=cch, slot=slot, jr=jr, bank=bank):
                    bb = cch * GBW
                    for k in range(4):
                        e.matmul(K.ps[bank][0:128, 0:512],
                                 dg4[:, slot * 512 + k * 128:
                                     slot * 512 + k * 128 + 128],
                                 gA[:, bb + 1 + jr * 256 + k:
                                    bb + 1 + jr * 256 + k + 512],
                                 start=(k == 0), stop=(k == 3))
                cm = E_.op("tensor", cmf,
                           waits=[dg, gdone] + K.wr(("ps", bank)), inc=K.sT)
                dg_mm[cch] = cm

                si = E_.op("scalar", lambda e, cch=cch, bank=bank:
                           e.activation(xc2[:, cch * 512:(cch + 1) * 512],
                                        K.ps[bank][0:128, 0:512], AF.Silu,
                                        bias=K.convb_sb[:, cch:cch + 1]),
                           waits=[cm, chunk_state.get((j - 1, "ye15"))],
                           inc=K.sA)
                K.rd(("ps", bank), si)
                conv_sil.append(si)
            chunk_state[(j, "convdone")] = conv_sil[-1]
        conv_done = chunk_state[(j - po, "convdone")]
        xc = xc2[:, po * 256:]

        def t3f(e, jr=jr):
            e.tensor_copy(
                tail3[:, 0:18 * 16].rearrange("p (a k) -> p a k", a=18)
                [:, :, 0:3],
                gA[:, 0:18 * GBW].rearrange("p (a k) -> p a k", a=18)
                [:, :, 4 + jr * 256 + 253:4 + jr * 256 + 256])
        t3 = E_.op("vector", t3f, waits=[gdone, conv_done], inc=K.sV)

        def dtc(e, jr=jr):
            e.tensor_scalar(dtf[0:16, :], dtg[0:16, jr * 256:(jr + 1) * 256],
                            sm[0:16, 0:1], None, ALU.add)
        d1 = E_.op("vector", dtc,
                   waits=[gdone, chunk_state.get((j - 1, "l2")),
                          chunk_state.get((j - 1, "dtT_done"))], inc=K.sV)
        d2 = E_.op("scalar", lambda e: [
            e.activation(dtm[0:16, 0:256], dtf[0:16, 0:256], AF.Abs),
            e.activation(dtm[0:16, 0:256], dtm[0:16, 0:256], AF.Exp,
                         scale=-1.0)],
            waits=[d1], inc=K.sA)
        d3 = E_.op("scalar", lambda e: e.activation(
            dtm[0:16, 0:256], dtm[0:16, 0:256], AF.Ln, bias=1.0),
            waits=[d2], inc=K.sA)

        def dtc2(e):
            e.tensor_relu(dtf[0:16, :], dtf[0:16, :])
            e.tensor_add(dtf[0:16, :], dtf[0:16, :], dtm[0:16, :])
            e.tensor_scalar(dtm[0:16, :], dtf[0:16, :], sm[0:16, 3:4],
                            None, ALU.mult)
        d4 = E_.op("vector", dtc2, waits=[d3], inc=K.sV)

        def scanf(e, j=j):
            e.tensor_tensor_scan(acs[0:16, j * 256:(j + 1) * 256],
                                 dtm[0:16, :], K.zeros16[0:16, :], 0.0,
                                 ALU.add, ALU.add)
        d5 = E_.op("vector", scanf, waits=[d4], inc=K.sV)
        d5b = E_.op("gpsimd", lambda e, j=j: e.tensor_copy(
            K.lasts[0:16, j:j + 1],
            acs[0:16, j * 256 + 255:j * 256 + 256]), waits=[d5], inc=K.sG)

        acsT_toks = []
        for sh in range(2):
            tt = E_.op("tensor", lambda e, j=j, sh=sh: e.transpose(
                K.ps[2][0:128, 0:16],
                acs[0:16, j * 256 + sh * 128:j * 256 + sh * 128 + 128],
                K.identf[0:16, 0:16]),
                waits=[d5] + K.wr(("ps", 2)), inc=K.sT)
            ev = E_.op("vector", lambda e, j=j, sh=sh: e.tensor_copy(
                K.acsT_g[:, j * 32 + sh * 16:j * 32 + sh * 16 + 16],
                K.ps[2][0:128, 0:16]), waits=[tt], inc=K.sV)
            K.rd(("ps", 2), ev)
            acsT_toks.append(ev)
        acsT_done = acsT_toks[-1]
        dtT_toks = []
        for sh in range(2):
            tt = E_.op("tensor", lambda e, sh=sh: e.transpose(
                K.ps[2][0:128, 0:16], dtf[0:16, sh * 128:sh * 128 + 128],
                K.identf[0:16, 0:16]),
                waits=[d4] + K.wr(("ps", 2)), inc=K.sT)
            ev = E_.op("vector", lambda e, sh=sh: e.tensor_copy(
                dtTb[:, sh * 16:sh * 16 + 16], K.ps[2][0:128, 0:16]),
                waits=[tt, chunk_state.get((j - 1, "xdt31"))], inc=K.sV)
            K.rd(("ps", 2), ev)
            dtT_toks.append(ev)
        dtT_done = dtT_toks[-1]

        l1 = E_.op("vector", lambda e, j=j: e.tensor_scalar(
            dtm[0:16, 0:32], K.e16_sb[0:16, 0:32], K.lasts[0:16, j:j + 1],
            None, ALU.mult), waits=[d5b], inc=K.sV)
        l2 = E_.op("tensor", lambda e: e.matmul(
            K.ps[3][0:128, 0:32], K.ones16f[0:16, :], dtm[0:16, 0:32],
            start=True, stop=True), waits=[l1] + K.wr(("ps", 3)), inc=K.sT)
        l3 = E_.op("vector", lambda e, j=j: e.tensor_sub(
            decT[:, 0:32], K.ps[3][0:128, 0:32],
            K.acsT_g[:, j * 32:j * 32 + 32]),
            waits=[l2, acsT_done, chunk_state.get((j - 1, "xdd15"))],
            inc=K.sV)

        def l4f(e, j=j):
            e.activation(decT[:, 0:32], decT[:, 0:32], AF.Exp)
            return e.activation(K.cdecb[:, j * 16:(j + 1) * 16],
                                K.ps[3][0:128, 0:16], AF.Exp)
        l4 = E_.op("scalar", l4f, waits=[l3], inc=K.sA)
        K.rd(("ps", 3), l4)

        xt_ev = {}
        for h in range(16):
            for sh in range(2):
                i = h * 2 + sh
                bank = i % 2

                def tf(e, h=h, sh=sh, bank=bank):
                    e.transpose(K.pbf[bank][0:128, 0:128],
                                xc[:, h * 512 + sh * 128:
                                   h * 512 + sh * 128 + 128],
                                K.ident[:])
                tt = E_.op("tensor", tf,
                           waits=[conv_done] + K.wr(("pbf", bank)), inc=K.sT)

                def ef(e, h=h, sh=sh, bank=bank):
                    e.tensor_scalar(
                        xdtT[:, h * 256 + sh * 128:h * 256 + sh * 128 + 128],
                        K.pbf[bank][0:128, 0:128],
                        dtTb[:, sh * 16 + h:sh * 16 + h + 1], None, ALU.mult)
                ev = E_.op("vector", ef, waits=[tt, dtT_done], inc=K.sV)
                K.rd(("pbf", bank), ev)
                xt_ev[i] = ev
        xdt_done = xt_ev[31]

        bt_toks = []
        for sh in range(2):
            bank = sh % 2
            tt = E_.op("tensor", lambda e, sh=sh, bank=bank: e.transpose(
                K.pbf[bank][0:128, 128:256],
                xc[:, 16 * 512 + sh * 128:16 * 512 + sh * 128 + 128],
                K.ident[:]),
                waits=[conv_done] + K.wr(("pbf", bank)), inc=K.sT)
            ev = E_.op("vector", lambda e, sh=sh, bank=bank: e.tensor_copy(
                btok[:, sh * 128:sh * 128 + 128],
                K.pbf[bank][0:128, 128:256]),
                waits=[tt, chunk_state.get((j - 1, "stm15"))], inc=K.sV)
            K.rd(("pbf", bank), ev)
            bt_toks.append(ev)
        btok_done = bt_toks[-1]

        E_.dma("sync", K.cfms[:, j * 256:(j + 1) * 256],
               xc[:, 17 * 512:17 * 512 + 256], waits=[conv_done])

        cb_toks = []
        for sh in range(2):
            mmc = E_.op("tensor", lambda e, sh=sh: e.matmul(
                K.ps[sh][0:128, 0:256],
                xc[:, 16 * 512 + sh * 128:16 * 512 + sh * 128 + 128],
                xc[:, 17 * 512:17 * 512 + 256], start=True, stop=True),
                waits=[conv_done] + K.wr(("ps", sh)), inc=K.sT)
            ev = E_.op("vector", lambda e, sh=sh: e.tensor_copy(
                cbt[:, sh * 256:(sh + 1) * 256], K.ps[sh][0:128, 0:256]),
                waits=[mmc, chunk_state.get((j - 1, "mt15"))], inc=K.sV)
            K.rd(("ps", sh), ev)
            cb_toks.append(ev)
        cbt_done = cb_toks[-1]

        prev = {}
        for kk2 in ("ym", "cu", "ye", "stm", "mt", "cexp"):
            prev[(kk2, -1)] = chunk_state.get((j - 1, kk2 + "15"))
        for h in range(16):
            ab = 2 + (h % 2)
            yb = 4 + (h % 2)
            stb = h % 2

            bca = E_.op("tensor", lambda e, h=h, j=j, ab=ab: e.matmul(
                K.ps[ab][0:128, 0:256],
                oh[0:16, h * 128:(h + 1) * 128],
                acs[0:16, j * 256:(j + 1) * 256],
                start=True, stop=True),
                waits=[d5, ohld] + K.wr(("ps", ab)), inc=K.sT)

            def ebf(e, h=h, j=j, ab=ab):
                for sh in range(2):
                    e.scalar_tensor_tensor(
                        eb2[:, sh * 256:(sh + 1) * 256],
                        K.ps[ab][0:128, 0:256],
                        K.acsT_g[:, j * 32 + sh * 16 + h:
                                 j * 32 + sh * 16 + h + 1],
                        W_ETM[:, sh * 256:(sh + 1) * 256],
                        ALU.subtract, ALU.add)
            eb = E_.op("vector", ebf,
                       waits=[bca, acsT_done,
                              prev.get(("mt", h - 1), prev[("mt", -1)])],
                       inc=K.sV)

            def exf(e, ab=ab):
                e.activation(ebc[:, 0:256], K.ps[ab][0:128, 0:256], AF.Exp)
                return e.activation(eb2[:, 0:512], eb2[:, 0:512], AF.Exp)
            ex = E_.op("scalar", exf,
                       waits=[eb, prev.get(("cexp", h - 1),
                                           prev[("cexp", -1)])], inc=K.sA)
            K.rd(("ps", ab), ex)

            def mtf(e, h=h):
                for sh in range(2):
                    e.tensor_mul(mt[:, sh * 256:(sh + 1) * 256],
                                 eb2[:, sh * 256:(sh + 1) * 256],
                                 cbt[:, sh * 256:(sh + 1) * 256])
                e.tensor_mul(cexp[:, 0:256], xc[:, 17 * 512:17 * 512 + 256],
                             ebc[:, 0:256])
            mtt = E_.op("vector", mtf,
                        waits=[ex, cbt_done,
                               prev.get(("ym", h - 1), prev[("ym", -1)])],
                        inc=K.sV)
            prev[("mt", h)] = mtt
            prev[("cexp", h)] = mtt

            def xdc(e, h=h):
                for sh in range(2):
                    e.tensor_scalar(xdd[:, sh * 128:(sh + 1) * 128],
                                    xdtT[:, h * 256 + sh * 128:
                                         h * 256 + sh * 128 + 128],
                                    decT[:, sh * 16 + h:sh * 16 + h + 1],
                                    None, ALU.mult)
                return e.tensor_copy(carryb[:, 0:128],
                                     carry[:, h * 128:(h + 1) * 128])
            xdt2 = E_.op("vector", xdc,
                         waits=[xdt_done, l4, cz,
                                prev.get(("cu", h - 1), prev[("cu", -1)]),
                                prev.get(("stm", h - 1), prev[("stm", -1)])],
                         inc=K.sV)

            smy = E_.op("tensor", lambda e, h=h, stb=stb, yb=yb: [
                e.matmul(K.ps[stb][0:128, 0:128], btok[:, 0:128],
                         xdd[:, 0:128], start=True, stop=False),
                e.matmul(K.ps[stb][0:128, 0:128], btok[:, 128:256],
                         xdd[:, 128:256], start=False, stop=True),
                e.matmul(K.ps[yb][0:128, 0:256], carryb[:, 0:128],
                         cexp[:, 0:256], start=True, stop=False),
                e.matmul(K.ps[yb][0:128, 0:256],
                         xdtT[:, h * 256:h * 256 + 128],
                         mt[:, 0:256], start=False, stop=False),
                e.matmul(K.ps[yb][0:128, 0:256],
                         xdtT[:, h * 256 + 128:h * 256 + 256],
                         mt[:, 256:512], start=False, stop=True)],
                waits=[btok_done, xdt2, mtt] +
                K.wr(("ps", stb), ("ps", yb)), inc=K.sT)
            prev[("stm", h)] = smy
            prev[("ym", h)] = smy

            dx = E_.op("vector", lambda e, h=h: e.tensor_scalar(
                dxs[:, 0:256], xc[:, h * 512:h * 512 + 256],
                K.dcols[:, h:h + 1], None, ALU.mult),
                waits=[conv_done,
                       prev.get(("ye", h - 1), prev[("ye", -1)])], inc=K.sV)

            def yecu(e, h=h, stb=stb, yb=yb, j=j):
                e.scalar_tensor_tensor(
                    ysl[:, (h % 8) * 256:(h % 8) * 256 + 256],
                    K.ps[yb][0:128, 0:256], 1.0, dxs[:, 0:256],
                    ALU.mult, ALU.add)
                return e.scalar_tensor_tensor(
                    carry[:, h * 128:(h + 1) * 128],
                    carry[:, h * 128:(h + 1) * 128],
                    K.cdecb[:, j * 16 + h:j * 16 + h + 1],
                    K.ps[stb][0:128, 0:128], ALU.mult, ALU.add)
            ye = E_.op("vector", yecu,
                       waits=[smy, dx, l4,
                              chunk_state.get((j, "ysl0")) if h >= 8 else
                              chunk_state.get((j - 1, "ysl1"))],
                       inc=K.sV)
            K.rd(("ps", yb), ye)
            K.rd(("ps", stb), ye)
            prev[("ye", h)] = ye
            prev[("cu", h)] = ye
            if h == 7:
                chunk_state[(j, "ysl0")] = E_.dma(
                    "sync",
                    K.ysp[:, :].rearrange("p (a t) -> p a t", a=16)
                    [:, 0:8, j * 256:(j + 1) * 256],
                    ysl[:, 0:2048].rearrange("p (a t) -> p a t", a=8),
                    waits=[ye])
            if h == 15:
                chunk_state[(j, "ysl1")] = E_.dma(
                    "sync",
                    K.ysp[:, :].rearrange("p (a t) -> p a t", a=16)
                    [:, 8:16, j * 256:(j + 1) * 256],
                    ysl[:, 0:2048].rearrange("p (a t) -> p a t", a=8),
                    waits=[ye])

        for kk2 in ("ym", "cu", "ye", "stm", "mt", "cexp"):
            chunk_state[(j, kk2 + "15")] = prev[(kk2, 15)]
        chunk_state[(j, "carry")] = prev[("cu", 15)]
        chunk_state[(j, "free")] = t3
        chunk_state[(j, "l2")] = l2
        chunk_state[(j, "dtT_done")] = dtT_done
        chunk_state[(j, "xdt31")] = xt_ev[31]
        chunk_state[(j, "xdd15")] = prev[("stm", 15)]
        chunk_state[(j, "stm15")] = prev[("stm", 15)]
        chunk_state[(j, "mt15")] = prev[("mt", 15)]

    for g in range(4):
        gfree = chunk_state.get((g * 4 - 1, "free"))
        gd = emit_inproj(g, gfree, winb_free=hprev if g == 0 else None)
        group_done[g] = gd
        for j in range(g * 4, g * 4 + 4):
            emit_chunk(j)

    st_dma = E_.dma("sync", K.cc_in[:, :], carry[:, :],
                    waits=[chunk_state[(15, "carry")]])

    def ccf2(e):
        return e.collective_compute(
            "AllGather", ALU.bypass, replica_groups=[list(range(8))],
            ins=[K.cc_in[:, :].opt()], outs=[K.cc_out[:, :].opt()])
    cc2 = E_.op("gpsimd", ccf2, waits=[st_dma], inc=K.sP)

    sbb = [u4.bitcast(F32)[:, 0:2048], u4.bitcast(F32)[:, 2048:4096]]
    acc = None
    for jj in range(8):
        ldt = E_.dma("sync", sbb[jj % 2][:, 0:2048],
                     K.cc_out[jj * 128:(jj + 1) * 128, :],
                     waits=[cc2, acc, mm_of_oc.get(("last", 0))])
        if jj == 0:
            acc = E_.op("vector", lambda e: e.tensor_scalar(
                top["sin"][:, :], sbb[0][:, 0:2048], K.rdmask_sb[:, 0:1],
                None, ALU.mult), waits=[ldt], inc=K.sV)
        else:
            acc = E_.op("vector", lambda e, jj=jj: e.scalar_tensor_tensor(
                top["sin"][:, :], sbb[jj % 2][:, 0:2048],
                K.rdmask_sb[:, jj:jj + 1], top["sin"][:, :],
                ALU.mult, ALU.add), waits=[ldt], inc=K.sV)
    sin_done = E_.op("vector", lambda e: e.tensor_copy(
        top["sinb"][:, :], top["sin"][:, :]), waits=[acc], inc=K.sV)

    pj1 = E_.op("vector", lambda e: e.tensor_tensor_scan(
        K.lasts[0:16, 20:36], K.lasts[0:16, 0:16],
        K.zeros16[0:16, 0:16], 0.0, ALU.add, ALU.add),
        waits=[chunk_state[(15, "carry")]], inc=K.sV)

    def pjshift(e):
        e.memset(K.lasts[0:16, 17:18], 0.0)
        e.tensor_copy(K.lasts[0:16, 18:18 + 15], K.lasts[0:16, 20:35])
    pj2 = E_.op("gpsimd", pjshift, waits=[pj1], inc=K.sG)
    pj3 = E_.op("scalar", lambda e: e.activation(
        K.pj_sb[0:16, 0:16], K.lasts[0:16, 17:33], AF.Exp),
        waits=[pj2], inc=K.sA)

    pjb_last = None
    for j in range(16):
        pd = E_.op("vector", lambda e, j=j: e.tensor_scalar(
            dtm[0:16, 0:16], K.e16_sb[0:16, 0:16], K.pj_sb[0:16, j:j + 1],
            None, ALU.mult), waits=[pj3, pjb_last], inc=K.sV)
        pm = E_.op("tensor", lambda e: e.matmul(
            K.ps[2][0:128, 0:16], K.ones16f[0:16, :], dtm[0:16, 0:16],
            start=True, stop=True), waits=[pd] + K.wr(("ps", 2)), inc=K.sT)
        pe = E_.op("vector", lambda e, j=j: e.tensor_copy(
            K.pj128[:, j * 16:(j + 1) * 16], K.ps[2][0:128, 0:16]),
            waits=[pm], inc=K.sV)
        K.rd(("ps", 2), pe)
        pjb_last = pe

    K.pop()
    return dict(sin=sin_done, pj=pjb_last,
                ydone=chunk_state[(15, "ysl1")])


def ph_mamba2(K, E_, m1):
    K.push()
    top = K.mamba_top()
    acs = top["acs"]
    oh = top["oh"]
    sz = K.a16(16 * 1024)
    wout = K.a16(16 * 1024)
    u4 = K.a16(8 * 1024)
    wzb = [K.a16(8 * 128), K.a16(8 * 128)]
    ybuf = K.a32(16 * 256)
    cfb = K.a16(256)
    ebc2 = K.a16(3 * 256)
    g2b = K.a16(16 * 256)
    xst = [K.a32(1024), K.a32(1024)]
    stage = [K.a32(1024), K.a32(1024)]
    rbc = K.a32(256)

    wol = None
    for kc in range(16):
        wol = E_.dma("gpsimd", wout[:, kc * 1024:(kc + 1) * 1024],
                     K.DR["woutT"][kc * 128:(kc + 1) * 128, :])

    zmm_of = {}
    prev_out_dma = None
    for tg in range(4):
        q0 = tg * 1024
        lnd = emit_ln(K, E_, K.x2, q0, 1024, u4,
                      extra=[zmm_of.get((tg - 1, 15))])

        zprev = None
        for zc in range(16):
            wl = None
            for dch in range(8):
                wl = E_.dma("gpsimd",
                            wzb[zc % 2][:, dch * 128:(dch + 1) * 128],
                            K.DR["winT"][dch * 128:(dch + 1) * 128,
                                         zc * 128:(zc + 1) * 128],
                            waits=[zmm_of.get((tg, zc - 2),
                                              zmm_of.get((tg - 1, 14 + zc)))]
                            if dch == 0 else ())

            def zmf(e, zc=zc):
                for dch in range(8):
                    for st in range(2):
                        e.matmul(K.ps[st][0:128, 0:512],
                                 wzb[zc % 2][:, dch * 128:(dch + 1) * 128],
                                 u4[:, dch * 1024 + st * 512:
                                    dch * 1024 + st * 512 + 512],
                                 start=(dch == 0), stop=(dch == 7))
            zm = E_.op("tensor", zmf,
                       waits=[lnd, wl, prev_out_dma] +
                       K.wr(("ps", 0), ("ps", 1)), inc=K.sT)
            zmm_of[(tg, zc)] = zm
            zprev = E_.op("scalar", lambda e, zc=zc: [
                e.activation(sz[:, zc * 1024:zc * 1024 + 512],
                             K.ps[0][0:128, 0:512], AF.Silu),
                e.activation(sz[:, zc * 1024 + 512:(zc + 1) * 1024],
                             K.ps[1][0:128, 0:512], AF.Silu)],
                waits=[zm, prev_out_dma], inc=K.sA)
            K.rd(("ps", 0), zprev)
            K.rd(("ps", 1), zprev)
        z_done = zprev

        gn_done = []
        prev_ctok = [None]
        for jj in range(4):
            j = tg * 4 + jj
            yl = E_.dma("sync",
                        ybuf[:, 0:4096].rearrange("p (a t) -> p a t", a=16),
                        K.ysp[:, :].rearrange("p (a t) -> p a t", a=16)
                        [:, :, j * 256:(j + 1) * 256],
                        waits=[m1["ydone"],
                               gn_done[-1] if gn_done else None])
            cl = E_.dma("sync", cfb[:, 0:256],
                        K.cfms[:, j * 256:(j + 1) * 256],
                        waits=[prev_ctok[0]])
            ctoks = []
            for h in range(16):
                ab = h % 4
                yb2 = 4 + (h % 2)
                eo = (h % 3) * 256
                bca = E_.op("tensor", lambda e, h=h, j=j, ab=ab: e.matmul(
                    K.ps[ab][0:128, 0:256],
                    oh[0:16, h * 128:(h + 1) * 128],
                    acs[0:16, j * 256:(j + 1) * 256],
                    start=True, stop=True),
                    waits=[m1["sin"], m1["pj"], yl, cl] + K.wr(("ps", ab)),
                    inc=K.sT)
                exb = E_.op("scalar", lambda e, ab=ab, eo=eo: e.activation(
                    ebc2[:, eo:eo + 256], K.ps[ab][0:128, 0:256], AF.Exp),
                    waits=[bca, ctoks[-3] if len(ctoks) >= 3 else None],
                    inc=K.sA)
                K.rd(("ps", ab), exb)
                cx = E_.op("vector", lambda e, eo=eo: e.tensor_mul(
                    ebc2[:, eo:eo + 256], cfb[:, 0:256], ebc2[:, eo:eo + 256]),
                    waits=[exb, cl], inc=K.sV)
                ym = E_.op("tensor", lambda e, h=h, yb2=yb2, eo=eo: e.matmul(
                    K.ps[yb2][0:128, 0:256],
                    top["sinb"][:, h * 128:(h + 1) * 128],
                    ebc2[:, eo:eo + 256],
                    start=True, stop=True),
                    waits=[cx, m1["sin"]] + K.wr(("ps", yb2)), inc=K.sT)
                ce = E_.op("vector",
                           lambda e, h=h, j=j, yb2=yb2:
                           e.scalar_tensor_tensor(
                               ybuf[:, h * 256:(h + 1) * 256],
                               K.ps[yb2][0:128, 0:256],
                               K.pj128[:, j * 16 + h:j * 16 + h + 1],
                               ybuf[:, h * 256:(h + 1) * 256],
                               ALU.mult, ALU.add),
                           waits=[ym, yl], inc=K.sV)
                K.rd(("ps", yb2), ce)
                ctoks.append(ce)
            corr_done = ctoks[-1]
            prev_ctok[0] = corr_done

            def gf(e, jj=jj):
                e.tensor_mul(
                    ybuf[:, 0:4096].rearrange("p (a t) -> p a t", a=16),
                    ybuf[:, 0:4096].rearrange("p (a t) -> p a t", a=16),
                    sz[:, 0:16 * 1024].rearrange("p (a t) -> p a t", a=16)
                    [:, :, jj * 256:(jj + 1) * 256])
            gt = E_.op("vector", gf, waits=[corr_done, z_done], inc=K.sV)

            sq = E_.op("scalar", lambda e: e.activation(
                g2b[:, 0:4096], ybuf[:, 0:4096], AF.Square), waits=[gt],
                inc=K.sA)

            def rmf(e):
                for blk in range(16):
                    e.matmul(K.ps[2][0:128, 0:256], K.onesb[:],
                             g2b[:, blk * 256:(blk + 1) * 256],
                             start=(blk == 0), stop=(blk == 15))
            rm = E_.op("tensor", rmf, waits=[sq] + K.wr(("ps", 2)), inc=K.sT)
            r2 = E_.op("scalar", lambda e: e.activation(
                rbc[:, 0:256], K.ps[2][0:128, 0:256], AF.Sqrt,
                scale=1.0 / DIN, bias=EPS), waits=[rm], inc=K.sA)
            K.rd(("ps", 2), r2)

            def gnf(e, jj=jj):
                e.reciprocal(rbc[:, 0:256], rbc[:, 0:256])
                K.rtrip(e)
                for blk in range(16):
                    e.scalar_tensor_tensor(
                        sz[:, blk * 1024 + jj * 256:
                           blk * 1024 + jj * 256 + 256],
                        ybuf[:, blk * 256:(blk + 1) * 256],
                        K.normw_sb[:, blk:blk + 1], rbc[:, 0:256],
                        ALU.mult, ALU.mult)
            g4 = E_.op("vector", gnf, waits=[r2, gt], inc=K.sV)
            gn_done.append(g4)

        prev_dma = None
        ev_of = {}
        for oc in range(8):
            xl = E_.dma("sync", xst[oc % 2][:, 0:1024],
                        K.x2[oc * 128:(oc + 1) * 128, q0:q0 + 1024],
                        waits=[ev_of.get(oc - 2, prev_out_dma)])

            def omf(e, oc=oc):
                for kc in range(16):
                    for st in range(2):
                        e.matmul(K.ps[st][0:128, 0:512],
                                 wout[:, kc * 1024 + oc * 128:
                                      kc * 1024 + oc * 128 + 128],
                                 sz[:, kc * 1024 + st * 512:
                                    kc * 1024 + st * 512 + 512],
                                 start=(kc == 0), stop=(kc == 15))
            om = E_.op("tensor", omf, waits=[gn_done[-1], wol] +
                       K.wr(("ps", 0), ("ps", 1)), inc=K.sT)

            def oef(e, oc=oc):
                for st in range(2):
                    e.tensor_add(stage[oc % 2][:, st * 512:st * 512 + 512],
                                 K.ps[st][0:128, 0:512],
                                 xst[oc % 2][:, st * 512:st * 512 + 512])
            ev = E_.op("vector", oef, waits=[om, xl, prev_dma], inc=K.sV)
            ev_of[oc] = ev
            K.rd(("ps", 0), ev)
            K.rd(("ps", 1), ev)
            prev_dma = E_.dma("sync",
                              K.x3[oc * 128:(oc + 1) * 128, q0:q0 + 1024],
                              stage[oc % 2][:, 0:1024], waits=[ev])
            if DEBUG:
                E_.dma("sync", K.DBG["dbg_x3"][oc * 128:(oc + 1) * 128,
                                               q0:q0 + 1024],
                       stage[oc % 2][:, 0:1024], out_sem=K.sO)
        prev_out_dma = prev_dma
    K.pop()


def build_program(upto=7):
    nc = bass.Bass("TRN2", target_bir_lowering=False)
    K = KB(nc)

    e0 = E(K)
    ph_init(K, e0)
    e0.emit("init")

    if upto >= 1:
        e1 = E(K)
        ph_qkv(K, e1)
        e1.emit("qkv")

    if upto >= 2:
        K.push()
        al = {"gp": K.a16(3 * NH * 384), "wo": K.a16(8 * 1024), "load": None}
        for ap in range(4):
            ei = E(K)
            ph_attn(K, ei, ap, al)
            ei.emit(f"attn{ap}")
            if upto == 2 and ap == 0:
                break
        K.pop()

    if upto >= 3:
        e2 = E(K)
        ph_ffn(K, e2, K.x1, K.x2, final=False,
               dbg="dbg_x2" if DEBUG else None)
        e2.emit("ffn1")

    if upto >= 4:
        e3 = E(K)
        m1 = ph_mamba1(K, e3)
        e3.emit("mamba1")

    if upto >= 5:
        e4 = E(K)
        ph_mamba2(K, e4, m1)
        e4.emit("mamba2")

    if upto >= 6:
        e5 = E(K)
        ph_ffn(K, e5, K.x3, K.OUT, final=True, dbg=None)

        def fin(e):
            e.wait_ge(K.sO.h, K.sO.v)
        e5.tape["sync"].append(fin)
        e5.emit("ffn2")
    return nc, K


def kernel(**inputs):
    full = {k: np.asarray(v) for k, v in inputs.items()}
    nc, K = build_program()
    in_maps = [make_core_inputs(c, full) for c in range(8)]
    res = run_bass_kernel_spmd(nc, in_maps, core_ids=list(range(8)))
    out = np.empty((B, S, D), np.float32)
    for c in range(8):
        b, hf = c // 2, c % 2
        out[b, hf * T:(hf + 1) * T] = np.asarray(res.results[c]["out"]).T
    return out


if __name__ == "__main__":
    nc, K = build_program()
    try:
        print("built ok; instructions:", nc.n_instructions())
    except Exception:
        print("built ok")


# version 09 (final)
# speedup vs baseline: 1.1207x; 1.0648x over previous
import numpy as np
import ml_dtypes
import concourse.bass as bass
import concourse.mybir as mybir
from concourse.bass_utils import run_bass_kernel_spmd

F32 = mybir.dt.float32
BF16 = mybir.dt.bfloat16
I32 = mybir.dt.int32
AF = mybir.ActivationFunctionType
ALU = mybir.AluOpType
AX = mybir.AxisListType

B, S, D = 4, 8192, 1024
T = 4096
HALO = 128
XT = T + 2 * HALO
NH, HD = 16, 64
HID = 3072
DIN, DST, NCH, CH = 2048, 128, 16, 256
CONV_DIM = DIN + 2 * DST
NPROJ = 2 * DIN + 2 * DST + NH
XBCDT = CONV_DIM + NH
EPS = 1e-5
NEG = -1e30
Q127 = 1.0 / 127.0

DEBUG = False
ENGS = ("sync", "tensor", "vector", "scalar", "gpsimd")


class Sem:
    def __init__(self, nc, name):
        self.h = nc.alloc_semaphore(name)
        self.v = 0


def _pad_slice_T(x_b, lo, hi):
    n = hi - lo
    out = np.zeros((D, n), np.float32)
    a, b2 = max(lo, 0), min(hi, S)
    if b2 > a:
        out[:, a - lo:b2 - lo] = np.ascontiguousarray(x_b[a:b2]).T
    return out


def _build_gp(rel, hf):
    p = np.arange(128)[:, None]
    f = np.arange(384)[None, :]
    idx = f - p
    band = (idx >= 0) & (idx <= 256)
    g = np.where(band[None], np.take(rel, np.clip(idx, 0, 256), axis=1), NEG)
    g = np.ascontiguousarray(g.astype(np.float32))
    g_first = g.copy()
    if hf == 0:
        g_first[:, :, :128] = NEG
    g_last = g.copy()
    if hf == 1:
        g_last[:, :, 256:] = NEG
    return g, g_first, g_last


def make_core_inputs(c, full):
    b, hf = c // 2, c % 2
    x = full["L0_attn_a_x"][b]
    lo = hf * T - HALO
    d = {}
    d["xin"] = _pad_slice_T(x, lo, lo + XT)
    d["wq"] = np.ascontiguousarray(full["L0_attn_a_Wq"], np.float32)
    d["wk"] = np.ascontiguousarray(full["L0_attn_a_Wk"], np.float32)
    d["wv"] = np.ascontiguousarray(full["L0_attn_a_Wv"], np.float32)
    d["wo"] = np.ascontiguousarray(full["L0_attn_a_Wo"], np.float32)
    for nm in ("bq", "bk", "bv", "bo"):
        v = np.asarray(full[f"L0_attn_a_{nm}"], np.float32)
        d[nm + "c"] = np.ascontiguousarray(v.reshape(8, 128).T)
        d[nm + "r"] = np.ascontiguousarray(v.reshape(1, D))
    g, gf, gl = _build_gp(np.asarray(full["L0_attn_a_rel_pos_bias"],
                                     np.float32), hf)
    d["gp_int"], d["gp_first"], d["gp_last"] = g, gf, gl
    d["wgT"] = np.ascontiguousarray(full["L0_ffn_w_gate"].T.astype(np.int32))
    d["wuT"] = np.ascontiguousarray(full["L0_ffn_w_up"].T.astype(np.int32))
    d["wdT"] = np.ascontiguousarray(full["L0_ffn_w_down"].T.astype(np.int32))
    d["winT"] = np.ascontiguousarray(full["L0_attn_b_W_in"].T.astype(np.float32))
    d["woutT"] = np.ascontiguousarray(full["L0_attn_b_W_out"].T.astype(np.float32))
    d["convw"] = np.ascontiguousarray(
        np.asarray(full["L0_attn_b_conv_w"], np.float32).reshape(CONV_DIM, 4))
    d["convbc"] = np.ascontiguousarray(
        np.asarray(full["L0_attn_b_conv_b"], np.float32).reshape(18, 128).T)
    d["smalls"] = np.ascontiguousarray(np.stack(
        [np.asarray(full["L0_attn_b_dt_bias"], np.float32),
         np.asarray(full["L0_attn_b_A_log"], np.float32),
         np.asarray(full["L0_attn_b_D"], np.float32)], axis=1))
    d["drow"] = np.ascontiguousarray(
        np.asarray(full["L0_attn_b_D"], np.float32).reshape(1, NH))
    d["normw"] = np.ascontiguousarray(
        np.asarray(full["L0_attn_b_norm_weight"], np.float32).reshape(16, 128).T)
    oh = np.zeros((16, 2048), np.float32)
    e16 = np.zeros((16, 256), np.float32)
    for h in range(16):
        oh[h, h * 128:(h + 1) * 128] = 1.0
        e16[h, h::16] = 1.0
    d["oh"] = oh
    d["e16"] = e16
    rm = np.zeros((128, 8), np.float32)
    if hf == 1:
        rm[:, c - 1] = 1.0
    d["rdmask"] = rm
    return d


class KB:
    def __init__(self, nc):
        self.nc = nc
        DR = {}

        def din(name, shape, dt=F32):
            DR[name] = nc.dram_tensor(name, list(shape), dt,
                                      kind="ExternalInput").ap()
        din("xin", [D, XT])
        for w in ("wq", "wk", "wv", "wo"):
            din(w, [D, D])
        for bn in ("bq", "bk", "bv", "bo"):
            din(bn + "c", [128, 8])
            din(bn + "r", [1, D])
        din("gp_int", [NH, 128, 384])
        din("gp_first", [NH, 128, 384])
        din("gp_last", [NH, 128, 384])
        din("wgT", [D, HID], I32)
        din("wuT", [D, HID], I32)
        din("wdT", [HID, D], I32)
        din("winT", [D, NPROJ])
        din("woutT", [DIN, D])
        din("convw", [CONV_DIM, 4])
        din("convbc", [128, 18])
        din("smalls", [NH, 3])
        din("drow", [1, NH])
        din("normw", [128, 16])
        din("oh", [16, 2048])
        din("e16", [16, 256])
        din("rdmask", [128, 8])
        self.DR = DR

        self.OUT = nc.dram_tensor("out", [D, T], F32, kind="ExternalOutput").ap()
        self.DBG = {}
        if DEBUG:
            for nm in ("dbg_x1", "dbg_x2", "dbg_x3"):
                self.DBG[nm] = nc.dram_tensor(nm, [D, T], F32,
                                              kind="ExternalOutput").ap()
            self.DBG["dbg_ao"] = nc.dram_tensor(
                "dbg_ao", [128, 8 * 1024], BF16, kind="ExternalOutput").ap()

        self.x1 = nc.dram_tensor("x1s", [D, T], F32).ap()
        self.x2 = nc.dram_tensor("x2s", [D, T], F32).ap()
        self.x3 = nc.dram_tensor("x3s", [D, T], F32).ap()
        kw = dict(kind="ExternalOutput") if DEBUG else {}
        self.qsc = nc.dram_tensor("qscr", [D, T], BF16, **kw).ap()
        self.ksc = nc.dram_tensor("kscr", [D, XT], BF16, **kw).ap()
        self.vsc = nc.dram_tensor("vscr", [XT, D], BF16, **kw).ap()
        self.ysp = nc.dram_tensor("yspill", [128, 16 * T], F32, **kw).ap()
        self.cfms = nc.dram_tensor("cfms", [DST, T], BF16, **kw).ap()
        self.cc_in = nc.dram_tensor("cc_in", [128, DIN], F32).ap()
        self.cc_out = nc.dram_tensor("cc_out", [8 * 128, DIN], F32).ap()
        self.ch_in = nc.dram_tensor("ch_in", [128, 54], F32).ap()
        self.ch_out = nc.dram_tensor("ch_out", [8 * 128, 54], F32).ap()

        self.PB = nc.alloc_sbuf_tensor("PB", [128, 96 * 1024 + 512], BF16)
        self.off = 0
        self.marks = []

        A = nc.alloc_sbuf_tensor
        self.ident = A("ident", [128, 128], BF16)
        self.identf = A("identf", [128, 128], F32)
        self.onesb = A("onesb", [128, 128], BF16)
        self.onesf = A("onesf", [128, 128], F32)
        self.ones16f = A("ones16f", [16, 128], F32)
        self.ones1 = A("ones1", [1, 512], BF16)
        self.biasc = A("biasc", [128, 32], F32)
        self.biasr = A("biasr", [1, D], BF16)
        self.convw_sb = A("convw_sb", [128, 72], F32)
        self.convb_sb = A("convb_sb", [128, 18], F32)
        self.smalls_sb = A("smalls_sb", [16, 8], F32)
        self.drow_sb = A("drow_sb", [1, NH], BF16)
        self.dcols = A("dcols", [128, NH], F32)
        self.normw_sb = A("normw_sb", [128, 16], F32)
        self.rdmask_sb = A("rdmask_sb", [128, 8], F32)
        self.etm = A("etm", [128, 512], F32)
        self.zeros16 = A("zeros16", [16, 256], F32)
        self.acsT_g = A("acsT_g", [128, NCH * 32], F32)
        self.lasts = A("lasts", [16, 40], F32)
        self.pj_sb = A("pj_sb", [16, NCH], F32)
        self.pj128 = A("pj128", [128, NCH * 16], F32)
        self.cdecb = A("cdecb", [128, NCH * 16], F32)
        self.e16_sb = A("e16_sb", [16, 256], F32)
        self.halo3 = A("halo3", [128, 54], F32)
        self.cvals = A("cvals", [128, 8], F32)
        self.scr_nop = A("scrnop", [1, 8], F32)

        self.ps = [nc.alloc_psum_tensor(f"ps{i}", [128, 512], F32)
                   for i in range(6)]
        self.pbf = [nc.alloc_psum_tensor(f"pbf{i}", [128, 1024], BF16)
                    for i in range(2)]
        self.ptp = [t[:, :].bitcast(F32) for t in self.pbf]

        for s in ("sS", "sG", "sC", "sT", "sV", "sA", "sP", "sO", "sRT",
                  "sRP"):
            setattr(self, s, Sem(nc, s))

        self.blr = {}

        self.CONSTS = [EPS, 1.0, -1.0, Q127, 1.0 / DIN, 0.5, 2.0, 0.0]
        for i, v in enumerate(self.CONSTS):
            nc.const_aps.register(F32, v, self.cvals[:, i:i + 1])

    def a16(self, cols):
        cols = (cols + 15) // 16 * 16
        v = self.PB[:, self.off:self.off + cols]
        self.off += cols
        assert self.off <= 96 * 1024 + 512, f"SBUF pool overflow: {self.off}"
        return v

    def a32(self, cols):
        return self.a16(2 * cols).bitcast(F32)

    def push(self):
        self.marks.append(self.off)

    def pop(self):
        self.off = self.marks.pop()

    def rtrip(self, e, eng="vector"):
        s = self.sRT if eng == "vector" else self.sRP
        s.v += 1
        e.engine_nop().then_inc(s.h, 1)
        e.wait_ge(s.h, s.v)

    def rd(self, bank, tok):
        self.blr[bank] = tok

    def wr(self, *banks):
        return [self.blr.get(b) for b in banks]

    def mamba_top(self):
        base = self.off
        t = dict(acs=self.a32(T), oh=self.a32(2048), sin=self.a32(2048),
                 sinb=self.a16(2048))
        if getattr(self, "_mtop_base", None) is None:
            self._mtop_base = base
        else:
            assert self._mtop_base == base, (self._mtop_base, base)
        return t


PE_DELAY = 500


class E:

    def __init__(self, K):
        self.K = K
        self.tape = {e: [] for e in ENGS}

    def op(self, eng, fn, waits=(), inc=None, incn=1):
        K = self.K
        wl = [(t[0].h, t[1]) for t in waits if t is not None]
        pe_tok = any((t is not None and t[0] is K.sT) for t in waits)
        if inc is not None:
            inc.v += incn

        def run(e, fn=fn, wl=wl, inc=inc, incn=incn, eng=eng, pe_tok=pe_tok):
            for h, v in wl:
                e.wait_ge(h, v)
            if pe_tok and eng != "tensor":
                e.nop(cycle_cnt=PE_DELAY)
            r = fn(e)
            if inc is not None:
                if isinstance(r, list):
                    r = r[-1] if r else None
                if eng == "tensor":
                    e.drain()
                    r = e.nop()
                elif r is None:
                    if eng in ("vector", "gpsimd"):
                        r = e.engine_nop()
                    elif eng == "scalar":
                        r = e.copy(K.scr_nop[0:1, 0:1], K.scr_nop[0:1, 1:2])
                    else:
                        r = e.nop()
                r.then_inc(inc.h, incn)
        self.tape[eng].append(run)
        return (inc, inc.v) if inc is not None else None

    def dma(self, eng, dst, src, waits=(), out_sem=None):
        K = self.K
        sem = out_sem if out_sem is not None else (
            K.sG if eng == "gpsimd" else (K.sC if eng == "scalar" else K.sS))
        return self.op(eng, lambda e: e.dma_start(out=dst, in_=src),
                       waits=waits, inc=sem, incn=16)

    def emit(self, name):
        with self.K.nc.Block(name) as blk:
            for eng in ENGS:
                lst = self.tape[eng]
                if not lst:
                    continue

                def body(e, lst=lst):
                    for fn in lst:
                        fn(e)
                getattr(blk, eng)(body)


def emit_ln(K, E_, src, col0, ncols, u_dst, extra=()):
    import os
    lnp = int(os.environ.get("LN_PART", 9))
    nt = (ncols + 511) // 512
    prev_ln = getattr(K, "_ln_prev", [])
    K.push()
    xc = [K.a16(8 * 512), K.a16(8 * 512)]
    x2b = K.a16(8 * 512)
    mbc = K.a32(512)
    rbc = K.a32(512)
    tsc = K.a32(512)
    done = []
    lastmm = None
    for t in range(nt):
        w = min(512, ncols - t * 512)
        bb = xc[t % 2]
        reuse = [done[t - 2]] if t >= 2 else list(prev_ln) + list(extra)
        ld = None
        for dch in range(8):
            ld = E_.dma("gpsimd", bb[:, dch * 512:dch * 512 + w],
                        src[dch * 128:dch * 128 + 128,
                            col0 + t * 512:col0 + t * 512 + w],
                        waits=reuse if dch == 0 else ())

        if lnp < 2:
            done.append(ld)
            continue

        def sf(e, bb=bb, w=w):
            r = None
            for dch in range(8):
                r = e.activation(x2b[:, dch * 512:dch * 512 + w],
                                 bb[:, dch * 512:dch * 512 + w], AF.Square)
            return r
        sq = E_.op("scalar", sf, waits=[ld] + ([done[t - 1]] if t >= 1
                                               else list(prev_ln)), inc=K.sA)
        if lnp < 3:
            done.append(sq)
            continue

        def mf(e, bb=bb, w=w):
            for dch in range(8):
                e.matmul(K.ps[0][:, 0:w], K.onesb[:],
                         bb[:, dch * 512:dch * 512 + w],
                         start=(dch == 0), stop=(dch == 7))
            for dch in range(8):
                e.matmul(K.ps[1][:, 0:w], K.onesb[:],
                         x2b[:, dch * 512:dch * 512 + w],
                         start=(dch == 0), stop=(dch == 7))
        mm = E_.op("tensor", mf,
                   waits=[ld, sq] + K.wr(("ps", 0), ("ps", 1)), inc=K.sT)
        lastmm = mm
        if lnp < 4:
            done.append(mm)
            continue

        def v1(e, w=w):
            e.tensor_scalar(mbc[:, 0:w], K.ps[0][:, 0:w], 1.0 / 1024, None,
                            ALU.mult)
            e.tensor_scalar(tsc[:, 0:w], K.ps[1][:, 0:w], 1.0 / 1024, None,
                            ALU.mult)
        t1 = E_.op("vector", v1, waits=[mm], inc=K.sV)
        K.rd(("ps", 0), t1)
        K.rd(("ps", 1), t1)
        if lnp < 5:
            done.append(t1)
            continue

        t2 = E_.op("scalar", lambda e, w=w: e.activation(
            rbc[:, 0:w], mbc[:, 0:w], AF.Square), waits=[t1], inc=K.sA)
        t3 = E_.op("vector", lambda e, w=w: e.tensor_sub(
            tsc[:, 0:w], tsc[:, 0:w], rbc[:, 0:w]), waits=[t2], inc=K.sV)
        t4 = E_.op("scalar", lambda e, w=w: e.activation(
            tsc[:, 0:w], tsc[:, 0:w], AF.Sqrt, bias=EPS), waits=[t3], inc=K.sA)

        def v2(e, bb=bb, w=w, t=t):
            e.reciprocal(rbc[:, 0:w], tsc[:, 0:w])
            for dch in range(8):
                e.tensor_sub(x2b[:, dch * 512:dch * 512 + w],
                             bb[:, dch * 512:dch * 512 + w], mbc[:, 0:w])
                e.tensor_mul(u_dst[:, dch * ncols + t * 512:
                                   dch * ncols + t * 512 + w],
                             x2b[:, dch * 512:dch * 512 + w], rbc[:, 0:w])
        t5 = E_.op("vector", v2, waits=[t4] + list(extra), inc=K.sV)
        done.append(t5)
    K.pop()
    K._ln_prev = [done[-1], lastmm]
    return done[-1]


def ph_init(K, E_):
    DR = K.DR

    def g0(e):
        def d(dst, src):
            e.dma_start(out=dst, in_=src).then_inc(K.sG.h, 16)
        d(K.biasc[:, 0:8], DR["bqc"])
        d(K.biasc[:, 8:16], DR["bkc"])
        d(K.biasc[:, 16:24], DR["bvc"])
        d(K.biasc[:, 24:32], DR["boc"])
        d(K.biasr[:, 0:D], DR["bor"])
        for a in range(18):
            d(K.convw_sb[:, a * 4:(a + 1) * 4],
              DR["convw"][a * 128:(a + 1) * 128, :])
        d(K.convb_sb[:], DR["convbc"])
        d(K.smalls_sb[0:16, 0:3], DR["smalls"])
        d(K.drow_sb[:], DR["drow"])
        d(K.normw_sb[:], DR["normw"])
        d(K.rdmask_sb[:], DR["rdmask"])
        d(K.e16_sb[:], DR["e16"])
        for i, v in enumerate(K.CONSTS):
            e.memset(K.cvals[:, i:i + 1], v)
        e.memset(K.onesb[:], 1.0)
        e.memset(K.onesf[:], 1.0)
        e.memset(K.ones16f[:], 1.0)
        e.memset(K.ones1[:], 1.0)
        e.memset(K.zeros16[:], 0.0)
        e.memset(K.scr_nop[:], 0.0)
        e.memset(K.etm[:], 0.0)
        K.rtrip(e, "gpsimd")
        e.affine_select(K.ident[:], K.onesb[:], pattern=[[-1, 128]],
                        compare_op=ALU.is_equal, fill=0.0, base=0,
                        channel_multiplier=1)
        e.affine_select(K.identf[:], K.onesf[:], pattern=[[-1, 128]],
                        compare_op=ALU.is_equal, fill=0.0, base=0,
                        channel_multiplier=1)
        e.affine_select(K.etm[:, 0:256], K.etm[:, 0:256], pattern=[[1, 256]],
                        compare_op=ALU.is_ge, fill=NEG, base=0,
                        channel_multiplier=-1)
        e.affine_select(K.etm[:, 256:512], K.etm[:, 256:512],
                        pattern=[[1, 256]], compare_op=ALU.is_ge, fill=NEG,
                        base=-128, channel_multiplier=-1)
    K.sG.v += 16 * 29
    ld = E_.op("gpsimd", g0, inc=K.sP)
    ldt = (K.sG, K.sG.v)

    s0a = E_.op("scalar", lambda e: e.activation(
        K.smalls_sb[0:16, 3:4], K.smalls_sb[0:16, 1:2], AF.Exp),
        waits=[ldt, ld], inc=K.sA)
    E_.op("vector", lambda e: e.tensor_scalar(
        K.smalls_sb[0:16, 3:4], K.smalls_sb[0:16, 3:4], -1.0, None,
        ALU.mult), waits=[s0a], inc=K.sV)

    mm = E_.op("tensor", lambda e: e.matmul(
        K.ps[0][0:128, 0:NH], K.ones1[0:1, 0:128], K.drow_sb[:],
        start=True, stop=True), waits=[ldt, ld], inc=K.sT)
    ev = E_.op("vector", lambda e: e.tensor_copy(K.dcols[:],
                                                 K.ps[0][0:128, 0:NH]),
               waits=[mm], inc=K.sV)
    K.rd(("ps", 0), ev)


def ph_qkv(K, E_):
    import os
    part = int(os.environ.get("QKV_PART", 9))
    K.push()
    u = K.a16(8 * XT)
    wbuf = [K.a16(8 * 1024), K.a16(8 * 1024)]
    stage = [K.a16(4 * 512), K.a16(4 * 512)]
    bvr = K.a16(1024)[0:1, :]

    bv_ld = E_.dma("gpsimd", bvr, K.DR["bvr"])
    ln_done = emit_ln(K, E_, K.DR["xin"], 0, XT, u)

    wname = ("wq", "wk", "wv")
    wld = [None] * 3

    def load_w(wi, waits):
        t = None
        for dch in range(8):
            t = E_.dma("gpsimd", wbuf[wi % 2][:, dch * 1024:(dch + 1) * 1024],
                       K.DR[wname[wi]][dch * 128:(dch + 1) * 128, :],
                       waits=waits if dch == 0 else ())
        return t

    if part < 1:
        K.pop()
        return
    wld[0] = load_w(0, [])
    wld[1] = load_w(1, [])

    def proj(wi, outdram, c0, nst, length, biasoff, qmul):
        groups = [list(range(i, min(i + 4, nst))) for i in range(0, nst, 4)]
        last_dma = None
        for grp in groups:
            prev_ev = None
            for oc in range(8):
                def mf(e, grp=grp, oc=oc, wi=wi, c0=c0, length=length):
                    for dch in range(8):
                        for si, st in enumerate(grp):
                            w0 = c0 + st * 512
                            wdt = min(512, length - st * 512)
                            e.matmul(K.ps[si][0:128, 0:wdt],
                                     wbuf[wi % 2][:, dch * 1024 + oc * 128:
                                                  dch * 1024 + oc * 128 + 128],
                                     u[:, dch * XT + w0:dch * XT + w0 + wdt],
                                     start=(dch == 0), stop=(dch == 7))
                w = [ln_done, wld[wi], prev_ev]
                w += K.wr(*[("ps", si) for si in range(len(grp))])
                mm = E_.op("tensor", mf, waits=w, inc=K.sT)

                def vf(e, grp=grp, oc=oc, biasoff=biasoff, qmul=qmul,
                       length=length):
                    for si, st in enumerate(grp):
                        wdt = min(512, length - st * 512)
                        if qmul is None:
                            e.tensor_scalar(
                                stage[oc % 2][:, si * 512:si * 512 + wdt],
                                K.ps[si][0:128, 0:wdt],
                                K.biasc[:, biasoff + oc:biasoff + oc + 1],
                                None, ALU.add)
                        else:
                            e.tensor_scalar(
                                stage[oc % 2][:, si * 512:si * 512 + wdt],
                                K.ps[si][0:128, 0:wdt],
                                K.biasc[:, biasoff + oc:biasoff + oc + 1],
                                qmul, ALU.add, ALU.mult)
                ev = E_.op("vector", vf, waits=[mm, last_dma], inc=K.sV)
                for si in range(len(grp)):
                    K.rd(("ps", si), ev)
                prev_ev = ev
                for si, st in enumerate(grp):
                    w0 = c0 + st * 512
                    wdt = min(512, length - st * 512)
                    last_dma = E_.dma(
                        "sync",
                        outdram[oc * 128:(oc + 1) * 128, w0 - c0:w0 - c0 + wdt],
                        stage[oc % 2][:, si * 512:si * 512 + wdt],
                        waits=[ev] if si == 0 else ())
        return last_dma

    if part < 2:
        K.pop()
        return
    qd = proj(0, K.qsc, HALO, 8, T, 0, 0.125)
    if part < 3:
        K.pop()
        return
    kd = proj(1, K.ksc, 0, 9, XT, 8, None)
    if part < 4:
        K.pop()
        return

    wld[2] = load_w(2, [qd])
    prevv = None
    for tc in range(34):
        def mf(e, tc=tc):
            for dch in range(8):
                for half in range(2):
                    e.matmul(K.ps[(tc % 2) * 2 + half][0:128, 0:512],
                             u[:, dch * XT + tc * 128:dch * XT + tc * 128 + 128],
                             wbuf[0][:, dch * 1024 + half * 512:
                                     dch * 1024 + half * 512 + 512],
                             start=(dch == 0), stop=False)
            for half in range(2):
                e.matmul(K.ps[(tc % 2) * 2 + half][0:128, 0:512],
                         K.ones1[0:1, 0:128],
                         bvr[0:1, half * 512:half * 512 + 512],
                         start=False, stop=True)
        b0, b1 = (tc % 2) * 2, (tc % 2) * 2 + 1
        mm = E_.op("tensor", mf, waits=[wld[2], bv_ld, ln_done] +
                   K.wr(("ps", b0), ("ps", b1)), inc=K.sT)

        def vf(e, tc=tc):
            for half in range(2):
                e.tensor_copy(stage[tc % 2][:, half * 512:half * 512 + 512],
                              K.ps[(tc % 2) * 2 + half][0:128, 0:512])
        ev = E_.op("vector", vf, waits=[mm, prevv], inc=K.sV)
        K.rd(("ps", b0), ev)
        K.rd(("ps", b1), ev)
        prevv = E_.dma("sync", K.vsc[tc * 128:(tc + 1) * 128, :],
                       stage[tc % 2][:, 0:1024], waits=[ev])
    K.pop()


def ph_attn(K, E_, ap, al):
    qs = ap * 1024
    K.push()
    qb = K.a16(8 * 1024)
    kb = K.a16(8 * 1280)
    vb = K.a16(10 * 1024)
    ao = K.a16(8 * 1024)
    ptg = K.a16(2 * 3072)
    pn = K.a16(4 * 768)
    nml = K.a32(16)
    dg = K.a16(4 * 256)
    xr = K.a32(8 * 1024)
    stage = [K.a32(1024), K.a32(1024)]

    if ap == 0:
        t = None
        for vi, vn in enumerate(("gp_int", "gp_first", "gp_last")):
            for h in range(NH):
                t = E_.dma("gpsimd",
                           al["gp"][:, (vi * NH + h) * 384:
                                    (vi * NH + h) * 384 + 384],
                           K.DR[vn][h])
        def expgp(e):
            r = None
            for ch in range(9):
                r = e.activation(al["gp"][:, ch * 2048:(ch + 1) * 2048],
                                 al["gp"][:, ch * 2048:(ch + 1) * 2048],
                                 AF.Exp)
            return r
        t = E_.op("scalar", expgp, waits=[t], inc=K.sA)
        for dch in range(8):
            t = E_.dma("gpsimd", al["wo"][:, dch * 1024:(dch + 1) * 1024],
                       K.DR["wo"][dch * 128:(dch + 1) * 128, :],
                       waits=[t] if dch == 0 else ())
        al["load"] = t

    spl = None
    for dch in range(8):
        spl = E_.dma("scalar", qb[:, dch * 1024:(dch + 1) * 1024],
                     K.qsc[dch * 128:(dch + 1) * 128, qs:qs + 1024])
        spl = E_.dma("scalar", kb[:, dch * 1280:(dch + 1) * 1280],
                     K.ksc[dch * 128:(dch + 1) * 128, qs:qs + 1280])
    for tc in range(10):
        spl = E_.dma("scalar", vb[:, tc * 1024:(tc + 1) * 1024],
                     K.vsc[qs + tc * 128:qs + (tc + 1) * 128, :])
    xrl = None
    for dch in range(8):
        xrl = E_.dma("sync", xr[:, dch * 1024:(dch + 1) * 1024],
                     K.DR["xin"][dch * 128:(dch + 1) * 128,
                                 HALO + qs:HALO + qs + 1024])

    def gpsel(qtg, h):
        vi = 1 if qtg == 0 else (2 if qtg == 31 else 0)
        return al["gp"][:, (vi * NH + h) * 384:(vi * NH + h) * 384 + 384]

    PVRANGE = {0: (0, 128), 1: (0, 256), 2: (0, 384), 3: (128, 512),
               4: (256, 512), 5: (384, 512)}

    exp_toks = {}
    nm_toks = {}
    dg_toks = {}
    tr_ev = {}
    pv_ev = {}

    for c in range(8):
        for sp in range(2):
            for qtj in range(4):
                qt = sp * 4 + qtj
                qtg = ap * 8 + qt
                bA = (qt % 2) * 2
                idx = c * 8 + qt

                sl = idx % 4

                def smf(e, c=c, qt=qt, bA=bA):
                    e.matmul(K.ps[bA][0:128, 0:384],
                             qb[0:64, c * 1024 + qt * 128:
                                c * 1024 + qt * 128 + 128],
                             kb[0:64, c * 1280 + qt * 128:
                                c * 1280 + qt * 128 + 384],
                             start=True, stop=True, tile_position=(0, 0))
                    e.matmul(K.ps[bA + 1][0:128, 0:384],
                             qb[64:128, c * 1024 + qt * 128:
                                c * 1024 + qt * 128 + 128],
                             kb[64:128, c * 1280 + qt * 128:
                                c * 1280 + qt * 128 + 384],
                             start=True, stop=True, tile_position=(64, 0))
                w = [spl] + K.wr(("ps", bA), ("ps", bA + 1))
                mm = E_.op("tensor", smf, waits=w, inc=K.sT)

                def emf(e, bA=bA, sl=sl):
                    o = sl * 768
                    e.activation(pn[:, o:o + 384], K.ps[bA][0:128, 0:384],
                                 AF.Exp)
                    return e.activation(pn[:, o + 384:o + 768],
                                        K.ps[bA + 1][0:128, 0:384], AF.Exp)
                w = [mm]
                if idx >= 4:
                    w.append(tr_ev[(idx - 4, 5)])
                ex = E_.op("scalar", emf, waits=w, inc=K.sA)
                exp_toks[idx] = ex
                K.rd(("ps", bA), ex)
                K.rd(("ps", bA + 1), ex)

                def lmf(e, c=c, qtg=qtg, sl=sl):
                    o = sl * 768
                    e.scalar_tensor_tensor(
                        pn[:, o:o + 384], pn[:, o:o + 384], 1.0,
                        gpsel(qtg, 2 * c), ALU.mult, ALU.mult,
                        accum_out=nml[:, sl * 4:sl * 4 + 1])
                    return e.scalar_tensor_tensor(
                        pn[:, o + 384:o + 768], pn[:, o + 384:o + 768], 1.0,
                        gpsel(qtg, 2 * c + 1), ALU.mult, ALU.mult,
                        accum_out=nml[:, sl * 4 + 1:sl * 4 + 2])
                w = [ex, al["load"]]
                if idx >= 4:
                    w.append(dg_toks[idx - 4])
                lm = E_.op("vector", lmf, waits=w, inc=K.sV)

                def nrm(e, sl=sl):
                    K.rtrip(e)
                    e.reciprocal(nml[:, sl * 4 + 2:sl * 4 + 4],
                                 nml[:, sl * 4:sl * 4 + 2])
                    K.rtrip(e)
                    p = sl * 256
                    e.tensor_scalar(dg[:, p:p + 128], K.ident[:],
                                    nml[:, sl * 4 + 2:sl * 4 + 3],
                                    None, ALU.mult)
                    return e.tensor_scalar(dg[:, p + 128:p + 256], K.ident[:],
                                           nml[:, sl * 4 + 3:sl * 4 + 4],
                                           None, ALU.mult)
                w = [lm]
                if idx >= 4:
                    w.append(tr_ev[(idx - 4, 5)])
                nm = E_.op("vector", nrm, waits=w, inc=K.sV)
                nm_toks[idx] = nm
                dgt = nm
                dg_toks[idx] = nm

                def tf6(e, sl=sl):
                    for h2 in range(2):
                        for kk in range(3):
                            o = sl * 768
                            p = sl * 256
                            e.matmul(K.ptp[h2][0:128, kk * 128:kk * 128 + 128],
                                     pn[:, o + h2 * 384 + kk * 128:
                                        o + h2 * 384 + kk * 128 + 128],
                                     dg[:, p + h2 * 128:p + h2 * 128 + 128],
                                     start=True, stop=True)
                w = [dgt] + K.wr(("pbf", 0), ("pbf", 1))
                tt = E_.op("tensor", tf6, waits=w, inc=K.sT)

                def ef6(e, qtj=qtj):
                    r = None
                    for h2 in range(2):
                        for kk in range(3):
                            rr = qtj + kk
                            r = e.tensor_copy(
                                ptg[:, h2 * 3072 + rr * 512 + qtj * 128:
                                    h2 * 3072 + rr * 512 + qtj * 128 + 128],
                                K.ptp[h2][0:128, kk * 128:kk * 128 + 128])
                    return r
                te = E_.op("vector", ef6, waits=[tt], inc=K.sV)
                K.rd(("pbf", 0), te)
                K.rd(("pbf", 1), te)
                tr_ev[(idx, 5)] = te

            spi = c * 2 + sp
            bank = 4 + (spi % 2)

            def pvf(e, c=c, sp=sp, bank=bank):
                for r in range(6):
                    c0, c1 = PVRANGE[r]
                    tc = sp * 4 + r
                    e.matmul(K.ps[bank][64:128, c0:c1],
                             vb[:, tc * 1024 + (2 * c + 1) * 64:
                                tc * 1024 + (2 * c + 1) * 64 + 64],
                             ptg[:, 3072 + r * 512 + c0:3072 + r * 512 + c1],
                             start=(r == 0), stop=(r == 5),
                             tile_position=(0, 64))
                    e.matmul(K.ps[bank][0:64, c0:c1],
                             vb[:, tc * 1024 + (2 * c) * 64:
                                tc * 1024 + (2 * c) * 64 + 64],
                             ptg[:, r * 512 + c0:r * 512 + c1],
                             start=(r == 0), stop=(r == 5),
                             tile_position=(0, 0))
            w = [tr_ev[(c * 8 + sp * 4 + j, 5)] for j in range(4)]
            w += K.wr(("ps", bank))
            pv = E_.op("tensor", pvf, waits=w, inc=K.sT)

            def pve(e, c=c, sp=sp, bank=bank):
                e.tensor_copy(ao[:, c * 1024 + sp * 512:
                                 c * 1024 + sp * 512 + 512],
                              K.ps[bank][0:128, 0:512])
            pev = E_.op("vector", pve, waits=[pv], inc=K.sV)
            K.rd(("ps", bank), pev)
            pv_ev[spi] = pev

    ao_done = pv_ev[15]
    if DEBUG and ap == 0:
        ao_done = E_.dma("sync", K.DBG["dbg_ao"], ao[:, 0:8 * 1024],
                         waits=[ao_done])

    prev_dma = None
    for oc in range(8):
        def mf(e, oc=oc):
            for dch in range(8):
                for st in range(2):
                    e.matmul(K.ps[st][0:128, 0:512],
                             al["wo"][:, dch * 1024 + oc * 128:
                                      dch * 1024 + oc * 128 + 128],
                             ao[:, dch * 1024 + st * 512:
                                dch * 1024 + st * 512 + 512],
                             start=(dch == 0), stop=False)
            for st in range(2):
                e.matmul(K.ps[st][0:128, 0:512],
                         K.biasr[0:1, oc * 128:oc * 128 + 128],
                         K.ones1[0:1, 0:512], start=False, stop=True)
        w = [ao_done, xrl, al["load"]] + K.wr(("ps", 0), ("ps", 1))
        mm = E_.op("tensor", mf, waits=w, inc=K.sT)

        def vf(e, oc=oc):
            for st in range(2):
                e.tensor_add(stage[oc % 2][:, st * 512:st * 512 + 512],
                             K.ps[st][0:128, 0:512],
                             xr[:, oc * 1024 + st * 512:
                                oc * 1024 + st * 512 + 512])
        ev = E_.op("vector", vf, waits=[mm, prev_dma], inc=K.sV)
        K.rd(("ps", 0), ev)
        K.rd(("ps", 1), ev)
        prev_dma = E_.dma("sync", K.x1[oc * 128:(oc + 1) * 128, qs:qs + 1024],
                          stage[oc % 2][:, 0:1024], waits=[ev])
        if DEBUG:
            E_.dma("sync", K.DBG["dbg_x1"][oc * 128:(oc + 1) * 128,
                                           qs:qs + 1024],
                   stage[oc % 2][:, 0:1024], out_sem=K.sO)
    K.pop()


def ph_ffn(K, E_, src, dst, final, dbg=None):
    K.push()
    wd = K.a16(24 * 1024)
    hb = K.a16(24 * 1024)
    u4 = K.a16(8 * 1024)
    wg = K.a16(8 * 768)
    wu = K.a16(8 * 768)
    sg = K.a16(1024)
    xst = [K.a32(1024), K.a32(1024)]
    stage = [K.a32(1024), K.a32(1024)]

    wdl = None
    for kc in range(24):
        wdl = E_.dma("gpsimd", wd[:, kc * 1024:(kc + 1) * 1024],
                     K.DR["wdT"][kc * 128:(kc + 1) * 128, :])

    prev_down = None
    last_gm = None
    last_um = None
    for tq in range(4):
        q0 = tq * 1024
        lnd = emit_ln(K, E_, src, q0, 1024, u4, extra=[last_gm, last_um])
        hdone = []
        for hs in range(4):
            wgl = None
            for dch in range(8):
                wgl = E_.dma("gpsimd", wg[:, dch * 768:(dch + 1) * 768],
                             K.DR["wgT"][dch * 128:(dch + 1) * 128,
                                         hs * 768:(hs + 1) * 768],
                             waits=[last_gm] if dch == 0 else ())
            wul = None
            for dch in range(8):
                wul = E_.dma("gpsimd", wu[:, dch * 768:(dch + 1) * 768],
                             K.DR["wuT"][dch * 128:(dch + 1) * 128,
                                         hs * 768:(hs + 1) * 768],
                             waits=[last_um] if dch == 0 else ())
            for oc in range(6):
                def gmf(e, oc=oc):
                    for dch in range(8):
                        for st in range(2):
                            e.matmul(K.ps[st][0:128, 0:512],
                                     wg[:, dch * 768 + oc * 128:
                                        dch * 768 + oc * 128 + 128],
                                     u4[:, dch * 1024 + st * 512:
                                        dch * 1024 + st * 512 + 512],
                                     start=(dch == 0), stop=(dch == 7))
                gm = E_.op("tensor", gmf,
                           waits=[lnd, wgl] + K.wr(("ps", 0), ("ps", 1)),
                           inc=K.sT)
                last_gm = gm

                def umf(e, oc=oc):
                    for dch in range(8):
                        for st in range(2):
                            e.matmul(K.ps[2 + st][0:128, 0:512],
                                     wu[:, dch * 768 + oc * 128:
                                        dch * 768 + oc * 128 + 128],
                                     u4[:, dch * 1024 + st * 512:
                                        dch * 1024 + st * 512 + 512],
                                     start=(dch == 0), stop=(dch == 7))
                um = E_.op("tensor", umf,
                           waits=[lnd, wul] + K.wr(("ps", 2), ("ps", 3)),
                           inc=K.sT)
                last_um = um

                silu = E_.op("scalar", lambda e: [
                    e.activation(sg[:, 0:512], K.ps[0][0:128, 0:512],
                                 AF.Silu, scale=Q127),
                    e.activation(sg[:, 512:1024], K.ps[1][0:128, 0:512],
                                 AF.Silu, scale=Q127)],
                    waits=[gm, hdone[-1] if hdone else None], inc=K.sA)
                K.rd(("ps", 0), silu)
                K.rd(("ps", 1), silu)

                def hf(e, oc=oc, hs=hs):
                    blk = hs * 6 + oc
                    for st in range(2):
                        e.scalar_tensor_tensor(
                            hb[:, blk * 1024 + st * 512:
                               blk * 1024 + st * 512 + 512],
                            K.ps[2 + st][0:128, 0:512], Q127,
                            sg[:, st * 512:st * 512 + 512],
                            ALU.mult, ALU.mult)
                ht = E_.op("vector", hf, waits=[um, silu], inc=K.sV)
                K.rd(("ps", 2), ht)
                K.rd(("ps", 3), ht)
                hdone.append(ht)

        prev_dma = None
        ev_of = {}
        for oc in range(8):
            xl = E_.dma("sync", xst[oc % 2][:, 0:1024],
                        src[oc * 128:(oc + 1) * 128, q0:q0 + 1024],
                        waits=[ev_of.get(oc - 2, prev_down)])

            def dmf(e, oc=oc):
                for kc in range(24):
                    for st in range(2):
                        e.matmul(K.ps[4 + st][0:128, 0:512],
                                 wd[:, kc * 1024 + oc * 128:
                                    kc * 1024 + oc * 128 + 128],
                                 hb[:, kc * 1024 + st * 512:
                                    kc * 1024 + st * 512 + 512],
                                 start=(kc == 0), stop=(kc == 23))
            dm = E_.op("tensor", dmf, waits=[hdone[-1], wdl] +
                       K.wr(("ps", 4), ("ps", 5)), inc=K.sT)

            def ef(e, oc=oc):
                for st in range(2):
                    e.scalar_tensor_tensor(
                        stage[oc % 2][:, st * 512:st * 512 + 512],
                        K.ps[4 + st][0:128, 0:512], Q127,
                        xst[oc % 2][:, st * 512:st * 512 + 512],
                        ALU.mult, ALU.add)
            ev = E_.op("vector", ef, waits=[dm, xl, prev_dma], inc=K.sV)
            ev_of[oc] = ev
            K.rd(("ps", 4), ev)
            K.rd(("ps", 5), ev)
            osem = K.sO if final else None
            prev_dma = E_.dma("sync", dst[oc * 128:(oc + 1) * 128, q0:q0 + 1024],
                              stage[oc % 2][:, 0:1024], waits=[ev],
                              out_sem=osem)
            if dbg is not None:
                E_.dma("sync", K.DBG[dbg][oc * 128:(oc + 1) * 128,
                                          q0:q0 + 1024],
                       stage[oc % 2][:, 0:1024], out_sem=K.sO)
        prev_down = prev_dma
    K.pop()
    return prev_down


def ph_mamba1(K, E_):
    K.push()
    top = K.mamba_top()
    acs = top["acs"]
    oh = top["oh"]
    gA = K.a16(18 * 1028)
    GBW = 1028
    dtg = K.a32(1024)
    u4 = K.a16(8 * 1024)
    winb = [K.a16(8 * 128), K.a16(8 * 128)]
    xc2 = K.a16(18 * 512)
    xdtT = K.a16(16 * 256)
    xdd = K.a16(2 * 128)
    btok = K.a16(2 * 128)
    cbt = K.a16(2 * 256)
    mt = K.a16(2 * 256)
    ebc = K.a16(256)
    cexp = K.a16(256)
    carryb = K.a16(128)
    dxs = K.a16(256)
    tail3 = K.a16(18 * 16)
    ysl = K.a32(8 * 256)
    carry = K.a32(DIN)
    dtf = K.a32(256)
    dtm = K.a32(256)
    eb2 = K.a32(512)
    dg4 = K.a16(2 * 512)
    decT = K.a32(32)
    dtTb = K.a32(32)
    haccum = K.a32(8 * 54)
    u3 = K.a16(8 * 16)
    hs54 = K.a32(54)

    sm = K.smalls_sb

    ohld = E_.dma("gpsimd", oh[0:16, :], K.DR["oh"])

    x3 = u3
    x3ld = None
    for dch in range(8):
        x3ld = E_.dma("gpsimd", x3[:, dch * 16:dch * 16 + 3],
                      K.x2[dch * 128:(dch + 1) * 128, T - 3:T])

    def mstat(e):
        for dch in range(8):
            e.matmul(K.ps[0][:, 0:3], K.onesb[:], x3[:, dch * 16:dch * 16 + 3],
                     start=(dch == 0), stop=(dch == 7))
    ms = E_.op("tensor", mstat, waits=[x3ld] + K.wr(("ps", 0)), inc=K.sT)

    def msq(e):
        r = None
        for dch in range(8):
            r = e.activation(u3[:, dch * 16 + 8:dch * 16 + 11],
                             x3[:, dch * 16:dch * 16 + 3], AF.Square)
        return r
    ms2 = E_.op("scalar", msq, waits=[x3ld], inc=K.sA)

    def mstat2(e):
        for dch in range(8):
            e.matmul(K.ps[1][:, 0:3], K.onesb[:],
                     u3[:, dch * 16 + 8:dch * 16 + 11],
                     start=(dch == 0), stop=(dch == 7))
    ms3 = E_.op("tensor", mstat2, waits=[ms2] + K.wr(("ps", 1)), inc=K.sT)

    def mv1(e):
        e.tensor_scalar(decT[:, 0:3], K.ps[0][:, 0:3], 1.0 / 1024, None,
                        ALU.mult)
        e.tensor_scalar(dtTb[:, 0:3], K.ps[1][:, 0:3], 1.0 / 1024, None,
                        ALU.mult)
    mv = E_.op("vector", mv1, waits=[ms, ms3], inc=K.sV)
    K.rd(("ps", 0), mv)
    K.rd(("ps", 1), mv)
    mv2 = E_.op("scalar", lambda e: e.activation(
        dtf[:, 0:3], decT[:, 0:3], AF.Square), waits=[mv], inc=K.sA)
    mv3 = E_.op("vector", lambda e: e.tensor_sub(
        dtTb[:, 0:3], dtTb[:, 0:3], dtf[:, 0:3]), waits=[mv2], inc=K.sV)
    mv4 = E_.op("scalar", lambda e: e.activation(
        dtTb[:, 0:3], dtTb[:, 0:3], AF.Sqrt, bias=EPS), waits=[mv3],
        inc=K.sA)

    def mv5a(e):
        for dch in range(8):
            e.tensor_sub(eb2[:, dch * 16:dch * 16 + 3],
                         x3[:, dch * 16:dch * 16 + 3], decT[:, 0:3])
    sub_d = E_.op("gpsimd", mv5a, waits=[mv, x3ld], inc=K.sG)

    def mv5(e):
        e.reciprocal(dtTb[:, 0:3], dtTb[:, 0:3])
        K.rtrip(e)
        for dch in range(8):
            e.tensor_mul(u3[:, dch * 16 + 4:dch * 16 + 7],
                         eb2[:, dch * 16:dch * 16 + 3], dtTb[:, 0:3])
    u3d = E_.op("vector", mv5, waits=[mv4, sub_d], inc=K.sV)

    hprev = None
    for oc in range(18):
        wl = None
        for dch in range(8):
            wl = E_.dma("gpsimd",
                        winb[oc % 2][:, dch * 128:dch * 128 + 128],
                        K.DR["winT"][dch * 128:(dch + 1) * 128,
                                     DIN + oc * 128:DIN + oc * 128 + 128],
                        waits=[hprev] if dch == 0 else ())

        def hmf(e, oc=oc):
            for dch in range(8):
                e.matmul(K.ps[2][0:128, 0:3],
                         winb[oc % 2][:, dch * 128:dch * 128 + 128],
                         u3[:, dch * 16 + 4:dch * 16 + 7],
                         start=(dch == 0), stop=(dch == 7))
        hm = E_.op("tensor", hmf, waits=[wl, u3d] + K.wr(("ps", 2)),
                   inc=K.sT)
        hprev = E_.op("vector", lambda e, oc=oc: e.tensor_copy(
            hs54[:, oc * 3:oc * 3 + 3], K.ps[2][0:128, 0:3]),
            waits=[hm], inc=K.sV)
        K.rd(("ps", 2), hprev)
    hdma = E_.dma("sync", K.ch_in[:, :], hs54[:, 0:54], waits=[hprev])

    def ccf(e):
        return e.collective_compute(
            "AllGather", ALU.bypass, replica_groups=[list(range(8))],
            ins=[K.ch_in[:, :].opt()], outs=[K.ch_out[:, :].opt()])
    cc1 = E_.op("gpsimd", ccf, waits=[hdma], inc=K.sP)
    hl = None
    for j in range(8):
        hl = E_.dma("sync", haccum[:, j * 54:(j + 1) * 54],
                    K.ch_out[j * 128:(j + 1) * 128, :], waits=[cc1])

    def hmask(e):
        e.tensor_scalar(K.halo3[:], haccum[:, 0:54], K.rdmask_sb[:, 0:1],
                        None, ALU.mult)
        for j in range(1, 8):
            K.rtrip(e)
            e.scalar_tensor_tensor(K.halo3[:], haccum[:, j * 54:(j + 1) * 54],
                                   K.rdmask_sb[:, j:j + 1], K.halo3[:],
                                   ALU.mult, ALU.add)
    halo_done = E_.op("vector", hmask, waits=[hl], inc=K.sV)

    cz = E_.op("vector", lambda e: e.memset(carry[:, :], 0.0), inc=K.sV)

    mm_of_oc = {}
    group_done = {}
    chunk_state = {}

    def emit_inproj(g, gb_free, winb_free=None):
        lnd = emit_ln(K, E_, K.x2, g * 1024, 1024, u4,
                      extra=[mm_of_oc.get(("last", 0))])
        prev_ev = None
        for oc in range(19):
            ow = 128 if oc < 18 else 16
            wl = None
            for dch in range(8):
                wl = E_.dma("gpsimd",
                            winb[oc % 2][:, dch * 128:dch * 128 + ow],
                            K.DR["winT"][dch * 128:(dch + 1) * 128,
                                         DIN + oc * 128:DIN + oc * 128 + ow],
                            waits=[mm_of_oc.get((g, oc - 2),
                                                mm_of_oc.get(("last", 0))),
                                   winb_free if oc < 2 else None]
                            if dch == 0 else ())

            def mf(e, oc=oc, ow=ow):
                op_ = min(ow, 128)
                for dch in range(8):
                    for st in range(2):
                        e.matmul(K.ps[st][0:op_, 0:512],
                                 winb[oc % 2][:, dch * 128:dch * 128 + ow],
                                 u4[:, dch * 1024 + st * 512:
                                    dch * 1024 + st * 512 + 512],
                                 start=(dch == 0), stop=(dch == 7))
            w = [lnd, wl, prev_ev] + K.wr(("ps", 0), ("ps", 1))
            mm = E_.op("tensor", mf, waits=w, inc=K.sT)
            mm_of_oc[(g, oc)] = mm
            mm_of_oc[("last", 0)] = mm

            def vf(e, oc=oc, ow=ow):
                for st in range(2):
                    if ow == 128:
                        e.tensor_copy(gA[:, oc * GBW + 4 + st * 512:
                                         oc * GBW + 4 + st * 512 + 512],
                                      K.ps[st][0:128, 0:512])
                    else:
                        e.tensor_copy(dtg[0:16, st * 512:st * 512 + 512],
                                      K.ps[st][0:16, 0:512])
            prev_ev = E_.op("vector", vf, waits=[mm, gb_free], inc=K.sV)
            K.rd(("ps", 0), prev_ev)
            K.rd(("ps", 1), prev_ev)
        return prev_ev

    W_ETM = K.etm

    def emit_chunk(j):
        g = j // 4
        jr = j % 4
        gdone = group_done[g]

        po = jr % 2
        if po == 0:
            if jr == 0:
                def ctxf(e, j=j):
                    e.tensor_copy(
                        gA[:, 0:18 * GBW].rearrange("p (a k) -> p a k", a=18)
                        [:, :, 1:4],
                        (K.halo3 if j == 0 else tail3)
                        [:, 0:18 * (3 if j == 0 else 16)]
                        .rearrange("p (a k) -> p a k", a=18)[:, :, 0:3])
                ctx = E_.op("vector", ctxf, waits=[gdone,
                                                   halo_done if j == 0
                                                   else None], inc=K.sV)
            else:
                ctx = None
            conv_sil = []
            dg_mm = {}
            for cch in range(18):
                slot = cch % 2

                def dgf(e, cch=cch, slot=slot):
                    for k in range(4):
                        e.tensor_scalar(
                            dg4[:, slot * 512 + k * 128:
                                slot * 512 + k * 128 + 128],
                            K.ident[:],
                            K.convw_sb[:, cch * 4 + k:cch * 4 + k + 1],
                            None, ALU.mult)
                dg = E_.op("vector", dgf,
                           waits=[ctx, dg_mm.get(cch - 2),
                                  chunk_state.get((j - 1, "ye15"))],
                           inc=K.sV)

                bank = 4 + (cch % 2)

                def cmf(e, cch=cch, slot=slot, jr=jr, bank=bank):
                    bb = cch * GBW
                    for k in range(4):
                        e.matmul(K.ps[bank][0:128, 0:512],
                                 dg4[:, slot * 512 + k * 128:
                                     slot * 512 + k * 128 + 128],
                                 gA[:, bb + 1 + jr * 256 + k:
                                    bb + 1 + jr * 256 + k + 512],
                                 start=(k == 0), stop=(k == 3))
                cm = E_.op("tensor", cmf,
                           waits=[dg, gdone] + K.wr(("ps", bank)), inc=K.sT)
                dg_mm[cch] = cm

                si = E_.op("scalar", lambda e, cch=cch, bank=bank:
                           e.activation(xc2[:, cch * 512:(cch + 1) * 512],
                                        K.ps[bank][0:128, 0:512], AF.Silu,
                                        bias=K.convb_sb[:, cch:cch + 1]),
                           waits=[cm, chunk_state.get((j - 1, "ye15"))],
                           inc=K.sA)
                K.rd(("ps", bank), si)
                conv_sil.append(si)
            chunk_state[(j, "convdone")] = conv_sil[-1]
        conv_done = chunk_state[(j - po, "convdone")]
        xc = xc2[:, po * 256:]

        def t3f(e, jr=jr):
            e.tensor_copy(
                tail3[:, 0:18 * 16].rearrange("p (a k) -> p a k", a=18)
                [:, :, 0:3],
                gA[:, 0:18 * GBW].rearrange("p (a k) -> p a k", a=18)
                [:, :, 4 + jr * 256 + 253:4 + jr * 256 + 256])
        t3 = E_.op("vector", t3f, waits=[gdone, conv_done], inc=K.sV)

        def dtc(e, jr=jr):
            e.tensor_scalar(dtf[0:16, :], dtg[0:16, jr * 256:(jr + 1) * 256],
                            sm[0:16, 0:1], None, ALU.add)
        d1 = E_.op("vector", dtc,
                   waits=[gdone, chunk_state.get((j - 1, "l2")),
                          chunk_state.get((j - 1, "dtT_done"))], inc=K.sV)
        d2 = E_.op("scalar", lambda e: [
            e.activation(dtm[0:16, 0:256], dtf[0:16, 0:256], AF.Abs),
            e.activation(dtm[0:16, 0:256], dtm[0:16, 0:256], AF.Exp,
                         scale=-1.0)],
            waits=[d1], inc=K.sA)
        d3 = E_.op("scalar", lambda e: e.activation(
            dtm[0:16, 0:256], dtm[0:16, 0:256], AF.Ln, bias=1.0),
            waits=[d2], inc=K.sA)

        def dtc2(e):
            e.tensor_relu(dtf[0:16, :], dtf[0:16, :])
            e.tensor_add(dtf[0:16, :], dtf[0:16, :], dtm[0:16, :])
            e.tensor_scalar(dtm[0:16, :], dtf[0:16, :], sm[0:16, 3:4],
                            None, ALU.mult)
        d4 = E_.op("vector", dtc2, waits=[d3], inc=K.sV)

        def scanf(e, j=j):
            e.tensor_tensor_scan(acs[0:16, j * 256:(j + 1) * 256],
                                 dtm[0:16, :], K.zeros16[0:16, :], 0.0,
                                 ALU.add, ALU.add)
        d5 = E_.op("vector", scanf, waits=[d4], inc=K.sV)
        d5b = E_.op("gpsimd", lambda e, j=j: e.tensor_copy(
            K.lasts[0:16, j:j + 1],
            acs[0:16, j * 256 + 255:j * 256 + 256]), waits=[d5], inc=K.sG)

        acsT_toks = []
        for sh in range(2):
            tt = E_.op("tensor", lambda e, j=j, sh=sh: e.transpose(
                K.ps[2][0:128, 0:16],
                acs[0:16, j * 256 + sh * 128:j * 256 + sh * 128 + 128],
                K.identf[0:16, 0:16]),
                waits=[d5] + K.wr(("ps", 2)), inc=K.sT)
            ev = E_.op("vector", lambda e, j=j, sh=sh: e.tensor_copy(
                K.acsT_g[:, j * 32 + sh * 16:j * 32 + sh * 16 + 16],
                K.ps[2][0:128, 0:16]), waits=[tt], inc=K.sV)
            K.rd(("ps", 2), ev)
            acsT_toks.append(ev)
        acsT_done = acsT_toks[-1]
        dtT_toks = []
        for sh in range(2):
            tt = E_.op("tensor", lambda e, sh=sh: e.transpose(
                K.ps[2][0:128, 0:16], dtf[0:16, sh * 128:sh * 128 + 128],
                K.identf[0:16, 0:16]),
                waits=[d4] + K.wr(("ps", 2)), inc=K.sT)
            ev = E_.op("vector", lambda e, sh=sh: e.tensor_copy(
                dtTb[:, sh * 16:sh * 16 + 16], K.ps[2][0:128, 0:16]),
                waits=[tt, chunk_state.get((j - 1, "xdt31"))], inc=K.sV)
            K.rd(("ps", 2), ev)
            dtT_toks.append(ev)
        dtT_done = dtT_toks[-1]

        l1 = E_.op("vector", lambda e, j=j: e.tensor_scalar(
            dtm[0:16, 0:32], K.e16_sb[0:16, 0:32], K.lasts[0:16, j:j + 1],
            None, ALU.mult), waits=[d5b], inc=K.sV)
        l2 = E_.op("tensor", lambda e: e.matmul(
            K.ps[3][0:128, 0:32], K.ones16f[0:16, :], dtm[0:16, 0:32],
            start=True, stop=True), waits=[l1] + K.wr(("ps", 3)), inc=K.sT)
        l3 = E_.op("vector", lambda e, j=j: e.tensor_sub(
            decT[:, 0:32], K.ps[3][0:128, 0:32],
            K.acsT_g[:, j * 32:j * 32 + 32]),
            waits=[l2, acsT_done, chunk_state.get((j - 1, "xdd15"))],
            inc=K.sV)

        def l4f(e, j=j):
            e.activation(decT[:, 0:32], decT[:, 0:32], AF.Exp)
            return e.activation(K.cdecb[:, j * 16:(j + 1) * 16],
                                K.ps[3][0:128, 0:16], AF.Exp)
        l4 = E_.op("scalar", l4f, waits=[l3], inc=K.sA)
        K.rd(("ps", 3), l4)

        xt_ev = {}
        for h in range(16):
            for sh in range(2):
                i = h * 2 + sh
                bank = i % 2

                def tf(e, h=h, sh=sh, bank=bank):
                    e.transpose(K.pbf[bank][0:128, 0:128],
                                xc[:, h * 512 + sh * 128:
                                   h * 512 + sh * 128 + 128],
                                K.ident[:])
                tt = E_.op("tensor", tf,
                           waits=[conv_done] + K.wr(("pbf", bank)), inc=K.sT)

                def ef(e, h=h, sh=sh, bank=bank):
                    e.tensor_scalar(
                        xdtT[:, h * 256 + sh * 128:h * 256 + sh * 128 + 128],
                        K.pbf[bank][0:128, 0:128],
                        dtTb[:, sh * 16 + h:sh * 16 + h + 1], None, ALU.mult)
                ev = E_.op("vector", ef, waits=[tt, dtT_done], inc=K.sV)
                K.rd(("pbf", bank), ev)
                xt_ev[i] = ev
        xdt_done = xt_ev[31]

        bt_toks = []
        for sh in range(2):
            bank = sh % 2
            tt = E_.op("tensor", lambda e, sh=sh, bank=bank: e.transpose(
                K.pbf[bank][0:128, 128:256],
                xc[:, 16 * 512 + sh * 128:16 * 512 + sh * 128 + 128],
                K.ident[:]),
                waits=[conv_done] + K.wr(("pbf", bank)), inc=K.sT)
            ev = E_.op("vector", lambda e, sh=sh, bank=bank: e.tensor_copy(
                btok[:, sh * 128:sh * 128 + 128],
                K.pbf[bank][0:128, 128:256]),
                waits=[tt, chunk_state.get((j - 1, "stm15"))], inc=K.sV)
            K.rd(("pbf", bank), ev)
            bt_toks.append(ev)
        btok_done = bt_toks[-1]

        E_.dma("sync", K.cfms[:, j * 256:(j + 1) * 256],
               xc[:, 17 * 512:17 * 512 + 256], waits=[conv_done])

        cb_toks = []
        for sh in range(2):
            mmc = E_.op("tensor", lambda e, sh=sh: e.matmul(
                K.ps[sh][0:128, 0:256],
                xc[:, 16 * 512 + sh * 128:16 * 512 + sh * 128 + 128],
                xc[:, 17 * 512:17 * 512 + 256], start=True, stop=True),
                waits=[conv_done] + K.wr(("ps", sh)), inc=K.sT)
            ev = E_.op("vector", lambda e, sh=sh: e.tensor_copy(
                cbt[:, sh * 256:(sh + 1) * 256], K.ps[sh][0:128, 0:256]),
                waits=[mmc, chunk_state.get((j - 1, "mt15"))], inc=K.sV)
            K.rd(("ps", sh), ev)
            cb_toks.append(ev)
        cbt_done = cb_toks[-1]

        prev = {}
        for kk2 in ("ym", "cu", "ye", "stm", "mt", "cexp"):
            prev[(kk2, -1)] = chunk_state.get((j - 1, kk2 + "15"))
        for h in range(16):
            ab = 2 + (h % 2)
            yb = 4 + (h % 2)
            stb = h % 2

            bca = E_.op("tensor", lambda e, h=h, j=j, ab=ab: e.matmul(
                K.ps[ab][0:128, 0:256],
                oh[0:16, h * 128:(h + 1) * 128],
                acs[0:16, j * 256:(j + 1) * 256],
                start=True, stop=True),
                waits=[d5, ohld] + K.wr(("ps", ab)), inc=K.sT)

            def ebf(e, h=h, j=j, ab=ab):
                for sh in range(2):
                    e.scalar_tensor_tensor(
                        eb2[:, sh * 256:(sh + 1) * 256],
                        K.ps[ab][0:128, 0:256],
                        K.acsT_g[:, j * 32 + sh * 16 + h:
                                 j * 32 + sh * 16 + h + 1],
                        W_ETM[:, sh * 256:(sh + 1) * 256],
                        ALU.subtract, ALU.add)
            eb = E_.op("vector", ebf,
                       waits=[bca, acsT_done,
                              prev.get(("mt", h - 1), prev[("mt", -1)])],
                       inc=K.sV)

            def exf(e, ab=ab):
                e.activation(ebc[:, 0:256], K.ps[ab][0:128, 0:256], AF.Exp)
                return e.activation(eb2[:, 0:512], eb2[:, 0:512], AF.Exp)
            ex = E_.op("scalar", exf,
                       waits=[eb, prev.get(("cexp", h - 1),
                                           prev[("cexp", -1)])], inc=K.sA)
            K.rd(("ps", ab), ex)

            def mtf(e, h=h):
                for sh in range(2):
                    e.tensor_mul(mt[:, sh * 256:(sh + 1) * 256],
                                 eb2[:, sh * 256:(sh + 1) * 256],
                                 cbt[:, sh * 256:(sh + 1) * 256])
                e.tensor_mul(cexp[:, 0:256], xc[:, 17 * 512:17 * 512 + 256],
                             ebc[:, 0:256])
            mtt = E_.op("vector", mtf,
                        waits=[ex, cbt_done,
                               prev.get(("ym", h - 1), prev[("ym", -1)])],
                        inc=K.sV)
            prev[("mt", h)] = mtt
            prev[("cexp", h)] = mtt

            def xdc(e, h=h):
                for sh in range(2):
                    e.tensor_scalar(xdd[:, sh * 128:(sh + 1) * 128],
                                    xdtT[:, h * 256 + sh * 128:
                                         h * 256 + sh * 128 + 128],
                                    decT[:, sh * 16 + h:sh * 16 + h + 1],
                                    None, ALU.mult)
                return e.tensor_copy(carryb[:, 0:128],
                                     carry[:, h * 128:(h + 1) * 128])
            xdt2 = E_.op("vector", xdc,
                         waits=[xdt_done, l4, cz,
                                prev.get(("cu", h - 1), prev[("cu", -1)]),
                                prev.get(("stm", h - 1), prev[("stm", -1)])],
                         inc=K.sV)

            smy = E_.op("tensor", lambda e, h=h, stb=stb, yb=yb: [
                e.matmul(K.ps[stb][0:128, 0:128], btok[:, 0:128],
                         xdd[:, 0:128], start=True, stop=False),
                e.matmul(K.ps[stb][0:128, 0:128], btok[:, 128:256],
                         xdd[:, 128:256], start=False, stop=True),
                e.matmul(K.ps[yb][0:128, 0:256], carryb[:, 0:128],
                         cexp[:, 0:256], start=True, stop=False),
                e.matmul(K.ps[yb][0:128, 0:256],
                         xdtT[:, h * 256:h * 256 + 128],
                         mt[:, 0:256], start=False, stop=False),
                e.matmul(K.ps[yb][0:128, 0:256],
                         xdtT[:, h * 256 + 128:h * 256 + 256],
                         mt[:, 256:512], start=False, stop=True)],
                waits=[btok_done, xdt2, mtt] +
                K.wr(("ps", stb), ("ps", yb)), inc=K.sT)
            prev[("stm", h)] = smy
            prev[("ym", h)] = smy

            dx = E_.op("vector", lambda e, h=h: e.tensor_scalar(
                dxs[:, 0:256], xc[:, h * 512:h * 512 + 256],
                K.dcols[:, h:h + 1], None, ALU.mult),
                waits=[conv_done,
                       prev.get(("ye", h - 1), prev[("ye", -1)])], inc=K.sV)

            def yecu(e, h=h, stb=stb, yb=yb, j=j):
                e.scalar_tensor_tensor(
                    ysl[:, (h % 8) * 256:(h % 8) * 256 + 256],
                    K.ps[yb][0:128, 0:256], 1.0, dxs[:, 0:256],
                    ALU.mult, ALU.add)
                return e.scalar_tensor_tensor(
                    carry[:, h * 128:(h + 1) * 128],
                    carry[:, h * 128:(h + 1) * 128],
                    K.cdecb[:, j * 16 + h:j * 16 + h + 1],
                    K.ps[stb][0:128, 0:128], ALU.mult, ALU.add)
            ye = E_.op("vector", yecu,
                       waits=[smy, dx, l4,
                              chunk_state.get((j, "ysl0")) if h >= 8 else
                              chunk_state.get((j - 1, "ysl1"))],
                       inc=K.sV)
            K.rd(("ps", yb), ye)
            K.rd(("ps", stb), ye)
            prev[("ye", h)] = ye
            prev[("cu", h)] = ye
            if h == 7:
                chunk_state[(j, "ysl0")] = E_.dma(
                    "sync",
                    K.ysp[:, :].rearrange("p (a t) -> p a t", a=16)
                    [:, 0:8, j * 256:(j + 1) * 256],
                    ysl[:, 0:2048].rearrange("p (a t) -> p a t", a=8),
                    waits=[ye])
            if h == 15:
                chunk_state[(j, "ysl1")] = E_.dma(
                    "sync",
                    K.ysp[:, :].rearrange("p (a t) -> p a t", a=16)
                    [:, 8:16, j * 256:(j + 1) * 256],
                    ysl[:, 0:2048].rearrange("p (a t) -> p a t", a=8),
                    waits=[ye])

        for kk2 in ("ym", "cu", "ye", "stm", "mt", "cexp"):
            chunk_state[(j, kk2 + "15")] = prev[(kk2, 15)]
        chunk_state[(j, "carry")] = prev[("cu", 15)]
        chunk_state[(j, "free")] = t3
        chunk_state[(j, "l2")] = l2
        chunk_state[(j, "dtT_done")] = dtT_done
        chunk_state[(j, "xdt31")] = xt_ev[31]
        chunk_state[(j, "xdd15")] = prev[("stm", 15)]
        chunk_state[(j, "stm15")] = prev[("stm", 15)]
        chunk_state[(j, "mt15")] = prev[("mt", 15)]

    for g in range(4):
        gfree = chunk_state.get((g * 4 - 1, "free"))
        gd = emit_inproj(g, gfree, winb_free=hprev if g == 0 else None)
        group_done[g] = gd
        for j in range(g * 4, g * 4 + 4):
            emit_chunk(j)

    st_dma = E_.dma("sync", K.cc_in[:, :], carry[:, :],
                    waits=[chunk_state[(15, "carry")]])

    def ccf2(e):
        return e.collective_compute(
            "AllGather", ALU.bypass, replica_groups=[list(range(8))],
            ins=[K.cc_in[:, :].opt()], outs=[K.cc_out[:, :].opt()])
    cc2 = E_.op("gpsimd", ccf2, waits=[st_dma], inc=K.sP)

    sbb = [u4.bitcast(F32)[:, 0:2048], u4.bitcast(F32)[:, 2048:4096]]
    acc = None
    for jj in range(8):
        ldt = E_.dma("sync", sbb[jj % 2][:, 0:2048],
                     K.cc_out[jj * 128:(jj + 1) * 128, :],
                     waits=[cc2, acc, mm_of_oc.get(("last", 0))])
        if jj == 0:
            acc = E_.op("vector", lambda e: e.tensor_scalar(
                top["sin"][:, :], sbb[0][:, 0:2048], K.rdmask_sb[:, 0:1],
                None, ALU.mult), waits=[ldt], inc=K.sV)
        else:
            acc = E_.op("vector", lambda e, jj=jj: e.scalar_tensor_tensor(
                top["sin"][:, :], sbb[jj % 2][:, 0:2048],
                K.rdmask_sb[:, jj:jj + 1], top["sin"][:, :],
                ALU.mult, ALU.add), waits=[ldt], inc=K.sV)
    sin_done = E_.op("vector", lambda e: e.tensor_copy(
        top["sinb"][:, :], top["sin"][:, :]), waits=[acc], inc=K.sV)

    pj1 = E_.op("vector", lambda e: e.tensor_tensor_scan(
        K.lasts[0:16, 20:36], K.lasts[0:16, 0:16],
        K.zeros16[0:16, 0:16], 0.0, ALU.add, ALU.add),
        waits=[chunk_state[(15, "carry")]], inc=K.sV)

    def pjshift(e):
        e.memset(K.lasts[0:16, 17:18], 0.0)
        e.tensor_copy(K.lasts[0:16, 18:18 + 15], K.lasts[0:16, 20:35])
    pj2 = E_.op("gpsimd", pjshift, waits=[pj1], inc=K.sG)
    pj3 = E_.op("scalar", lambda e: e.activation(
        K.pj_sb[0:16, 0:16], K.lasts[0:16, 17:33], AF.Exp),
        waits=[pj2], inc=K.sA)

    pjb_last = None
    for j in range(16):
        pd = E_.op("vector", lambda e, j=j: e.tensor_scalar(
            dtm[0:16, 0:16], K.e16_sb[0:16, 0:16], K.pj_sb[0:16, j:j + 1],
            None, ALU.mult), waits=[pj3, pjb_last], inc=K.sV)
        pm = E_.op("tensor", lambda e: e.matmul(
            K.ps[2][0:128, 0:16], K.ones16f[0:16, :], dtm[0:16, 0:16],
            start=True, stop=True), waits=[pd] + K.wr(("ps", 2)), inc=K.sT)
        pe = E_.op("vector", lambda e, j=j: e.tensor_copy(
            K.pj128[:, j * 16:(j + 1) * 16], K.ps[2][0:128, 0:16]),
            waits=[pm], inc=K.sV)
        K.rd(("ps", 2), pe)
        pjb_last = pe

    K.pop()
    return dict(sin=sin_done, pj=pjb_last,
                ydone=chunk_state[(15, "ysl1")])


def ph_mamba2(K, E_, m1):
    K.push()
    top = K.mamba_top()
    acs = top["acs"]
    oh = top["oh"]
    sz = K.a16(16 * 1024)
    wout = K.a16(16 * 1024)
    u4 = K.a16(8 * 1024)
    wzb = [K.a16(8 * 128), K.a16(8 * 128)]
    ybuf = K.a32(16 * 256)
    cfb = K.a16(256)
    ebc2 = K.a16(3 * 256)
    g2b = K.a16(16 * 256)
    xst = [K.a32(1024), K.a32(1024)]
    stage = [K.a32(1024), K.a32(1024)]
    rbc = K.a32(256)

    wol = None
    for kc in range(16):
        wol = E_.dma("gpsimd", wout[:, kc * 1024:(kc + 1) * 1024],
                     K.DR["woutT"][kc * 128:(kc + 1) * 128, :])

    zmm_of = {}
    prev_out_dma = None
    for tg in range(4):
        q0 = tg * 1024
        lnd = emit_ln(K, E_, K.x2, q0, 1024, u4,
                      extra=[zmm_of.get((tg - 1, 15))])

        zprev = None
        for zc in range(16):
            wl = None
            for dch in range(8):
                wl = E_.dma("gpsimd",
                            wzb[zc % 2][:, dch * 128:(dch + 1) * 128],
                            K.DR["winT"][dch * 128:(dch + 1) * 128,
                                         zc * 128:(zc + 1) * 128],
                            waits=[zmm_of.get((tg, zc - 2),
                                              zmm_of.get((tg - 1, 14 + zc)))]
                            if dch == 0 else ())

            def zmf(e, zc=zc):
                for dch in range(8):
                    for st in range(2):
                        e.matmul(K.ps[st][0:128, 0:512],
                                 wzb[zc % 2][:, dch * 128:(dch + 1) * 128],
                                 u4[:, dch * 1024 + st * 512:
                                    dch * 1024 + st * 512 + 512],
                                 start=(dch == 0), stop=(dch == 7))
            zm = E_.op("tensor", zmf,
                       waits=[lnd, wl, prev_out_dma] +
                       K.wr(("ps", 0), ("ps", 1)), inc=K.sT)
            zmm_of[(tg, zc)] = zm
            zprev = E_.op("scalar", lambda e, zc=zc: [
                e.activation(sz[:, zc * 1024:zc * 1024 + 512],
                             K.ps[0][0:128, 0:512], AF.Silu),
                e.activation(sz[:, zc * 1024 + 512:(zc + 1) * 1024],
                             K.ps[1][0:128, 0:512], AF.Silu)],
                waits=[zm, prev_out_dma], inc=K.sA)
            K.rd(("ps", 0), zprev)
            K.rd(("ps", 1), zprev)
        z_done = zprev

        gn_done = []
        prev_ctok = [None]
        for jj in range(4):
            j = tg * 4 + jj
            yl = E_.dma("sync",
                        ybuf[:, 0:4096].rearrange("p (a t) -> p a t", a=16),
                        K.ysp[:, :].rearrange("p (a t) -> p a t", a=16)
                        [:, :, j * 256:(j + 1) * 256],
                        waits=[m1["ydone"],
                               gn_done[-1] if gn_done else None])
            cl = E_.dma("sync", cfb[:, 0:256],
                        K.cfms[:, j * 256:(j + 1) * 256],
                        waits=[prev_ctok[0]])
            ctoks = []
            for h in range(16):
                ab = h % 4
                yb2 = 4 + (h % 2)
                eo = (h % 3) * 256
                bca = E_.op("tensor", lambda e, h=h, j=j, ab=ab: e.matmul(
                    K.ps[ab][0:128, 0:256],
                    oh[0:16, h * 128:(h + 1) * 128],
                    acs[0:16, j * 256:(j + 1) * 256],
                    start=True, stop=True),
                    waits=[m1["sin"], m1["pj"], yl, cl] + K.wr(("ps", ab)),
                    inc=K.sT)
                exb = E_.op("scalar", lambda e, ab=ab, eo=eo: e.activation(
                    ebc2[:, eo:eo + 256], K.ps[ab][0:128, 0:256], AF.Exp),
                    waits=[bca, ctoks[-3] if len(ctoks) >= 3 else None],
                    inc=K.sA)
                K.rd(("ps", ab), exb)
                cx = E_.op("vector", lambda e, eo=eo: e.tensor_mul(
                    ebc2[:, eo:eo + 256], cfb[:, 0:256], ebc2[:, eo:eo + 256]),
                    waits=[exb, cl], inc=K.sV)
                ym = E_.op("tensor", lambda e, h=h, yb2=yb2, eo=eo: e.matmul(
                    K.ps[yb2][0:128, 0:256],
                    top["sinb"][:, h * 128:(h + 1) * 128],
                    ebc2[:, eo:eo + 256],
                    start=True, stop=True),
                    waits=[cx, m1["sin"]] + K.wr(("ps", yb2)), inc=K.sT)
                ce = E_.op("vector",
                           lambda e, h=h, j=j, yb2=yb2:
                           e.scalar_tensor_tensor(
                               ybuf[:, h * 256:(h + 1) * 256],
                               K.ps[yb2][0:128, 0:256],
                               K.pj128[:, j * 16 + h:j * 16 + h + 1],
                               ybuf[:, h * 256:(h + 1) * 256],
                               ALU.mult, ALU.add),
                           waits=[ym, yl], inc=K.sV)
                K.rd(("ps", yb2), ce)
                ctoks.append(ce)
            corr_done = ctoks[-1]
            prev_ctok[0] = corr_done

            def gf(e, jj=jj):
                e.tensor_mul(
                    ybuf[:, 0:4096].rearrange("p (a t) -> p a t", a=16),
                    ybuf[:, 0:4096].rearrange("p (a t) -> p a t", a=16),
                    sz[:, 0:16 * 1024].rearrange("p (a t) -> p a t", a=16)
                    [:, :, jj * 256:(jj + 1) * 256])
            gt = E_.op("vector", gf, waits=[corr_done, z_done], inc=K.sV)

            sq = E_.op("scalar", lambda e: e.activation(
                g2b[:, 0:4096], ybuf[:, 0:4096], AF.Square), waits=[gt],
                inc=K.sA)

            def rmf(e):
                for blk in range(16):
                    e.matmul(K.ps[2][0:128, 0:256], K.onesb[:],
                             g2b[:, blk * 256:(blk + 1) * 256],
                             start=(blk == 0), stop=(blk == 15))
            rm = E_.op("tensor", rmf, waits=[sq] + K.wr(("ps", 2)), inc=K.sT)
            r2 = E_.op("scalar", lambda e: e.activation(
                rbc[:, 0:256], K.ps[2][0:128, 0:256], AF.Sqrt,
                scale=1.0 / DIN, bias=EPS), waits=[rm], inc=K.sA)
            K.rd(("ps", 2), r2)

            def gnf(e, jj=jj):
                e.reciprocal(rbc[:, 0:256], rbc[:, 0:256])
                K.rtrip(e)
                for blk in range(16):
                    e.scalar_tensor_tensor(
                        sz[:, blk * 1024 + jj * 256:
                           blk * 1024 + jj * 256 + 256],
                        ybuf[:, blk * 256:(blk + 1) * 256],
                        K.normw_sb[:, blk:blk + 1], rbc[:, 0:256],
                        ALU.mult, ALU.mult)
            g4 = E_.op("vector", gnf, waits=[r2, gt], inc=K.sV)
            gn_done.append(g4)

        prev_dma = None
        ev_of = {}
        for oc in range(8):
            xl = E_.dma("sync", xst[oc % 2][:, 0:1024],
                        K.x2[oc * 128:(oc + 1) * 128, q0:q0 + 1024],
                        waits=[ev_of.get(oc - 2, prev_out_dma)])

            def omf(e, oc=oc):
                for kc in range(16):
                    for st in range(2):
                        e.matmul(K.ps[st][0:128, 0:512],
                                 wout[:, kc * 1024 + oc * 128:
                                      kc * 1024 + oc * 128 + 128],
                                 sz[:, kc * 1024 + st * 512:
                                    kc * 1024 + st * 512 + 512],
                                 start=(kc == 0), stop=(kc == 15))
            om = E_.op("tensor", omf, waits=[gn_done[-1], wol] +
                       K.wr(("ps", 0), ("ps", 1)), inc=K.sT)

            def oef(e, oc=oc):
                for st in range(2):
                    e.tensor_add(stage[oc % 2][:, st * 512:st * 512 + 512],
                                 K.ps[st][0:128, 0:512],
                                 xst[oc % 2][:, st * 512:st * 512 + 512])
            ev = E_.op("vector", oef, waits=[om, xl, prev_dma], inc=K.sV)
            ev_of[oc] = ev
            K.rd(("ps", 0), ev)
            K.rd(("ps", 1), ev)
            prev_dma = E_.dma("sync",
                              K.x3[oc * 128:(oc + 1) * 128, q0:q0 + 1024],
                              stage[oc % 2][:, 0:1024], waits=[ev])
            if DEBUG:
                E_.dma("sync", K.DBG["dbg_x3"][oc * 128:(oc + 1) * 128,
                                               q0:q0 + 1024],
                       stage[oc % 2][:, 0:1024], out_sem=K.sO)
        prev_out_dma = prev_dma
    K.pop()


def build_program(upto=7):
    nc = bass.Bass("TRN2", target_bir_lowering=False)
    K = KB(nc)

    e0 = E(K)
    ph_init(K, e0)
    e0.emit("init")

    if upto >= 1:
        e1 = E(K)
        ph_qkv(K, e1)
        e1.emit("qkv")

    if upto >= 2:
        K.push()
        al = {"gp": K.a16(3 * NH * 384), "wo": K.a16(8 * 1024), "load": None}
        for ap in range(4):
            ei = E(K)
            ph_attn(K, ei, ap, al)
            ei.emit(f"attn{ap}")
            if upto == 2 and ap == 0:
                break
        K.pop()

    if upto >= 3:
        e2 = E(K)
        ph_ffn(K, e2, K.x1, K.x2, final=False,
               dbg="dbg_x2" if DEBUG else None)
        e2.emit("ffn1")

    if upto >= 4:
        e3 = E(K)
        m1 = ph_mamba1(K, e3)
        e3.emit("mamba1")

    if upto >= 5:
        e4 = E(K)
        ph_mamba2(K, e4, m1)
        e4.emit("mamba2")

    if upto >= 6:
        e5 = E(K)
        ph_ffn(K, e5, K.x3, K.OUT, final=True, dbg=None)

        def fin(e):
            e.wait_ge(K.sO.h, K.sO.v)
        e5.tape["sync"].append(fin)
        e5.emit("ffn2")
    return nc, K


def kernel(**inputs):
    full = {k: np.asarray(v) for k, v in inputs.items()}
    nc, K = build_program()
    in_maps = [make_core_inputs(c, full) for c in range(8)]
    res = run_bass_kernel_spmd(nc, in_maps, core_ids=list(range(8)))
    out = np.empty((B, S, D), np.float32)
    for c in range(8):
        b, hf = c // 2, c % 2
        out[b, hf * T:(hf + 1) * T] = np.asarray(res.results[c]["out"]).T
    return out


if __name__ == "__main__":
    nc, K = build_program()
    try:
        print("built ok; instructions:", nc.n_instructions())
    except Exception:
        print("built ok")
